# version 04
# baseline (speedup 1.0000x reference)
import contextlib
import numpy as np

import concourse.bass as bass
import concourse.mybir as mybir
from concourse.bass_utils import run_bass_kernel_spmd

F32 = mybir.dt.float32
BF16 = mybir.dt.bfloat16
I32 = mybir.dt.int32
AF = mybir.ActivationFunctionType
ALU = mybir.AluOpType
AX = mybir.AxisListType

N_CORES = 8
T = 2048
D = 1024
NTC = 16
NDC = 8
H = 16
DH = 64
VW = DH + 1
FFH = 3072
NHC = 24
ROPE = 32
VD1 = 32
VW1 = VD1 + 1
LAT = 128
NEXP = 64
PSLOT = 32
EMINE = 8
NEG = -3.0e38
XND = 2064
XPAD = 16
FULL = 9


def ts(i, w):
    return slice(i * w, (i + 1) * w)


class SemCtr:
    __slots__ = ("h", "n", "name")

    def __init__(self, h, name):
        self.h, self.n, self.name = h, 0, name


class View:
    __slots__ = ("t", "o", "rl")

    def __init__(self, t, o=0):
        self.t, self.o = t, o
        self.rl = int(t.shape[1])

    def __getitem__(self, idx):
        r, c = idx
        r0 = r.start or 0
        r1 = r.stop if r.stop is not None else int(self.t.shape[0])
        c0 = c.start or 0
        c1 = c.stop
        return bass.AP(self.t, r0 * self.rl + self.o + c0,
                       [[self.rl, r1 - r0], [1, c1 - c0]])

    def ap(self, r0, nr, free_dims, c0=0):
        return bass.AP(self.t, r0 * self.rl + self.o + c0,
                       [[self.rl, nr]] + [list(f) for f in free_dims])


class Prog:
    ENGS = ("tensor", "vector", "scalar", "gpsimd", "sync")

    def __init__(self, nc):
        self.nc = nc
        self.q = {e: [] for e in self.ENGS}
        self.es = contextlib.ExitStack()
        self.sems = []

    def sem(self, name):
        s = SemCtr(self.es.enter_context(self.nc.semaphore(name)), name)
        self.sems.append(s)
        return s

    def _emit(self, eng, fn, wait=(), inc=(), inc16=(), tag=""):
        wait = tuple((s, v) for (s, v) in wait if v > 0)
        self.q[eng].append((fn, wait, tuple(inc), tuple(inc16), tag))
        for s in inc:
            s.n += 1
        for s in inc16:
            s.n += 16

    def t(self, fn, **kw):
        self._emit("tensor", fn, **kw)

    def v(self, fn, wait=(), inc=(), **kw):
        vs = getattr(self, "vser", None)
        if vs is not None:
            wait = tuple(wait) + ((vs, vs.n),)
            if vs not in inc:
                inc = tuple(inc) + (vs,)
        self._emit("vector", fn, wait=wait, inc=inc, **kw)

    def s(self, fn, **kw):
        self._emit("scalar", fn, **kw)

    def g(self, fn, **kw):
        self._emit("gpsimd", fn, **kw)

    def y(self, fn, **kw):
        self._emit("sync", fn, **kw)

    def check_schedule(self):
        idx = {e: 0 for e in self.ENGS}
        cnt = {s.name: 0 for s in self.sems}
        progress = True
        while progress:
            progress = False
            for e in self.ENGS:
                while idx[e] < len(self.q[e]):
                    fn, wait, inc, inc16, tag = self.q[e][idx[e]]
                    if any(cnt[s.name] < v for s, v in wait):
                        break
                    for s in inc:
                        cnt[s.name] += 1
                    for s in inc16:
                        cnt[s.name] += 16
                    idx[e] += 1
                    progress = True
        stuck = {e: i for e, i in idx.items() if i < len(self.q[e])}
        if stuck:
            msg = []
            for e, i in stuck.items():
                fn, wait, inc, inc16, tag = self.q[e][i]
                miss = [(s.name, v, cnt[s.name]) for s, v in wait
                        if cnt[s.name] < v]
                msg.append(f"  {e}[{i}/{len(self.q[e])}] tag={tag} miss={miss}")
            raise RuntimeError("schedule deadlock:\n" + "\n".join(msg))

    def finalize(self):
        self.check_schedule()
        with self.nc.Block() as blk:
            for eng in self.ENGS:
                ops = self.q[eng]

                def make(ops=ops):
                    def run(e):
                        hi = {}
                        for fn, waits, incs, incs16, tag in ops:
                            for s, v in waits:
                                if hi.get(s.name, 0) < v:
                                    e.wait_ge(s.h, v)
                                    hi[s.name] = v
                            inst = fn(e)
                            for s in incs:
                                inst = inst.then_inc(s.h, 1)
                            for s in incs16:
                                inst = inst.then_inc(s.h, 16)
                    return run

                getattr(blk, eng)(make())


def build_program(stages=FULL):
    dbg92 = (stages == 92)
    if dbg92:
        stages = 1
    tapsel = None
    TAPLVL = {1: 1, 2: 2, 21: 3, 22: 3, 3: 4, 4: 5, 5: 6, 6: 8, 7: 8}
    if stages in TAPLVL:
        tapsel, stages = stages, TAPLVL[stages]
    nc = bass.Bass("TRN2", target_bir_lowering=False)
    P = Prog(nc)
    es = P.es

    def din(name, shape, dtype=F32):
        return nc.dram_tensor(name, list(shape), dtype, kind="ExternalInput").ap()

    x_hbm = din("x", (T, D))
    qkvw = din("qkvw", (D, 3 * D))
    qkvbT = din("qkvbT", (128, 24))
    qkvb_row = din("qkvb_row", (1, 3 * D))
    w1w = din("w1w", (2 * D, DH))
    w2w = din("w2w", (D, DH))
    r1w = din("r1w", (D, DH))
    r2w = din("r2w", (D, DH))
    mvw = din("mvw", (D, D))
    mvb_row = din("mvb_row", (1, D))
    mgw = din("mgw", (D, H))
    mgb_row = din("mgb_row", (1, H))
    mscale_row = din("mscale_row", (1, H))
    outw = din("outw", (D, D))
    outb_row = din("outb_row", (1, D))
    dlog_row = din("dlog_row", (1, H))
    Jm = din("Jm", (6, 6))
    wgT = din("wgT", (D, FFH), I32)
    wuT = din("wuT", (D, FFH), I32)
    wdT = din("wdT", (FFH, D), I32)
    wq1 = din("wq1", (D, D))
    wkv1 = din("wkv1", (D, LAT))
    wk1 = din("wk1", (LAT, H * ROPE))
    wv1 = din("wv1", (LAT, H * VD1))
    wo1e = din("wo1e", (H * VD1, D))
    qg_row = din("qg_row", (1, DH))
    kg_row = din("kg_row", (1, ROPE))
    phiT = din("phiT", (D, NEXP * PSLOT))
    phib_row = din("phib_row", (1, NEXP * PSLOT))
    expw = din("expw", (EMINE, D, D))
    expb = din("expb", (1, EMINE * D))
    identb = din("identb", (128, 128), BF16)
    identnb = din("identnb", (128, 128), BF16)
    onesrow = din("onesrow", (1, T), BF16)
    cmask4 = din("cmask4", (128, 4 * 512), BF16)
    sampmask = din("sampmask", (128, 16))
    iotatri = din("iotatri", (128, 128))
    mask01 = din("mask01", (128, 128))
    iotarev2 = din("iotarev2", (128, 128))
    iotac16 = din("iotac16", (16, 16))
    mask01_16 = din("mask01_16", (16, 16))
    iotap1 = din("iotap1", (128, 16))

    out_hbm = nc.dram_tensor("out", [T, D], F32, kind="ExternalOutput").ap()

    x1sp = nc.dram_tensor("x1sp", [T, D], F32, kind="Internal").ap()
    hvsp = nc.dram_tensor("hvsp", [NHC, 128, T], BF16, kind="Internal").ap()
    ctsp = nc.dram_tensor("ctsp", [2048, 2048], BF16, kind="Internal").ap()
    r2sp = nc.dram_tensor("r2sp", [16, 128, T], BF16, kind="Internal").ap()
    ccx = nc.dram_tensor("ccx", [2048, 1024], F32, kind="Internal").ap()
    ccxrs = nc.dram_tensor("ccxrs", [256, 1024], F32, kind="Internal").ap()
    ccyt = nc.dram_tensor("ccyt", [256, 1024], BF16, kind="Internal").ap()
    ccytag = nc.dram_tensor("ccytag", [2048, 1024], BF16, kind="Internal",
                            addr_space="Shared").ap()
    bounce = nc.dram_tensor("bounce", [8, 2048], F32, kind="Internal").ap()

    dW = P.sem("dW")
    dG = P.sem("dG")
    dS = P.sem("dS")
    dV = P.sem("dV")
    sT = P.sem("sT")
    sV = P.sem("sV")
    sS = P.sem("sS")
    sG = P.sem("sG")
    sC = P.sem("sC")
    P.vser = sV

    def sbt(name, p, f, dt=F32, stack=es):
        return stack.enter_context(nc.sbuf_tensor(name, [p, f], dt))

    def sbv(name, p, f, dt=F32, stack=es):
        return View(sbt(name, p, f, dt, stack))

    xnf_t = sbt("xnf", 128, NDC * XND, BF16)
    xnt_t = sbt("xnt", 128, NTC * 1024, BF16)
    attn_t = sbt("attn", 128, NTC * 1024, BF16)
    sarena_t = sbt("sarena", 128, NTC * 1024, BF16)
    varena_t = sbt("varena", 128, NTC * H * VW, BF16)
    xnf = View(xnf_t)
    xnt = View(xnt_t)
    attn = View(attn_t)
    sarena = View(sarena_t)
    varena = View(varena_t)
    c_ident = sbv("c_ident", 128, 128, BF16)
    c_identn = sbv("c_identn", 128, 128, BF16)
    c_ones = sbv("c_ones", 1, T, BF16)
    c_cmask4 = sbv("c_cmask4", 128, 4 * 512, BF16)
    c_sampm = sbv("c_sampm", 128, 16, F32)
    qkvbT_sb = sbv("qkvbT_sb", 128, 24, F32)
    lnsc = sbv("lnsc", 128, 48, F32)
    lndbg = sbv("lndbg", 128, 128, F32) if dbg92 else None
    rows_tmp = sbv("rows_tmp", 1, 1024, F32)

    PS = es.enter_context(nc.psum_tensor("PS", [128, 6 * 512], F32))
    PT0_t = es.enter_context(nc.psum_tensor("PT0", [128, 1024], BF16))
    PT1_t = es.enter_context(nc.psum_tensor("PT1", [128, 1024], BF16))
    PT0 = View(PT0_t)
    PT1 = View(PT1_t)

    def bank(i, w=512, off=0):
        return PS[:, i * 512 + off: i * 512 + off + w]

    st8 = {}
    taps = {}

    def gate(tag):
        P.g(lambda e: e.nop(),
            wait=[(sT, sT.n), (sV, sV.n), (sS, sS.n), (dV, dV.n),
                  (dW, dW.n), (dG, dG.n), (dS, dS.n)],
            inc=[sG], tag="gate_" + tag)
        return (sG, sG.n)

    def row2bf(src_dram, dst_ap_fn, w, tagn):
        for c0 in range(0, w, 1024):
            cw = min(1024, w - c0)
            P.y(lambda e, c0=c0, cw=cw: e.dma_start(
                    rows_tmp[0:1, 0:cw],
                    bass.AP(src_dram.tensor, c0, [[w, 1], [1, cw]])),
                wait=[(sV, sV.n)], inc16=[dW], tag=tagn)
            wv = dW.n
            P.v(lambda e, c0=c0, cw=cw: e.tensor_copy(
                    dst_ap_fn(c0, cw), rows_tmp[0:1, 0:cw]),
                wait=[(dW, wv)], inc=[sV], tag=tagn)
        return sV.n

    P.y(lambda e: e.dma_start(c_ident[0:128, 0:128], identb), inc16=[dW])
    P.y(lambda e: e.dma_start(c_identn[0:128, 0:128], identnb), inc16=[dW])
    P.y(lambda e: e.dma_start(c_ones[0:1, 0:T], onesrow), inc16=[dW])
    P.y(lambda e: e.dma_start(c_cmask4[0:128, 0:2048], cmask4), inc16=[dW])
    P.y(lambda e: e.dma_start(c_sampm[0:128, 0:16], sampmask), inc16=[dW])
    P.y(lambda e: e.dma_start(qkvbT_sb[0:128, 0:24], qkvbT), inc16=[dW])
    CONST_W = dW.n

    def emit_ln(src_ap, dst_ap, n, eps, slot, xq, wait_v=(), wait_s=(),
                tag="ln", final_func=None, clip=None):
        invn = 1.0 / n
        c = 8 * (slot % 4)
        P.v(lambda e: e.tensor_reduce(lnsc[0:128, c:c + 1], src_ap, AX.X,
                                      ALU.add, negate=True),
            wait=tuple(wait_v) + ((sS, st8.get(f"lnfin{slot % 4}", 0)),),
            inc=[sV], tag=tag)
        P.v(lambda e: e.tensor_scalar(lnsc[0:128, c + 1:c + 2],
                                      lnsc[0:128, c:c + 1],
                                      invn, None, ALU.mult), inc=[sV], tag=tag)
        P.s(lambda e: e.activation(xq, src_ap, AF.Square),
            wait=tuple(wait_s) + ((sV, st8.get("lnsq", 0)),),
            inc=[sS], tag=tag + "sq")
        sq_s = sS.n
        P.v(lambda e: e.tensor_reduce(lnsc[0:128, c + 2:c + 3], xq, AX.X,
                                      ALU.add),
            wait=[(sS, sq_s)], inc=[sV], tag=tag)
        st8["lnsq"] = sV.n
        P.v(lambda e: e.tensor_tensor(lnsc[0:128, c + 3:c + 4],
                                      lnsc[0:128, c + 1:c + 2],
                                      lnsc[0:128, c + 1:c + 2], ALU.mult),
            inc=[sV], tag=tag)
        P.v(lambda e: e.tensor_scalar(lnsc[0:128, c + 4:c + 5],
                                      lnsc[0:128, c + 2:c + 3],
                                      invn, None, ALU.mult), inc=[sV], tag=tag)
        P.v(lambda e: e.tensor_tensor(lnsc[0:128, c + 4:c + 5],
                                      lnsc[0:128, c + 4:c + 5],
                                      lnsc[0:128, c + 3:c + 4], ALU.subtract),
            inc=[sV], tag=tag)
        P.v(lambda e: e.tensor_scalar(lnsc[0:128, c + 4:c + 5],
                                      lnsc[0:128, c + 4:c + 5],
                                      eps, None, ALU.add), inc=[sV], tag=tag)
        vv = sV.n
        P.s(lambda e: e.activation(lnsc[0:128, c + 5:c + 6],
                                   lnsc[0:128, c + 4:c + 5],
                                   AF.Sqrt),
            wait=[(sV, vv)], inc=[sS], tag=tag)
        sv2 = sS.n
        P.v(lambda e: e.reciprocal(lnsc[0:128, c + 6:c + 7],
                                   lnsc[0:128, c + 5:c + 6]),
            wait=[(sS, sv2)], inc=[sV], tag=tag)
        P.v(lambda e: e.tensor_tensor(lnsc[0:128, c + 7:c + 8],
                                      lnsc[0:128, c + 1:c + 2],
                                      lnsc[0:128, c + 6:c + 7], ALU.mult),
            inc=[sV], tag=tag)
        vv2 = sV.n
        ff = final_func if final_func is not None else AF.Identity
        P.s(lambda e: e.activation(dst_ap, src_ap, ff,
                                   bias=lnsc[0:128, c + 7:c + 8],
                                   scale=lnsc[0:128, c + 6:c + 7]),
            wait=[(sV, vv2)], inc=[sS], tag=tag + "n")
        st8[f"lnfin{slot % 4}"] = sS.n
        return sS.n

    scopeL0 = contextlib.ExitStack()
    rowq = sbv("rowq", 1, 1024, BF16, scopeL0)
    rowm = sbv("rowm", 1, 2 * D, BF16, scopeL0)
    rowg = sbv("rowg", 1, 32, BF16, scopeL0)
    msc = sbv("msc", 128, 256, F32, scopeL0)
    gated = sbv("gated", 128, 16, F32, scopeL0)
    mscale_bc = sbv("mscale_bc", 128, 16, F32, scopeL0)
    toe = sbv("toe", 128, 16 * 128, BF16, scopeL0)
    dpow = sbv("dpow", 128, 16, F32, scopeL0)
    ddbc = sbv("ddbc", 128, 16 * 128, BF16, scopeL0)
    Th = sbv("Th", 16, 16 * 16, BF16, scopeL0)
    mjj = sbv("mjj", 128, 36, BF16, scopeL0)

    P.y(lambda e: e.dma_start(rows_tmp[0:1, 0:1024],
                              bass.AP(qkvb_row.tensor, 2 * D,
                                      [[3 * D, 1], [1, 1024]])),
        inc16=[dW], tag="rq")
    _wv = dW.n
    P.v(lambda e: e.tensor_copy(rowq[0:1, 0:1024], rows_tmp[0:1, 0:1024]),
        wait=[(dW, _wv)], inc=[sV], tag="rq")
    row2bf(mvb_row, lambda c0, cw: rowm[0:1, c0:c0 + cw], D, "rm")
    row2bf(outb_row, lambda c0, cw: rowm[0:1, D + c0:D + c0 + cw], D, "ro")
    row2bf(mgb_row, lambda c0, cw: rowg[0:1, c0:c0 + cw], H, "rg")
    ROWS_V = sV.n
    P.s(lambda e: e.nop(), wait=[(sV, ROWS_V)], inc=[sS], tag="rowsbar")
    ROWS_S = sS.n

    P.y(lambda e: e.dma_start(rows_tmp[0:1, 0:H], mscale_row),
        wait=[(sV, ROWS_V)], inc16=[dW])
    wv = dW.n
    P.y(lambda e: e.dma_start(bounce[0:1, 0:H], rows_tmp[0:1, 0:H]),
        wait=[(dW, wv)], inc16=[dW])
    wv = dW.n
    P.y(lambda e: e.dma_start(mscale_bc[0:128, 0:H],
                              bass.AP(bounce.tensor, 0, [[0, 128], [1, H]])),
        wait=[(dW, wv)], inc16=[dW], tag="mscbc")
    MSCALE_W = dW.n

    setupS = contextlib.ExitStack()
    iotatri_sb = sbv("iotatri_sb", 128, 128, F32, setupS)
    mask01_sb = sbv("mask01_sb", 128, 128, F32, setupS)
    iotarev2_sb = sbv("iotarev2_sb", 128, 128, F32, setupS)
    iotac16_sb = sbv("iotac16_sb", 16, 16, F32, setupS)
    mask0116_sb = sbv("mask0116_sb", 16, 16, F32, setupS)
    iotap1_sb = sbv("iotap1_sb", 128, 16, F32, setupS)
    lndrow = sbv("lndrow", 1, 16, F32, setupS)
    lndbc = sbv("lndbc", 128, 16, F32, setupS)
    lndCbc = sbv("lndCbc", 128, 16, F32, setupS)
    Thf = sbv("Thf", 16, 16, F32, setupS)
    toef = sbv("toef", 128, 128, F32, setupS)
    mjj_a = sbv("mjj_a", 128, 6, F32, setupS)
    mjj_b = sbv("mjj_b", 128, 6, F32, setupS)

    for src, dst, pp, ff in ((iotatri, iotatri_sb, 128, 128),
                             (mask01, mask01_sb, 128, 128),
                             (iotarev2, iotarev2_sb, 128, 128),
                             (iotac16, iotac16_sb, 16, 16),
                             (mask01_16, mask0116_sb, 16, 16),
                             (iotap1, iotap1_sb, 128, 16)):
        P.y(lambda e, src=src, dst=dst, pp=pp, ff=ff: e.dma_start(
            dst[0:pp, 0:ff], src), inc16=[dW], tag="dk0")
    DKC_W = dW.n
    P.y(lambda e: e.dma_start(rows_tmp[0:1, 0:H], dlog_row),
        wait=[(dW, MSCALE_W)], inc16=[dW])
    wv = dW.n
    P.s(lambda e: e.activation(lndrow[0:1, 0:H], rows_tmp[0:1, 0:H],
                               AF.Sigmoid),
        wait=[(dW, wv)], inc=[sS], tag="dk1")
    P.s(lambda e: e.activation(lndrow[0:1, 0:H], lndrow[0:1, 0:H], AF.Ln),
        inc=[sS], tag="dk2")
    sv = sS.n
    P.y(lambda e: e.dma_start(bounce[1:2, 0:H], lndrow[0:1, 0:H]),
        wait=[(sS, sv)], inc16=[dW])
    wv = dW.n
    P.y(lambda e: e.dma_start(lndbc[0:128, 0:H],
                              bass.AP(bounce.tensor, 2048, [[0, 128], [1, H]])),
        wait=[(dW, wv)], inc16=[dW], tag="dk3")
    LND_W = dW.n
    P.v(lambda e: e.tensor_scalar(lndCbc[0:128, 0:H], lndbc[0:128, 0:H],
                                  128.0, None, ALU.mult),
        wait=[(dW, LND_W)], inc=[sV], tag="dk4")
    LND_V = sV.n
    prev_g = 0
    for h in range(H):
        P.s(lambda e, h=h: e.activation(toef[0:128, 0:128],
                                        iotatri_sb[0:128, 0:128],
                                        AF.Exp, bias=0.0,
                                        scale=lndbc[0:128, h:h + 1]),
            wait=[(dW, DKC_W), (dW, LND_W), (sV, prev_g)], inc=[sS],
            tag=f"toe{h}")
        sv = sS.n
        P.v(lambda e, h=h: e.tensor_tensor(toe[0:128, ts(h, 128)],
                                           toef[0:128, 0:128],
                                           mask01_sb[0:128, 0:128], ALU.mult),
            wait=[(sS, sv)], inc=[sV], tag=f"toem{h}")
        P.s(lambda e, h=h: e.activation(ddbc[0:128, ts(h, 128)],
                                        iotarev2_sb[0:128, 0:128],
                                        AF.Exp, bias=0.0,
                                        scale=lndbc[0:128, h:h + 1]),
            inc=[sS], tag=f"dd{h}")
        P.s(lambda e, h=h: e.activation(Thf[0:16, 0:16],
                                        iotac16_sb[0:16, 0:16], AF.Exp,
                                        bias=0.0,
                                        scale=lndCbc[0:16, h:h + 1]),
            wait=[(sV, LND_V), (sV, prev_g)], inc=[sS], tag=f"th{h}")
        sv = sS.n
        P.v(lambda e, h=h: e.tensor_tensor(Th[0:16, ts(h, 16)],
                                           Thf[0:16, 0:16],
                                           mask0116_sb[0:16, 0:16], ALU.mult),
            wait=[(sS, sv)], inc=[sV], tag=f"thm{h}")
        prev_g = sV.n
    TOE_V = sV.n
    DD_S = sS.n
    P.v(lambda e: e.tensor_tensor(dpow[0:128, 0:16], iotap1_sb[0:128, 0:16],
                                  lndbc[0:128, 0:16], ALU.mult),
        wait=[(dW, DKC_W), (dW, LND_W)], inc=[sV], tag="dpow")
    sv = sV.n
    P.s(lambda e: e.activation(dpow[0:128, 0:16], dpow[0:128, 0:16], AF.Exp),
        wait=[(sV, sv)], inc=[sS], tag="dpow2")
    DPOW_S = sS.n
    P.y(lambda e: e.dma_start(rows_tmp[0:1, 0:36],
                              bass.AP(Jm.tensor, 0, [[36, 1], [1, 36]])),
        wait=[(dW, LND_W)], inc16=[dW], tag="mjj0")
    wv = dW.n
    P.y(lambda e: e.dma_start(bounce[2:3, 0:36], rows_tmp[0:1, 0:36]),
        wait=[(dW, wv)], inc16=[dW])
    wv = dW.n
    for i1 in range(6):
        P.g(lambda e, i1=i1: e.dma_start(
            mjj_a[6 * i1:6 * i1 + 6, 0:6],
            bass.AP(bounce.tensor, 2 * 2048 + 6 * i1, [[0, 6], [1, 6]])),
            wait=[(dW, wv)], inc16=[dG], tag="mjja")
    P.g(lambda e: e.dma_start(
        mjj_b[0:36, 0:6],
        bass.AP(bounce.tensor, 2 * 2048, [[0, 6], [6, 6], [1, 6]])),
        wait=[(dW, wv)], inc16=[dG], tag="mjjb")
    wvg = dG.n
    P.v(lambda e: e.tensor_tensor(
            mjj[0:36, 0:36],
            mjj_a.ap(0, 36, [[1, 6], [0, 6]]),
            mjj_b.ap(0, 36, [[0, 6], [1, 6]]),
            ALU.mult),
        wait=[(dG, wvg)], inc=[sV], tag="mjjm")
    MJJ_V = sV.n
    setupS.close()

    p0S = contextlib.ExitStack()
    _xin0 = sbv("xin0", 128, 1024, F32, p0S)
    xin = [_xin0, _xin0]
    xsq = sbv("xsq", 128, 1024, F32, p0S)

    ln0fin = {}
    SETUP_DONE = ((sV, TOE_V), (sS, DPOW_S), (sV, MJJ_V), (sS, DD_S))
    for tc in range(NTC):
        b = tc % 2
        P.y(lambda e, tc=tc, b=b: e.dma_start(xin[b][0:128, 0:1024],
                                              x_hbm[ts(tc, 128), :]),
            wait=[(sS, ln0fin.get(tc - 1, 0))] + list(SETUP_DONE),
            inc16=[dW], tag=f"p0x{tc}")
        wv = dW.n
        ln0fin[tc] = emit_ln(xin[b][0:128, 0:1024], xnt[0:128, ts(tc, 1024)],
                             D, 1e-5, slot=tc, xq=xsq[0:128, 0:1024],
                             wait_v=[(dW, wv)], wait_s=[(dW, wv)],
                             tag=f"ln0_{tc}")
        if dbg92:
            c92 = 8 * (tc % 4)
            P.s(lambda e, tc=tc, c92=c92: e.copy(
                    lndbg[0:128, tc * 8:tc * 8 + 8], lnsc[0:128, c92:c92 + 8]),
                inc=[sS], tag=f"lndbg{tc}")
            st8[f"lnfin{tc % 4}"] = sS.n
    LN0_S = sS.n

    P.v(lambda e: e.memset(
        xnf.ap(0, 128, [[XND, NDC], [1, XPAD]]), 0.0), inc=[sV], tag="p0pad")
    PAD_V = sV.n

    def transposes_to_fm(fin_of_tc, tag):
        for tc in range(NTC):
            for dc in range(NDC):
                P.s(lambda e, tc=tc, dc=dc: e.dma_start_transpose(
                        xnf[0:128, dc * XND + XPAD + tc * 128:
                            dc * XND + XPAD + (tc + 1) * 128],
                        xnt[0:128, tc * 1024 + dc * 128:
                            tc * 1024 + (dc + 1) * 128]),
                    wait=[(sS, fin_of_tc[tc])], inc16=[dS],
                    tag=f"{tag}t{tc}{dc}")
        w = dS.n
        P.s(lambda e: e.nop(), wait=[(dS, w)], inc=[sS], tag=f"{tag}fin")
        return sS.n

    XNF_S = transposes_to_fm(ln0fin, "p0f")
    p0S.close()
    taps[1] = ("arena_tm", xnt, 1024)
    if dbg92:
        taps[1] = ("small", lndbg, 128, 128)

    if stages >= 2:
        P.v(lambda e: e.memset(
            varena.ap(0, 128, [[H * VW, NTC], [VW, H], [1, 1]], c0=DH),
            1.0), inc=[sV], tag="vones")
        VONES_V = sV.n
        wvb = View(sarena_t, 0)
        P.g(lambda e: e.dma_start(
            wvb.ap(0, 128, [[1024, 8], [1, 1024]]),
            qkvw[:, 2 * D:3 * D].rearrange("(dc p) c -> p dc c", p=128)),
            inc16=[dG], tag="vw")
        VW_G = dG.n
        for tc in range(NTC):
            for cb in range(2):
                k = (tc * 2 + cb) % 2
                pb = bank(k)
                for dc in range(NDC):
                    P.t(lambda e, tc=tc, cb=cb, dc=dc, pb=pb: e.matmul(
                            pb,
                            xnf[0:128, dc * XND + XPAD + tc * 128:
                                dc * XND + XPAD + (tc + 1) * 128],
                            wvb[0:128, dc * 1024 + cb * 512:
                                dc * 1024 + (cb + 1) * 512],
                            start=(dc == 0), stop=False),
                        wait=[(dG, VW_G), (sS, XNF_S),
                              (sS, st8.get(f"vev{k}", 0))],
                        inc=[sT], tag=f"vmm{tc}{cb}")
                P.t(lambda e, tc=tc, cb=cb, pb=pb: e.matmul(
                        pb, c_ones[0:1, ts(tc, 128)],
                        rowq[0:1, cb * 512:(cb + 1) * 512],
                        start=False, stop=True),
                    wait=[(sS, ROWS_S)], inc=[sT], tag=f"vb{tc}{cb}")
                tv = sT.n
                P.s(lambda e, tc=tc, cb=cb, pb=pb: e.copy(
                        varena.ap(0, 128, [[VW, 8], [1, DH]],
                                  c0=tc * H * VW + cb * 8 * VW),
                        pb[:, 0:512].rearrange("p (h d) -> p h d", h=8)),
                    wait=[(sT, tv), (sV, VONES_V)], inc=[sS],
                    tag=f"vev{tc}{cb}")
                st8[f"vev{k}"] = sS.n
        V_S = sS.n

        paS = contextlib.ExitStack()
        qt = [[View(xnt_t, (2 * b + j) * 2048) for j in range(2)]
              for b in range(2)]
        kt = [[View(xnt_t, (4 + 2 * b + j) * 2048) for j in range(2)]
              for b in range(2)]
        stgq = [View(sarena_t, 8192 + b * 2048) for b in range(2)]
        stgk = [View(sarena_t, 12288 + b * 2048) for b in range(2)]
        NPH = 18
        phat = [View(sarena_t, i * 512) for i in range(16)] + \
               [sbv(f"phat{i}", 128, 512, BF16, paS) for i in (16, 17)]
        wqk = [sbv(f"wqk{b}", 128, 2048, BF16, paS) for b in range(2)]
        mcols = sbv("mcols", 128, 32, BF16, paS)
        mstg = sbv("mstg", 32, 128, BF16, paS)
        rl = sbv("rl", 128, 8, F32, paS)

        for b in range(2):
            for j in range(2):
                P.v(lambda e, b=b, j=j: e.memset(kt[b][j][64:65, 0:T], 1.0),
                    wait=[(sS, XNF_S)], inc=[sV], tag="kones")
        KONES_V = sV.n

        phslot = {}
        slot_use = {}
        seq = [0]

        for p in range(8):
            b = p % 2
            P.g(lambda e, p=p, b=b: e.dma_start(
                wqk[b].ap(0, 128, [[128, 8], [1, 128]]),
                qkvw[:, p * 128:(p + 1) * 128]
                .rearrange("(dc p) c -> p dc c", p=128)),
                wait=[(sT, st8.get(f"a0last{b}", 0)),
                      (sV, st8.get(f"a0lastv{b}", 0)),
                      (sS, LN0_S)],
                inc16=[dG], tag=f"a0wq{p}")
            P.g(lambda e, p=p, b=b: e.dma_start(
                wqk[b].ap(0, 128, [[128, 8], [1, 128]], c0=1024),
                qkvw[:, D + p * 128: D + (p + 1) * 128]
                .rearrange("(dc p) c -> p dc c", p=128)),
                inc16=[dG], tag=f"a0wk{p}")
            WQK_G = dG.n
            for qk in range(2):
                stg = (stgq if qk == 0 else stgk)[b]
                for tb in range(4):
                    k2 = (qk * 4 + tb) % 2
                    pb = bank(k2)
                    for dc in range(NDC):
                        P.t(lambda e, qk=qk, tb=tb, dc=dc, pb=pb, b=b:
                            e.matmul(
                                pb,
                                wqk[b][0:128, qk * 1024 + dc * 128:
                                       qk * 1024 + (dc + 1) * 128],
                                xnf[0:128, dc * XND + XPAD + tb * 512:
                                    dc * XND + XPAD + (tb + 1) * 512],
                                start=(dc == 0), stop=(dc == NDC - 1)),
                            wait=[(dG, WQK_G), (sS, XNF_S),
                                  (sS, st8.get(f"a0ev{k2}", 0)),
                                  (sV, st8.get(f"a0mx{k2}", 0))],
                            inc=[sT], tag=f"a0qkv{p}{qk}{tb}")
                    tv = sT.n
                    bcol = p + 8 * qk
                    P.s(lambda e, pb=pb, stg=stg, tb=tb, bcol=bcol:
                        e.activation(stg[0:128, ts(tb, 512)], pb, AF.Identity,
                                     bias=qkvbT_sb[0:128, bcol:bcol + 1],
                                     scale=1.0),
                        wait=[(sT, tv), (dW, CONST_W),
                              (sV, st8.get(f"a0stg{qk}{b}", 0))],
                        inc=[sS], tag=f"a0st{p}{qk}{tb}")
                    st8[f"a0ev{k2}"] = sS.n
            EV_S = sS.n
            for qk in range(2):
                stg = (stgq if qk == 0 else stgk)[b]
                dd = (qt if qk == 0 else kt)[b]
                P.s(lambda e, stg=stg, dd=dd: e.dma_start(
                        dd[0][0:64, 0:T], stg[0:64, 0:T]),
                    wait=[(sS, EV_S)], inc16=[dV], tag=f"a0d{p}{qk}a")
                P.s(lambda e, stg=stg, dd=dd: e.dma_start(
                        dd[1][0:64, 0:T], stg[64:128, 0:T]),
                    inc16=[dV], tag=f"a0d{p}{qk}b")
                st8[f"a0stg{qk}{b}"] = dV.n
            QK_DV = dV.n
            for ti in range(NTC):
                ns = 16 * (ti + 1)
                for hh in range(2):
                    k2 = (2 * ti + hh) % 2
                    P.t(lambda e, ti=ti, ns=ns, hh=hh, k2=k2, b=b: e.matmul(
                            bank(k2, w=ns),
                            qt[b][hh][0:64, ts(ti, 128)],
                            kt[b][hh].ap(0, 64, [[8, ns]]),
                            start=True, stop=True),
                        wait=[(dV, QK_DV), (sV, st8.get(f"a0mx{k2}", 0)),
                              (sS, st8.get(f"a0ev{k2}", 0))],
                        inc=[sT], tag=f"a0sp{p}{ti}{hh}")
                    tv = sT.n
                    P.v(lambda e, ns=ns, k2=k2: e.tensor_tensor(
                            bank(k2, w=16, off=ns - 16),
                            bank(k2, w=16, off=ns - 16),
                            c_sampm[0:128, 0:16], ALU.add),
                        wait=[(sT, tv), (dW, CONST_W)], inc=[sV],
                        tag=f"a0spm{p}{ti}{hh}")
                    P.v(lambda e, ti=ti, ns=ns, hh=hh, k2=k2: e.tensor_reduce(
                            mcols[0:128, hh * 16 + ti: hh * 16 + ti + 1],
                            bank(k2, w=ns), AX.X, ALU.max),
                        inc=[sV], tag=f"a0mx{p}{ti}{hh}")
                    st8[f"a0mx{k2}"] = sV.n
            MX_V = sV.n
            P.t(lambda e: e.transpose(PT0[0:32, 0:128], mcols[0:128, 0:32],
                                      c_identn[0:128, 0:128]),
                wait=[(sV, MX_V), (sS, st8.get("a0mse", 0))],
                inc=[sT], tag=f"a0mt{p}")
            tv = sT.n
            P.s(lambda e: e.copy(mstg[0:32, 0:128], PT0[0:32, 0:128]),
                wait=[(sT, tv)], inc=[sS], tag=f"a0mse{p}")
            st8["a0mse"] = sS.n
            sv = sS.n
            for hh in range(2):
                dst = qt[b][hh]
                P.s(lambda e, hh=hh, dst=dst: e.dma_start(
                        dst.ap(64, 1, [[128, 16], [1, 128]]),
                        mstg[hh * 16:(hh + 1) * 16, 0:128]),
                    wait=[(sS, sv)], inc16=[dV], tag=f"a0md{p}{hh}")
            MF_DV = dV.n
            for hh in range(2):
                hcur = 2 * p + hh
                qq, kk = qt[b][hh], kt[b][hh]
                for tb in range(4):
                    nsi = 4 * tb + 4
                    for si in range(nsi):
                        k2 = 2 + (si % 2)
                        pb = bank(k2)
                        P.t(lambda e, si=si, tb=tb, qq=qq, kk=kk, pb=pb:
                            e.matmul(pb, kk[0:65, ts(si, 128)],
                                     qq[0:65, ts(tb, 512)],
                                     start=True, stop=True),
                            wait=[(dV, MF_DV), (sV, KONES_V),
                                  (sS, st8.get(f"a0pe{k2}", 0))],
                            inc=[sT], tag=f"a0s{p}{hh}{tb}{si}")
                        tv = sT.n
                        wvx = []
                        if si >= 4 * tb:
                            kk4 = si - 4 * tb
                            P.v(lambda e, kk4=kk4, pb=pb: e.tensor_tensor(
                                    pb, pb, c_cmask4[0:128, ts(kk4, 512)],
                                    ALU.add),
                                wait=[(sT, tv), (dW, CONST_W)], inc=[sV],
                                tag=f"a0msk{p}")
                            wvx = [(sV, sV.n)]
                        slot = seq[0] % NPH
                        seq[0] += 1
                        ph = phat[slot]
                        P.s(lambda e, pb=pb, ph=ph: e.activation(
                                ph[0:128, 0:512], pb, AF.Exp, bias=0.0,
                                scale=0.125),
                            wait=[(sT, tv),
                                  (sT, slot_use.get(slot, 0))] + wvx,
                            inc=[sS], tag=f"a0e{p}{hh}{tb}{si}")
                        st8[f"a0pe{k2}"] = sS.n
                        phslot[si] = (slot, sS.n)
                    for tix in range(4):
                        ti = 4 * tb + tix
                        kpv = 4 + (tix % 2)
                        pvb = bank(kpv)
                        for si in range(ti + 1):
                            slot, pr = phslot[si]
                            P.t(lambda e, slot=slot, tix=tix, si=si, ti=ti,
                                hcur=hcur, pvb=pvb: e.matmul(
                                    pvb[:, 0:VW],
                                    phat[slot][0:128, ts(tix, 128)],
                                    varena[0:128,
                                           si * H * VW + hcur * VW:
                                           si * H * VW + (hcur + 1) * VW],
                                    start=(si == 0), stop=(si == ti)),
                                wait=[(sS, pr), (sS, V_S),
                                      (sV, st8.get(f"a0pv{kpv}", 0))],
                                inc=[sT], tag=f"a0pv{p}{hh}{ti}{si}")
                            slot_use[slot] = sT.n
                        tv = sT.n
                        P.v(lambda e, tix=tix, pvb=pvb: e.reciprocal(
                                rl[0:128, tix:tix + 1], pvb[:, DH:DH + 1]),
                            wait=[(sT, tv)], inc=[sV], tag=f"a0rl{p}")
                        P.v(lambda e, ti=ti, tix=tix, hcur=hcur, pvb=pvb:
                            e.tensor_scalar(
                                attn[0:128, ti * 1024 + hcur * 64:
                                     ti * 1024 + (hcur + 1) * 64],
                                pvb[:, 0:DH], rl[0:128, tix:tix + 1], None,
                                ALU.mult),
                            inc=[sV], tag=f"a0n{p}{hh}{ti}")
                        st8[f"a0pv{kpv}"] = sV.n
            st8[f"a0last{b}"] = sT.n
            st8[f"a0lastv{b}"] = sV.n
        ATT0_V = sV.n
        ATT0_S = sS.n
        ATT0_T = sT.n
        paS.close()
        taps[2] = ("arena_tm", attn, 1024)

    if stages >= 3:
        plkS = contextlib.ExitStack()
        w2oh = [View(xnt_t, 0), View(xnt_t, 2048)]
        ptmp = View(xnt_t, 4096)
        wtm = View(xnt_t, 6144)
        Oraw = View(xnt_t, 10240)
        ostg = View(xnt_t, 12288)
        Lu = View(sarena_t, 0)
        r2h = [View(sarena_t, 3072), View(sarena_t, 5120)]
        w2oraw = [View(sarena_t, 7168), View(sarena_t, 9216)]
        wbig = View(sarena_t, 11264)
        wfmbuf = [sbv(f"wfmbuf{i}", 64, 512, BF16, plkS) for i in range(2)]
        lsq = sbv("lsq", 128, 96, F32, plkS)
        lss = sbv("lss", 128, 64, F32, plkS)
        psumh = sbv("psumh", 36, 16, BF16, plkS)
        psumf = sbv("psumf", 36, 16, F32, plkS)
        ptsb = sbv("ptsb", 16, 36, BF16, plkS)
        gtsb = sbv("gtsb", 16, 36, BF16, plkS)
        gh = sbv("gh", 36, 16, BF16, plkS)
        iscr = sbv("iscr", 128, 128, BF16, plkS)
        gsig = sbv("gsig", 128, 16, F32, plkS)
        gtmp = sbv("gtmp", 128, 16, F32, plkS)

        ATTG = ((sV, ATT0_V), (sS, ATT0_S))

        off = 0
        wpos = {}
        for name, src, kch, w in (("w1", w1w, 16, 64), ("w2", w2w, 8, 64),
                                  ("r1", r1w, 8, 64), ("r2", r2w, 8, 64),
                                  ("mg", mgw, 8, 16)):
            wpos[name] = off
            P.g(lambda e, src=src, kch=kch, w=w, off=off: e.dma_start(
                wbig.ap(0, 128, [[w, kch], [1, w]], c0=off),
                src.rearrange("(k p) c -> p k c", p=128)),
                wait=[(sT, ATT0_T)] + list(ATTG), inc16=[dG], tag=f"pw{name}")
            off += kch * w
        PW_G = dG.n

        for wi, name in enumerate(("w1", "w2", "r1", "r2")):
            if name == "w1":
                parts = [(dc, wpos["w1"] + dc * 64, -2) for dc in range(8)] + \
                        [(dc, wpos["w1"] + (8 + dc) * 64, -1) for dc in range(8)]
            else:
                parts = [(dc, wpos[name] + dc * 64, 0) for dc in range(8)]
            n_parts = len(parts)
            for tb in range(4):
                k2 = (wi * 4 + tb) % 2
                pb = bank(k2)
                for i, (dc, woff, sh) in enumerate(parts):
                    P.t(lambda e, dc=dc, woff=woff, sh=sh, tb=tb, pb=pb,
                        i=i, n_parts=n_parts: e.matmul(
                            pb[0:64, :],
                            wbig[0:128, woff:woff + 64],
                            xnf[0:128, dc * XND + XPAD + tb * 512 + sh:
                                dc * XND + XPAD + (tb + 1) * 512 + sh],
                            start=(i == 0), stop=(i == n_parts - 1)),
                        wait=[(dG, PW_G), (sS, XNF_S), (sV, PAD_V),
                              (sS, st8.get(f"pwe{k2}", 0))],
                        inc=[sT], tag=f"pw{name}{tb}")
                tv = sT.n
                wf = wfmbuf[tb % 2]
                P.s(lambda e, pb=pb, wf=wf: e.copy(wf[0:64, 0:512],
                                                   pb[0:64, :]),
                    wait=[(sT, tv), (sT, st8.get(f"pwt{tb % 2}", 0))],
                    inc=[sS], tag=f"pwe{name}{tb}")
                st8[f"pwe{k2}"] = sS.n
                sv = sS.n
                for k in range(4):
                    tc = tb * 4 + k
                    i2 = wi * 16 + tc
                    pt = (PT0 if i2 % 2 == 0 else PT1)[0:128, 128:192]
                    P.t(lambda e, k=k, wf=wf, pt=pt: e.transpose(
                            pt, wf[0:64, ts(k, 128)], c_ident[0:64, 0:64]),
                        wait=[(sS, sv), (sS, st8.get(f"pwte{i2 - 2}", 0))],
                        inc=[sT], tag=f"pwt{name}{tc}")
                    tv2 = sT.n
                    P.s(lambda e, tc=tc, wi=wi, pt=pt: e.copy(
                            wtm[0:128, tc * 256 + wi * 64:
                                tc * 256 + (wi + 1) * 64],
                            bass.AP(pt.tensor, pt.offset,
                                    [[1024, 128], [1, 64]])),
                        wait=[(sT, tv2)], inc=[sS], tag=f"pwte{name}{tc}")
                    st8[f"pwte{i2}"] = sS.n
                st8[f"pwt{tb % 2}"] = sT.n
        WTM_S = sS.n

        IGRP = [(0, 3, 1, 0), (1, 2, 2, 3), (2, 1, 3, 5)]
        for tc in range(NTC):
            for lt in range(2):
                aoff = tc * 256 + (0 if lt == 0 else 128)
                luoff = tc * 192 + lt * 96
                for src0 in range(2):
                    toff = 0 if src0 == 0 else 96
                    for (i, nj, j0, po) in IGRP:
                        p1o = aoff + (0 if src0 == 0 else 64)
                        p2o = aoff + (64 if src0 == 0 else 0)
                        P.v(lambda e, i=i, nj=nj, j0=j0, p1o=p1o, p2o=p2o,
                            po=po, toff=toff: e.tensor_tensor(
                                Oraw.ap(0, 128, [[6, 16], [1, nj]],
                                        c0=toff + po),
                                wtm.ap(0, 128, [[4, 16], [0, nj]],
                                       c0=p1o + i),
                                wtm.ap(0, 128, [[4, 16], [1, nj]],
                                       c0=p2o + j0),
                                ALU.mult),
                            wait=[(sS, WTM_S), (sT, ATT0_T),
                                  (sV, st8.get("lu_guard", 0)),
                                  (sS, st8.get("lu_guard_s", 0))],
                            inc=[sV], tag=f"ext{tc}{lt}")
                P.v(lambda e: e.tensor_tensor(
                        Oraw.ap(0, 128, [[1, 96]], c0=192),
                        Oraw.ap(0, 128, [[1, 96]], c0=0),
                        Oraw.ap(0, 128, [[1, 96]], c0=96),
                        ALU.subtract),
                    inc=[sV], tag=f"extL{tc}{lt}")
                lv = sV.n
                P.s(lambda e: e.activation(
                        lsq[0:128, 0:96],
                        Oraw.ap(0, 128, [[1, 96]], c0=192), AF.Square),
                    wait=[(sV, lv)], inc=[sS], tag=f"nsq{tc}{lt}")
                sv = sS.n
                P.v(lambda e: e.tensor_reduce(
                        lss[0:128, 0:16],
                        lsq.ap(0, 128, [[6, 16], [1, 6]]),
                        AX.X, ALU.add),
                    wait=[(sS, sv)], inc=[sV], tag=f"nrd{tc}{lt}")
                vv = sV.n
                P.s(lambda e: e.activation(lss[0:128, 16:32],
                                           lss[0:128, 0:16], AF.Sqrt),
                    wait=[(sV, vv)], inc=[sS], tag=f"nsr{tc}{lt}")
                sv = sS.n
                P.v(lambda e: e.tensor_scalar(lss[0:128, 16:32],
                                              lss[0:128, 16:32],
                                              1e-12, None, ALU.max),
                    wait=[(sS, sv)], inc=[sV], tag=f"nmx{tc}{lt}")
                P.v(lambda e: e.reciprocal(lss[0:128, 32:48],
                                           lss[0:128, 16:32]),
                    inc=[sV], tag=f"nrc{tc}{lt}")
                P.v(lambda e, luoff=luoff: e.tensor_tensor(
                        Lu.ap(0, 128, [[1, 96]], c0=luoff),
                        Oraw.ap(0, 128, [[1, 96]], c0=192),
                        lss.ap(0, 128, [[1, 16], [0, 6]], c0=32),
                        ALU.mult),
                    inc=[sV], tag=f"nlu{tc}{lt}")
                st8["lu_guard"] = sV.n
                st8["lu_guard_s"] = sS.n
        LU_V = sV.n

        for tc in range(NTC):
            for lt in range(2):
                luoff = tc * 192 + lt * 96
                base = 0 if lt == 1 else 1024
                for i in range(6):
                    P.v(lambda e, luoff=luoff, base=base, i=i: e.tensor_tensor(
                            Oraw.ap(0, 128, [[64, 16], [1, 6]],
                                    c0=base + i * 6),
                            Lu.ap(0, 128, [[6, 16], [0, 6]], c0=luoff + i),
                            Lu.ap(0, 128, [[6, 16], [1, 6]], c0=luoff),
                            ALU.mult),
                        wait=[(sV, LU_V), (dS, st8.get("o_guard", 0))],
                        inc=[sV], tag=f"out{tc}{lt}{i}")
            OV = sV.n
            for blk in range(16):
                P.s(lambda e, blk=blk: e.dma_start_transpose(
                        ostg[0:128, ts(blk, 128)],
                        Oraw[0:128, ts(blk, 128)]),
                    wait=[(sV, OV), (dG, st8.get("osp_prev", 0))],
                    inc16=[dS], tag=f"ot{tc}{blk}")
            st8["o_guard"] = dS.n
            P.g(lambda e, tc=tc: e.dma_start(
                    bass.AP(r2sp.tensor, tc * 128,
                            [[T, 128], [128 * T, 16], [1, 128]]),
                    ostg.ap(0, 128, [[128, 16], [1, 128]])),
                wait=[(dS, dS.n)], inc16=[dG], tag=f"osp{tc}")
            st8["osp_prev"] = dG.n
        OSP_G = dG.n

        for pg in range(8):
            for hh in range(2):
                P.g(lambda e, pg=pg, hh=hh: e.dma_start(
                        r2h[hh][0:36, 0:T],
                        r2sp[pg, hh * 64: hh * 64 + 36, :]),
                    wait=[(dG, OSP_G), (sT, st8.get("pl_lastT", 0)),
                          (sV, st8.get("pl_lastV", 0)),
                          (sV, LU_V)],
                    inc16=[dG], tag=f"plr{pg}{hh}")
                P.g(lambda e, pg=pg, hh=hh: e.dma_start(
                        w2oraw[hh][0:36, 0:T],
                        r2sp[8 + pg, hh * 64: hh * 64 + 36, :]),
                    inc16=[dG], tag=f"plw{pg}{hh}")
            RS_G = dG.n
            for hh in range(2):
                h = 2 * pg + hh
                for sl in range(4):
                    k2 = sl % 2
                    pb = bank(k2)
                    P.t(lambda e, hh=hh, sl=sl, pb=pb: e.matmul(
                            pb[0:36, :], mjj[0:36, 0:36],
                            w2oraw[hh][0:36, ts(sl, 512)],
                            start=True, stop=True),
                        wait=[(dG, RS_G), (sV, MJJ_V),
                              (sS, st8.get(f"mje{k2}", 0))],
                        inc=[sT], tag=f"mj{pg}{hh}{sl}")
                    tv = sT.n
                    P.s(lambda e, hh=hh, sl=sl, pb=pb: e.copy(
                            w2oh[hh][0:36, ts(sl, 512)], pb[0:36, :]),
                        wait=[(sT, tv)], inc=[sS], tag=f"mje{pg}{hh}{sl}")
                    st8[f"mje{k2}"] = sS.n
                W2O_S = sS.n
                P.v(lambda e, hh=hh, h=h: e.tensor_tensor(
                        ptmp[0:36, 0:T], w2oh[hh][0:36, 0:T],
                        ddbc.ap(0, 36, [[0, 16], [1, 128]], c0=h * 128),
                        ALU.mult),
                    wait=[(sS, W2O_S), (sS, DD_S),
                          (sV, st8.get("ptmp_guard", 0))],
                    inc=[sV], tag=f"ph{pg}{hh}")
                vv = sV.n
                P.v(lambda e: e.tensor_reduce(
                        psumf[0:36, 0:16],
                        ptmp.ap(0, 36, [[128, 16], [1, 128]]),
                        AX.X, ALU.add),
                    wait=[(sV, vv)], inc=[sV], tag=f"pr{pg}{hh}")
                st8["ptmp_guard"] = sV.n
                P.v(lambda e: e.tensor_copy(psumh[0:36, 0:16],
                                            psumf[0:36, 0:16]),
                    inc=[sV], tag=f"prc{pg}{hh}")
                vv = sV.n
                P.t(lambda e: e.transpose(PT0[0:16, 384:420],
                                          psumh[0:36, 0:16],
                                          c_ident[0:36, 0:36]),
                    wait=[(sV, vv), (sS, st8.get("gse", 0))], inc=[sT],
                    tag=f"gp{pg}{hh}")
                tv = sT.n
                P.s(lambda e: e.copy(ptsb[0:16, 0:36], PT0[0:16, 384:420]),
                    wait=[(sT, tv)], inc=[sS], tag=f"gpe{pg}{hh}")
                sv = sS.n
                P.t(lambda e, h=h: e.matmul(PS[0:16, 1024:1060],
                                            Th[0:16, ts(h, 16)],
                                            ptsb[0:16, 0:36],
                                            start=True, stop=True),
                    wait=[(sS, sv), (sV, TOE_V),
                          (sS, st8.get("gse2", 0)),
                          (sS, st8.get("a0pe2", 0))],
                    inc=[sT], tag=f"gm{pg}{hh}")
                tv = sT.n
                P.s(lambda e: e.copy(gtsb[0:16, 0:36],
                                     bank(2, w=36)[0:16, :]),
                    wait=[(sT, tv)], inc=[sS], tag=f"gme{pg}{hh}")
                st8["gse2"] = sS.n
                sv = sS.n
                P.t(lambda e: e.transpose(PT0[0:36, 420:436],
                                          gtsb[0:16, 0:36],
                                          c_ident[0:16, 0:16]),
                    wait=[(sS, sv)], inc=[sT], tag=f"gt{pg}{hh}")
                tv = sT.n
                P.s(lambda e: e.copy(gh[0:36, 0:16], PT0[0:36, 420:436]),
                    wait=[(sT, tv)], inc=[sS], tag=f"gte{pg}{hh}")
                st8["gse"] = sS.n
                G_S = sS.n
                for c in range(16):
                    kc = 3 + (c % 2)
                    pb = bank(kc)
                    P.t(lambda e, hh=hh, c=c, pb=pb: e.matmul(
                            pb[:, 0:128], r2h[hh][0:36, ts(c, 128)],
                            w2oh[hh][0:36, ts(c, 128)],
                            start=True, stop=True),
                        wait=[(sS, W2O_S),
                              (sV, st8.get(f"ince{kc}", 0))],
                        inc=[sT], tag=f"inc{pg}{hh}{c}")
                    P.t(lambda e, hh=hh, c=c, pb=pb: e.matmul(
                            pb[:, 128:144], r2h[hh][0:36, ts(c, 128)],
                            gh[0:36, 0:16], start=False, stop=True),
                        wait=[(sS, G_S)], inc=[sT], tag=f"int{pg}{hh}{c}")
                    tv = sT.n
                    P.v(lambda e, h=h, c=c, pb=pb: e.scalar_tensor_tensor(
                            iscr[0:128, 0:128], pb[:, 0:128], 1.0,
                            toe[0:128, ts(h, 128)],
                            ALU.mult, ALU.mult,
                            accum_out=msc[0:128, h * 16 + c: h * 16 + c + 1]),
                        wait=[(sT, tv), (sV, TOE_V)], inc=[sV],
                        tag=f"intra{pg}{hh}{c}")
                    P.v(lambda e, h=h, c=c, pb=pb: e.scalar_tensor_tensor(
                            msc[0:128, h * 16 + c: h * 16 + c + 1],
                            pb[:, 128 + c:129 + c],
                            dpow[0:128, h:h + 1],
                            msc[0:128, h * 16 + c: h * 16 + c + 1],
                            ALU.mult, ALU.add),
                        wait=[(sS, DPOW_S)], inc=[sV], tag=f"comb{pg}{hh}{c}")
                    st8[f"ince{kc}"] = sV.n
            st8["pl_lastT"] = sT.n
            st8["pl_lastV"] = sV.n
        MSC_V = sV.n
        taps[21] = ("small", msc, 128, 256)

        for tc in range(NTC):
            kc = tc % 2
            pb = bank(kc, w=16)
            for dc in range(NDC):
                P.t(lambda e, tc=tc, dc=dc, pb=pb: e.matmul(
                        pb,
                        xnf[0:128, dc * XND + XPAD + tc * 128:
                            dc * XND + XPAD + (tc + 1) * 128],
                        wbig[0:128, wpos["mg"] + dc * 16:
                             wpos["mg"] + (dc + 1) * 16],
                        start=(dc == 0), stop=False),
                    wait=[(dG, PW_G), (sS, XNF_S),
                          (sS, st8.get(f"mje{kc}", 0)),
                          (sV, st8.get(f"ggv{kc}", 0)),
                          (sV, st8.get(f"ince{3 + kc}", 0))],
                    inc=[sT], tag=f"gg{tc}{dc}")
            P.t(lambda e, tc=tc, pb=pb: e.matmul(
                    pb, c_ones[0:1, ts(tc, 128)], rowg[0:1, 0:16],
                    start=False, stop=True),
                wait=[(sS, ROWS_S)], inc=[sT], tag=f"ggb{tc}")
            tv = sT.n
            P.s(lambda e, pb=pb: e.activation(gsig[0:128, 0:16], pb,
                                              AF.Sigmoid),
                wait=[(sT, tv), (sV, st8.get("ggs_g", 0))], inc=[sS],
                tag=f"ggs{tc}")
            sv = sS.n
            P.v(lambda e, tc=tc: e.tensor_tensor(
                    gtmp[0:128, 0:16],
                    msc.ap(0, 128, [[16, 16], [1, 1]], c0=tc),
                    mscale_bc[0:128, 0:16], ALU.mult),
                wait=[(sV, MSC_V), (dW, MSCALE_W), (sS, sv),
                      (sV, st8.get("gt_guard", 0))],
                inc=[sV], tag=f"gg2{tc}")
            vv = sV.n
            P.s(lambda e: e.activation(gtmp[0:128, 0:16], gtmp[0:128, 0:16],
                                       AF.Sigmoid),
                wait=[(sV, vv)], inc=[sS], tag=f"gg3{tc}")
            sv = sS.n
            P.v(lambda e: e.tensor_tensor(gtmp[0:128, 0:16],
                                          gtmp[0:128, 0:16],
                                          gsig[0:128, 0:16], ALU.mult),
                wait=[(sS, sv)], inc=[sV], tag=f"gg4{tc}")
            st8["ggs_g"] = sV.n
            P.v(lambda e, tc=tc: e.tensor_reduce(
                    gated[0:128, tc:tc + 1], gtmp[0:128, 0:16], AX.X,
                    ALU.add),
                inc=[sV], tag=f"gg5{tc}")
            st8["gt_guard"] = sV.n
            st8[f"ggv{kc}"] = sV.n
        sv = sV.n
        P.s(lambda e: e.activation(gated[0:128, 0:16], gated[0:128, 0:16],
                                   AF.Identity, bias=0.0, scale=1.0 / 16.0),
            wait=[(sV, sv)], inc=[sS], tag="ggs6")
        GATED_S = sS.n
        taps[22] = ("small", gated, 128, 16)
        plkS.close()

    if stages >= 4:
        g3 = gate("p3")
        p3S = contextlib.ExitStack()
        atot = View(xnt_t, 0)
        mvwb = View(varena_t, 0)
        outwb = View(varena_t, 8192)
        P.g(lambda e: e.dma_start(
            mvwb.ap(0, 128, [[1024, 8], [1, 1024]]),
            mvw.rearrange("(dc p) c -> p dc c", p=128)),
            wait=[g3], inc16=[dG], tag="mvw")
        P.g(lambda e: e.dma_start(
            outwb.ap(0, 128, [[1024, 8], [1, 1024]]),
            outw.rearrange("(dc p) c -> p dc c", p=128)),
            inc16=[dG], tag="outw")
        MVW_G = dG.n
        for tc in range(NTC):
            for cb in range(2):
                k2 = (tc * 2 + cb) % 2
                pb = bank(k2)
                for dc in range(NDC):
                    P.t(lambda e, tc=tc, cb=cb, dc=dc, pb=pb: e.matmul(
                            pb,
                            xnf[0:128, dc * XND + XPAD + tc * 128:
                                dc * XND + XPAD + (tc + 1) * 128],
                            mvwb[0:128, dc * 1024 + cb * 512:
                                 dc * 1024 + (cb + 1) * 512],
                            start=(dc == 0), stop=False),
                        wait=[(dG, MVW_G), g3,
                              (sV, st8.get(f"mvev{k2}", 0))],
                        inc=[sT], tag=f"mv{tc}{cb}")
                P.t(lambda e, tc=tc, cb=cb, pb=pb: e.matmul(
                        pb, c_ones[0:1, ts(tc, 128)],
                        rowm[0:1, cb * 512:(cb + 1) * 512],
                        start=False, stop=True),
                    inc=[sT], tag=f"mvb{tc}{cb}")
                tv = sT.n
                P.v(lambda e, tc=tc, cb=cb, pb=pb: e.scalar_tensor_tensor(
                        atot[0:128, tc * 1024 + cb * 512:
                             tc * 1024 + (cb + 1) * 512],
                        pb, gated[0:128, tc:tc + 1],
                        attn[0:128, tc * 1024 + cb * 512:
                             tc * 1024 + (cb + 1) * 512],
                        ALU.mult, ALU.add),
                    wait=[(sT, tv), (sS, GATED_S)], inc=[sV],
                    tag=f"mvev{tc}{cb}")
                st8[f"mvev{k2}"] = sV.n
        ATOT_V = sV.n
        for tc in range(NTC):
            for dc in range(NDC):
                P.s(lambda e, tc=tc, dc=dc: e.dma_start_transpose(
                        attn[0:128, dc * 2048 + tc * 128:
                             dc * 2048 + (tc + 1) * 128],
                        atot[0:128, tc * 1024 + dc * 128:
                             tc * 1024 + (dc + 1) * 128]),
                    wait=[(sV, ATOT_V)], inc16=[dS], tag=f"att{tc}{dc}")
        _w = dS.n
        P.s(lambda e: e.nop(), wait=[(dS, _w)], inc=[sS], tag="atfin")
        ATFM_S = sS.n
        for tc in range(NTC):
            for cb in range(2):
                k2 = (tc * 2 + cb) % 2
                pb = bank(2 + k2)
                for dc in range(NDC):
                    P.t(lambda e, tc=tc, cb=cb, dc=dc, pb=pb: e.matmul(
                            pb,
                            attn[0:128, dc * 2048 + tc * 128:
                                 dc * 2048 + (tc + 1) * 128],
                            outwb[0:128, dc * 1024 + cb * 512:
                                  dc * 1024 + (cb + 1) * 512],
                            start=(dc == 0), stop=False),
                        wait=[(dG, MVW_G), (sS, ATFM_S),
                              (sS, st8.get(f"opev{k2}", 0))],
                        inc=[sT], tag=f"op{tc}{cb}")
                P.t(lambda e, tc=tc, cb=cb, pb=pb: e.matmul(
                        pb, c_ones[0:1, ts(tc, 128)],
                        rowm[0:1, D + cb * 512: D + (cb + 1) * 512],
                        start=False, stop=True),
                    inc=[sT], tag=f"opb{tc}{cb}")
                tv = sT.n
                P.s(lambda e, tc=tc, cb=cb, pb=pb: e.copy(
                        sarena[0:128, tc * 1024 + cb * 512:
                               tc * 1024 + (cb + 1) * 512], pb),
                    wait=[(sT, tv)], inc=[sS], tag=f"opev{tc}{cb}")
                st8[f"opev{k2}"] = sS.n
        S0_S = sS.n
        p3S.close()
        scopeL0.close()
        taps[3] = ("arena_tm", sarena, 1024)

    ln1fin = {}
    if stages >= 5:
        g4 = gate("p4")
        p4S = contextlib.ExitStack()
        S127 = 1.0 / 127.0
        wgu = sbv("wgu", 128, 2048, BF16, p4S)
        hvt = sbv("hvt", 128, 2048, BF16, p4S)
        gsl = sbv("gsl", 128, 512, BF16, p4S)
        hvb = View(varena_t, 8192)
        g1t = sbv("g1t", 128, 1024, BF16, p4S)
        t1t = sbv("t1t", 128, 1024, BF16, p4S)
        x1t = [sbv(f"x1t{i}", 128, 1024, F32, p4S) for i in range(2)]
        xinb = [sbv(f"xinb{i}", 128, 1024, F32, p4S) for i in range(2)]
        xsq4 = sbv("xsq4", 128, 1024, F32, p4S)

        def wdv(hc):
            if hc < 16:
                return attn[0:128, hc * 1024:(hc + 1) * 1024]
            return varena[0:128, (hc - 16) * 1024:(hc - 15) * 1024]

        for hc in range(NHC):
            P.g(lambda e, hc=hc: e.dma_start(
                    wgu.ap(0, 128, [[128, 8], [1, 128]]),
                    wgT[:, ts(hc, 128)].rearrange("(dc p) c -> p dc c",
                                                  p=128)),
                wait=[g4, (sT, st8.get("p4w", 0))],
                inc16=[dG], tag=f"p4wg{hc}")
            P.g(lambda e, hc=hc: e.dma_start(
                    wgu.ap(0, 128, [[128, 8], [1, 128]], c0=1024),
                    wuT[:, ts(hc, 128)].rearrange("(dc p) c -> p dc c",
                                                  p=128)),
                inc16=[dG], tag=f"p4wu{hc}")
            WL_G = dG.n
            for tb in range(4):
                bg = tb % 2
                bu = 2 + (tb % 2)
                for which, bk in ((0, bg), (1, bu)):
                    for dc in range(NDC):
                        P.t(lambda e, dc=dc, bk=bk, which=which, tb=tb:
                            e.matmul(
                                bank(bk),
                                wgu[0:128, which * 1024 + dc * 128:
                                    which * 1024 + (dc + 1) * 128],
                                xnf[0:128, dc * XND + XPAD + tb * 512:
                                    dc * XND + XPAD + (tb + 1) * 512],
                                start=(dc == 0), stop=(dc == NDC - 1)),
                            wait=[(dG, WL_G), g4,
                                  (sV, st8.get(f"p4hv{bk}", 0))],
                            inc=[sT], tag=f"p4g{hc}{tb}{which}")
                GU_T = sT.n
                P.s(lambda e, bg=bg: e.activation(gsl[0:128, 0:512],
                                                  bank(bg), AF.Silu,
                                                  bias=0.0, scale=S127),
                    wait=[(sT, GU_T), (sV, st8.get("p4gsl", 0))],
                    inc=[sS], tag=f"p4sl{hc}{tb}")
                sv = sS.n
                P.v(lambda e, tb=tb, bu=bu: e.scalar_tensor_tensor(
                        hvt[0:128, ts(tb, 512)], bank(bu), S127,
                        gsl[0:128, 0:512], ALU.mult, ALU.mult),
                    wait=[(sS, sv), (dG, st8.get("p4hsd", 0))],
                    inc=[sV], tag=f"p4h{hc}{tb}")
                st8[f"p4hv{bg}"] = sV.n
                st8[f"p4hv{bu}"] = sV.n
                st8["p4gsl"] = sV.n
            hvv = sV.n
            st8["p4w"] = sT.n
            P.g(lambda e, hc=hc: e.dma_start(hvsp[hc, :, :],
                                             hvt[0:128, 0:T]),
                wait=[(sV, hvv)], inc16=[dG], tag=f"p4hs{hc}")
            st8["p4hsd"] = dG.n
        HV_G = dG.n
        for hc in range(NHC):
            P.g(lambda e, hc=hc: e.dma_start(wdv(hc), wdT[ts(hc, 128), :]),
                inc16=[dG], tag=f"p4wd{hc}")
        WD_G = dG.n

        def wdvh(hc, half):
            if hc < 16:
                return attn[0:128, hc * 1024 + half * 512:
                            hc * 1024 + (half + 1) * 512]
            return varena[0:128, (hc - 16) * 1024 + half * 512:
                          (hc - 16) * 1024 + (half + 1) * 512]

        for tgi in range(8):
            tc0 = 2 * tgi
            P.y(lambda e, tc0=tc0: e.dma_start(
                    hvb.ap(0, 128, [[256, NHC], [1, 256]]),
                    bass.AP(hvsp.tensor, tc0 * 128,
                            [[T, 128], [128 * T, NHC], [1, 256]])),
                wait=[(dG, HV_G), (sT, st8.get("p4yw", 0)),
                      (sS, st8.get(f"p4x{tc0 - 2}", 0)),
                      (sS, st8.get(f"p4x{tc0 - 1}", 0))],
                inc16=[dW], tag=f"p4hr{tgi}")
            HR_W = dW.n
            for j in range(2):
                P.y(lambda e, tc=tc0 + j, j=j: e.dma_start(
                        xinb[j][0:128, 0:1024], x_hbm[ts(tc, 128), :]),
                    inc16=[dW], tag=f"p4xr{tc0 + j}")
            XR_W = dW.n
            for hc in range(NHC):
                for j in range(2):
                    for half in range(2):
                        bk = 2 * j + half
                        P.t(lambda e, j=j, half=half, bk=bk, hc=hc: e.matmul(
                                bank(bk),
                                hvb[0:128, hc * 256 + j * 128:
                                    hc * 256 + (j + 1) * 128],
                                wdvh(hc, half),
                                start=(hc == 0), stop=(hc == NHC - 1)),
                            wait=[(dW, HR_W), (dG, WD_G),
                                  (sS, st8.get(f"p4ge{bk}", 0))],
                            inc=[sT], tag=f"p4y{tgi}_{hc}_{bk}")
            Y_T = sT.n
            st8["p4yw"] = Y_T
            for j in range(2):
                tc = tc0 + j
                P.s(lambda e, j=j: e.activation(
                        g1t[0:128, 0:1024], PS[:, j * 1024:(j + 1) * 1024],
                        AF.Silu, bias=0.0, scale=S127),
                    wait=[(sT, Y_T), (sV, st8.get("p4g1v", 0))],
                    inc=[sS], tag=f"p4ge{tc}")
                st8[f"p4ge{2 * j}"] = sS.n
                st8[f"p4ge{2 * j + 1}"] = sS.n
                sv = sS.n
                P.v(lambda e, tc=tc: e.tensor_tensor(
                        t1t[0:128, 0:1024], g1t[0:128, 0:1024],
                        sarena[0:128, ts(tc, 1024)], ALU.mult),
                    wait=[(sS, sv)], inc=[sV], tag=f"p4t1{tc}")
                vv = sV.n
                st8["p4g1v"] = vv
                P.v(lambda e, j=j: e.tensor_tensor(
                        x1t[j][0:128, 0:1024], xinb[j][0:128, 0:1024],
                        t1t[0:128, 0:1024], ALU.add),
                    wait=[(sV, vv), (dW, XR_W),
                          (dW, st8.get(f"p4spw{j}", 0))],
                    inc=[sV], tag=f"p4x1{tc}")
                x1v = sV.n
                P.y(lambda e, tc=tc, j=j: e.dma_start(
                        x1sp[ts(tc, 128), :], x1t[j][0:128, 0:1024]),
                    wait=[(sV, x1v)], inc16=[dW], tag=f"p4sp{tc}")
                st8[f"p4spw{j}"] = dW.n
                ln1fin[tc] = emit_ln(x1t[j][0:128, 0:1024],
                                     xnt[0:128, ts(tc, 1024)], D, 1e-5,
                                     slot=tc, xq=xsq4[0:128, 0:1024],
                                     wait_v=[(sV, x1v)], wait_s=[(sV, x1v)],
                                     tag=f"ln1_{tc}")
                st8[f"p4x{tc}"] = ln1fin[tc]
        LN1_S = sS.n
        p4S.close()
        taps[4] = ("x1sp",)

    if stages >= 6:
        gate("p5")
        XNF1_S = transposes_to_fm(ln1fin, "p5f")
        taps[5] = ("arena_tm", xnt, 1024)

    if stages >= 7:
        g6 = gate("p6")
        p6S = contextlib.ExitStack()
        phiwb = View(sarena_t, 0)
        rowp = sbv("rowp", 1, NEXP * PSLOT, BF16, p6S)
        escr = sbv("escr", 128, 2048, BF16, p6S)
        ctile = sbv("ctile", 128, 2048, BF16, p6S)
        ctstg = sbv("ctstg", 128, 2048, BF16, p6S)
        dsum = sbv("dsum", 128, 64, F32, p6S)
        csum = sbv("csum", 128, 32, F32, p6S)
        rdsum = sbv("rdsum", 128, 64, F32, p6S)
        rcsum = sbv("rcsum", 128, 32, F32, p6S)
        negm6 = sbv("negm6", 128, 4, F32, p6S)
        xstage = sbv("xstage", 128, 1024, F32, p6S)
        row2bf(phib_row, lambda c0, cw: rowp[0:1, c0:c0 + cw], NEXP * PSLOT,
               "rp")
        ROWP_V = sV.n
        P.s(lambda e: e.nop(), wait=[(sV, ROWP_V)], inc=[sS])
        ROWP_S = sS.n

        def dtile(tc):
            if tc < 8:
                return View(attn_t, tc * 2048)
            return View(varena_t, (tc - 8) * 2048)

        P.g(lambda e: e.dma_start(
                phiwb.ap(0, 128, [[2048, 8], [1, 2048]]),
                phiT.rearrange("(dc p) c -> p dc c", p=128)),
            wait=[g6], inc16=[dG], tag="p6phi")
        PHI_G = dG.n
        for tc in range(NTC):
            for eb in range(4):
                for dc in range(NDC):
                    P.t(lambda e, tc=tc, eb=eb, dc=dc: e.matmul(
                            bank(eb),
                            xnf[0:128, dc * XND + XPAD + tc * 128:
                                dc * XND + XPAD + (tc + 1) * 128],
                            phiwb[0:128, dc * 2048 + eb * 512:
                                  dc * 2048 + (eb + 1) * 512],
                            start=(dc == 0), stop=False),
                        wait=[(dG, PHI_G), g6, (sS, XNF1_S),
                              (sS, st8.get("p6e", 0)),
                              (sV, st8.get("p6v", 0))],
                        inc=[sT], tag=f"p6l{tc}{eb}")
                P.t(lambda e, tc=tc, eb=eb: e.matmul(
                        bank(eb), c_ones[0:1, ts(tc, 128)],
                        rowp[0:1, ts(eb, 512)],
                        start=False, stop=True),
                    wait=[(sS, ROWP_S)], inc=[sT], tag=f"p6lb{tc}{eb}")
            tv = sT.n
            P.v(lambda e, tc=tc: e.tensor_reduce(
                    negm6[0:128, tc % 4: tc % 4 + 1], PS[:, 0:2048], AX.X,
                    ALU.max, negate=True),
                wait=[(sT, tv), (sS, st8.get("p6eg", 0))],
                inc=[sV], tag=f"p6m{tc}")
            vv = sV.n
            P.s(lambda e, tc=tc: e.activation(
                    escr[0:128, 0:2048], PS[:, 0:2048], AF.Exp,
                    bias=negm6[0:128, tc % 4: tc % 4 + 1], scale=1.0),
                wait=[(sV, vv), (sV, st8.get("p6cv", 0))],
                inc=[sS], tag=f"p6e{tc}")
            st8["p6e"] = sS.n
            st8["p6eg"] = sS.n
            sv = sS.n
            P.v(lambda e: e.tensor_reduce(
                    dsum[0:128, 0:64],
                    escr.ap(0, 128, [[32, 64], [1, 32]]),
                    AX.X, ALU.add),
                wait=[(sS, sv)], inc=[sV], tag=f"p6ds{tc}")
            P.v(lambda e: e.tensor_reduce(
                    csum[0:128, 0:32],
                    escr.ap(0, 128, [[1, 32], [32, 64]]),
                    AX.X, ALU.add),
                inc=[sV], tag=f"p6cs{tc}")
            P.v(lambda e: e.reciprocal(rdsum[0:128, 0:64], dsum[0:128, 0:64]),
                inc=[sV])
            P.v(lambda e: e.reciprocal(rcsum[0:128, 0:32], csum[0:128, 0:32]),
                inc=[sV])
            vv = sV.n
            dt = dtile(tc)
            P.v(lambda e, dt=dt: e.tensor_tensor(
                    dt.ap(0, 128, [[1, 2048]]),
                    escr[0:128, 0:2048],
                    rdsum.ap(0, 128, [[1, 64], [0, 32]]),
                    ALU.mult),
                wait=[(sV, vv), (sT, st8.get("p6xt", 0))],
                inc=[sV], tag=f"p6d{tc}")
            P.v(lambda e: e.tensor_tensor(
                    ctile[0:128, 0:2048], escr[0:128, 0:2048],
                    rcsum.ap(0, 128, [[0, 64], [1, 32]]),
                    ALU.mult),
                wait=[(dS, st8.get("p6ctw", 0))],
                inc=[sV], tag=f"p6c{tc}")
            st8["p6v"] = sV.n
            st8["p6cv"] = sV.n
            cv = sV.n
            for blk in range(16):
                P.s(lambda e, blk=blk: e.dma_start_transpose(
                        ctstg[0:128, ts(blk, 128)],
                        ctile[0:128, ts(blk, 128)]),
                    wait=[(sV, cv), (dG, st8.get("ctsp_prev", 0))],
                    inc16=[dS], tag=f"ctt{tc}{blk}")
            st8["p6ctw"] = dS.n
            P.g(lambda e, tc=tc: e.dma_start(
                    bass.AP(ctsp.tensor, tc * 128,
                            [[2048, 128], [128 * 2048, 16], [1, 128]]),
                    ctstg.ap(0, 128, [[128, 16], [1, 128]])),
                wait=[(dS, dS.n)], inc16=[dG], tag=f"p6ct{tc}")
            st8["ctsp_prev"] = dG.n
        D_V = sV.n
        CT_G = dG.n
        for ec in range(16):
            for tc in range(NTC):
                dt = dtile(tc)
                for half in range(2):
                    P.t(lambda e, dt=dt, ec=ec, tc=tc, half=half: e.matmul(
                            bank(4 + half),
                            dt[0:128, ec * 128:(ec + 1) * 128],
                            xnt[0:128, tc * 1024 + half * 512:
                                tc * 1024 + (half + 1) * 512],
                            start=(tc == 0), stop=(tc == NTC - 1)),
                        wait=[(sV, D_V), (sS, LN1_S),
                              (sS, st8.get("p6xe", 0))],
                        inc=[sT], tag=f"p6x{ec}{tc}{half}")
            st8["p6xt"] = sT.n
            tv = sT.n
            P.s(lambda e: e.copy(xstage[0:128, 0:1024], PS[:, 2048:3072]),
                wait=[(sT, tv)], inc=[sS], tag=f"p6xe{ec}")
            st8["p6xe"] = sS.n
            sv = sS.n
            P.g(lambda e, ec=ec: e.dma_start(ccx[ts(ec, 128), :],
                                             xstage[0:128, 0:1024]),
                wait=[(sS, sv)], inc16=[dG], tag=f"p6xd{ec}")
        XP_G = dG.n
        p6S.close()

    if stages >= 8:
        g7 = gate("p7")
        p7S = contextlib.ExitStack()
        kvfm = sbv("kvfm", 128, 2048, BF16, p7S)
        qn1 = sbv("qn1", 128, 512, BF16, p7S)
        kn1 = sbv("kn1", 128, 512, BF16, p7S)
        rms1 = sbv("rms1", 128, 64, F32, p7S)
        xsq7 = sbv("xsq7", 128, 1024, F32, p7S)
        ggrep = sbv("ggrep", 128, 1, F32, p7S)
        mcols1 = sbv("mcols1", 128, 256, BF16, p7S)
        mstg1 = sbv("mstg1", 128, 256, BF16, p7S)
        rl1 = sbv("rl1", 128, 8, F32, p7S)
        qstg1 = [View(attn_t, 8192 + g * 2048) for g in range(4)]
        kstg1 = [View(varena_t, 8448 + g * 2048) for g in range(4)]
        qtp = [View(xnt_t, j * 2048) for j in range(8)]
        ktp = [View(sarena_t, j * 2048) for j in range(8)]
        expwb = [View(xnf_t, i * 8208) for i in range(2)]

        p7a = contextlib.ExitStack()
        wq1b = sbv("wq1b", 128, 8 * 1024, BF16, p7a)
        wkv1b = sbv("wkv1b", 128, 8 * 128, BF16, p7a)
        wk1b = sbv("wk1b", 128, 512, BF16, p7a)
        wv1b = sbv("wv1b", 128, 512, BF16, p7a)

        P.g(lambda e: e.dma_start(
                wq1b.ap(0, 128, [[1024, 8], [1, 1024]]),
                wq1.rearrange("(dc p) c -> p dc c", p=128)),
            wait=[g7], inc16=[dG], tag="p7wq")
        P.g(lambda e: e.dma_start(
                wkv1b.ap(0, 128, [[128, 8], [1, 128]]),
                wkv1.rearrange("(dc p) c -> p dc c", p=128)),
            inc16=[dG], tag="p7wkv")
        P.g(lambda e: e.dma_start(wk1b[0:128, 0:512], wk1), inc16=[dG])
        P.g(lambda e: e.dma_start(wv1b[0:128, 0:512], wv1), inc16=[dG])
        W7_G = dG.n
        P.y(lambda e: e.dma_start(rows_tmp[0:1, 0:DH], qg_row),
            wait=[g7], inc16=[dW])
        P.y(lambda e: e.dma_start(rows_tmp[0:1, 64:64 + ROPE], kg_row),
            inc16=[dW])
        wv = dW.n
        P.v(lambda e: e.tensor_tensor(rows_tmp[0:1, 128:128 + ROPE],
                                      rows_tmp[0:1, 0:ROPE],
                                      rows_tmp[0:1, 64:64 + ROPE], ALU.mult),
            wait=[(dW, wv)], inc=[sV], tag="p7gg")
        vv = sV.n
        P.y(lambda e: e.dma_start(bounce[3:4, 0:ROPE],
                                  rows_tmp[0:1, 128:128 + ROPE]),
            wait=[(sV, vv)], inc16=[dW])
        wv = dW.n
        P.g(lambda e: e.dma_start(
                ggrep[0:128, 0:1],
                bass.AP(bounce.tensor, 3 * 2048, [[0, 4], [1, 32]])),
            wait=[(dW, wv)], inc16=[dG], tag="p7ggr")
        GG_W = (dG, dG.n)
        for tb in range(4):
            k2 = tb % 2
            pb = bank(k2)
            for dc in range(NDC):
                P.t(lambda e, tb=tb, dc=dc, pb=pb: e.matmul(
                        pb, wkv1b[0:128, ts(dc, 128)],
                        xnf[0:128, dc * XND + XPAD + tb * 512:
                            dc * XND + XPAD + (tb + 1) * 512],
                        start=(dc == 0), stop=(dc == NDC - 1)),
                    wait=[(dG, W7_G), g7, (sS, st8.get(f"p7kv{k2}", 0)),
                          (sS, st8.get("p6xe", 0))],
                    inc=[sT], tag=f"p7kv{tb}")
            tv = sT.n
            P.s(lambda e, tb=tb, pb=pb: e.copy(kvfm[0:128, ts(tb, 512)], pb),
                wait=[(sT, tv)], inc=[sS], tag=f"p7kve{tb}")
            st8[f"p7kv{k2}"] = sS.n
        KV_S = sS.n
        for tc in range(NTC):
            for cb in range(2):
                for dc in range(NDC):
                    P.t(lambda e, tc=tc, cb=cb, dc=dc: e.matmul(
                            bank(cb),
                            xnf[0:128, dc * XND + XPAD + tc * 128:
                                dc * XND + XPAD + (tc + 1) * 128],
                            wq1b[0:128, dc * 1024 + cb * 512:
                                 dc * 1024 + (cb + 1) * 512],
                            start=(dc == 0), stop=(dc == NDC - 1)),
                        wait=[(dG, W7_G), (sS, KV_S),
                              (sS, st8.get("p7qe", 0)),
                              (sV, st8.get("p7qv", 0))],
                        inc=[sT], tag=f"p7q{tc}{cb}")
            tv = sT.n
            P.s(lambda e: e.activation(xsq7[0:128, 0:1024], PS[:, 0:1024],
                                       AF.Square),
                wait=[(sT, tv), (sV, st8.get("p7sqv", 0))], inc=[sS],
                tag=f"p7qs{tc}")
            sv = sS.n
            P.v(lambda e: e.tensor_reduce(
                    rms1[0:128, 0:16],
                    xsq7.ap(0, 128, [[64, 16], [1, 64]]),
                    AX.X, ALU.add),
                wait=[(sS, sv)], inc=[sV], tag=f"p7qr{tc}")
            st8["p7sqv"] = sV.n
            vv = sV.n
            P.v(lambda e: e.tensor_scalar(rms1[0:128, 16:32],
                                          rms1[0:128, 0:16],
                                          1.0 / DH, 1e-6, ALU.mult, ALU.add),
                wait=[(sV, vv)], inc=[sV], tag=f"p7qm{tc}")
            vv = sV.n
            P.s(lambda e: e.activation(rms1[0:128, 16:32], rms1[0:128, 16:32],
                                       AF.Sqrt),
                wait=[(sV, vv)], inc=[sS], tag=f"p7qq{tc}")
            sv = sS.n
            P.v(lambda e: e.reciprocal(rms1[0:128, 32:48],
                                       rms1[0:128, 16:32]),
                wait=[(sS, sv)], inc=[sV], tag=f"p7qc{tc}")
            vv = sV.n
            P.v(lambda e: e.tensor_tensor(
                    qn1[0:128, 0:512],
                    bass.AP(PS, 0, [[3072, 128], [64, 16], [1, 32]]),
                    rms1.ap(0, 128, [[1, 16], [0, 32]], c0=32),
                    ALU.mult),
                wait=[(sV, vv), (dS, st8.get("p7qdma", 0))],
                inc=[sV], tag=f"p7qn{tc}")
            st8["p7qv"] = sV.n
            qnv = sV.n
            P.s(lambda e: e.nop(), wait=[(sV, qnv)], inc=[sS])
            st8["p7qe"] = sS.n
            for g in range(4):
                P.s(lambda e, g=g, tc=tc: e.dma_start_transpose(
                        qstg1[g][0:128, ts(tc, 128)], qn1[0:128, ts(g, 128)]),
                    wait=[(sV, qnv)], inc16=[dS], tag=f"p7qt{tc}{g}")
            st8["p7qdma"] = dS.n
            P.t(lambda e, tc=tc: e.matmul(
                    bank(2), kvfm[0:128, ts(tc, 128)], wk1b[0:128, 0:512],
                    start=True, stop=True),
                wait=[(sS, KV_S), (sS, st8.get("p7ke", 0))],
                inc=[sT], tag=f"p7k{tc}")
            tv = sT.n
            P.s(lambda e: e.activation(xsq7[0:128, 0:512], bank(2),
                                       AF.Square),
                wait=[(sT, tv), (sV, st8.get("p7krv", 0))], inc=[sS],
                tag=f"p7ks{tc}")
            sv = sS.n
            P.v(lambda e: e.tensor_reduce(
                    rms1[0:128, 48:64],
                    xsq7.ap(0, 128, [[32, 16], [1, 32]]),
                    AX.X, ALU.add),
                wait=[(sS, sv)], inc=[sV], tag=f"p7kr{tc}")
            vv = sV.n
            P.v(lambda e: e.tensor_scalar(rms1[0:128, 48:64],
                                          rms1[0:128, 48:64],
                                          1.0 / ROPE, 1e-6, ALU.mult, ALU.add),
                wait=[(sV, vv)], inc=[sV], tag=f"p7km{tc}")
            vv = sV.n
            P.s(lambda e: e.activation(rms1[0:128, 48:64], rms1[0:128, 48:64],
                                       AF.Sqrt),
                wait=[(sV, vv)], inc=[sS], tag=f"p7kq{tc}")
            sv = sS.n
            P.v(lambda e: e.reciprocal(rms1[0:128, 48:64],
                                       rms1[0:128, 48:64]),
                wait=[(sS, sv)], inc=[sV], tag=f"p7kc{tc}")
            vv = sV.n
            P.v(lambda e: e.tensor_tensor(
                    kn1[0:128, 0:512], bank(2),
                    rms1.ap(0, 128, [[1, 16], [0, 32]], c0=48),
                    ALU.mult),
                wait=[(sV, vv), (dS, st8.get("p7kdma", 0))],
                inc=[sV], tag=f"p7kn{tc}")
            knv = sV.n
            st8["p7krv"] = knv
            P.s(lambda e: e.nop(), wait=[(sV, knv)], inc=[sS])
            st8["p7ke"] = sS.n
            for g in range(4):
                P.s(lambda e, g=g, tc=tc: e.dma_start_transpose(
                        kstg1[g][0:128, ts(tc, 128)], kn1[0:128, ts(g, 128)]),
                    wait=[(sV, knv)], inc16=[dS], tag=f"p7kt{tc}{g}")
            st8["p7kdma"] = dS.n
            P.t(lambda e, tc=tc: e.matmul(
                    bank(3), kvfm[0:128, ts(tc, 128)], wv1b[0:128, 0:512],
                    start=True, stop=True),
                wait=[(sS, KV_S), (sS, st8.get("p7vee", 0))],
                inc=[sT], tag=f"p7v{tc}")
            tv = sT.n
            P.s(lambda e, tc=tc: e.copy(
                    varena.ap(0, 128, [[VW1, 16], [1, VD1]], c0=tc * 528),
                    bank(3).rearrange("p (h d) -> p h d", h=16)),
                wait=[(sT, tv)], inc=[sS], tag=f"p7ve{tc}")
            st8["p7vee"] = sS.n
        QKV1_T = sT.n
        P.s(lambda e: e.nop(), wait=[(dS, dS.n)], inc=[sS], tag="p7qkfin")
        QKV1_S = sS.n
        p7a.close()
        P.v(lambda e: e.memset(
            varena.ap(0, 128, [[528, NTC], [VW1, 16], [1, 1]], c0=VD1),
            1.0), wait=[(sS, QKV1_S)], inc=[sV], tag="p7vones")
        VONES1_V = sV.n
        for g in range(4):
            P.s(lambda e, g=g: e.activation(kstg1[g][0:128, 0:T],
                                            kstg1[g][0:128, 0:T],
                                            AF.Identity, bias=0.0,
                                            scale=ggrep[0:128, 0:1]),
                wait=[GG_W, (sS, QKV1_S)], inc=[sS], tag=f"p7gg{g}")
        GGS_S = sS.n
        for n in range(2):
            P.g(lambda e, n=n: e.dma_start(
                    expwb[n].ap(0, 128, [[1024, 8], [1, 1024]]),
                    expw[n].rearrange("(dc p) c -> p dc c", p=128)),
                wait=[(sT, QKV1_T)], inc16=[dG], tag=f"p8w{n}")
        EXW_G = dG.n
        P.g(lambda e: e.collective_compute(
                "ReduceScatter", ALU.add,
                replica_groups=[list(range(N_CORES))],
                ins=[ccx], outs=[ccxrs]),
            wait=[(dG, XP_G)], inc=[sC], tag="RS")
        RS_C = sC.n
        for h in range(H):
            j, rb = h // 2, 64 * (h % 2)
            g, gi = h // 4, h % 4
            P.s(lambda e, j=j, rb=rb, g=g, gi=gi: e.dma_start(
                    qtp[j].ap(rb, 32, [[1, T]]),
                    qstg1[g][gi * 32:(gi + 1) * 32, 0:T]),
                wait=[(sS, QKV1_S)], inc16=[dV], tag=f"p7qd{h}")
            P.s(lambda e, j=j, rb=rb, g=g, gi=gi: e.dma_start(
                    ktp[j].ap(rb, 32, [[1, T]]),
                    kstg1[g][gi * 32:(gi + 1) * 32, 0:T]),
                wait=[(sS, GGS_S)], inc16=[dV], tag=f"p7kd{h}")
        PAIR_DV = dV.n
        for j in range(8):
            for rb in (32, 96):
                P.v(lambda e, j=j, rb=rb: e.memset(
                        ktp[j][rb:rb + 1, 0:T], 1.0),
                    wait=[(dV, PAIR_DV)], inc=[sV], tag=f"p7ko{j}")
        KONES1_V = sV.n
        for h in range(H):
            j, rb = h // 2, 64 * (h % 2)
            for ti in range(NTC):
                ns = 16 * (ti + 1)
                k2 = (h * NTC + ti) % 2
                P.t(lambda e, j=j, rb=rb, ti=ti, ns=ns, k2=k2: e.matmul(
                        bank(k2, w=ns),
                        qtp[j].ap(rb, 32, [[1, 128]], c0=ti * 128),
                        ktp[j].ap(rb, 32, [[8, ns]]),
                        start=True, stop=True),
                    wait=[(dV, PAIR_DV), (sV, KONES1_V),
                          (sV, st8.get(f"p7mx{k2}", 0)),
                          (sS, st8.get("p7vee", 0))],
                    inc=[sT], tag=f"p7sp{h}{ti}")
                tv = sT.n
                P.v(lambda e, ns=ns, k2=k2: e.tensor_tensor(
                        bank(k2, w=16, off=ns - 16),
                        bank(k2, w=16, off=ns - 16),
                        c_sampm[0:128, 0:16], ALU.add),
                    wait=[(sT, tv)], inc=[sV], tag=f"p7spm{h}{ti}")
                P.v(lambda e, h=h, ti=ti, ns=ns, k2=k2: e.tensor_reduce(
                        mcols1[0:128, h * 16 + ti: h * 16 + ti + 1],
                        bank(k2, w=ns), AX.X, ALU.max),
                    inc=[sV], tag=f"p7mx{h}{ti}")
                st8[f"p7mx{k2}"] = sV.n
        MX1_V = sV.n
        for half in range(2):
            P.t(lambda e, half=half: e.transpose(
                    PT0[0:128, 0:128], mcols1[0:128, ts(half, 128)],
                    c_identn[0:128, 0:128]),
                wait=[(sV, MX1_V), (sS, st8.get("p7mse", 0))],
                inc=[sT], tag=f"p7mt{half}")
            tv = sT.n
            P.s(lambda e, half=half: e.copy(mstg1[0:128, ts(half, 128)],
                                            PT0[0:128, 0:128]),
                wait=[(sT, tv)], inc=[sS], tag=f"p7mse{half}")
            st8["p7mse"] = sS.n
        MS1_S = sS.n
        for h in range(H):
            j = h // 2
            rowi = 32 + 64 * (h % 2)
            src_r = (h % 8) * 16
            src_c = (h // 8) * 128
            P.s(lambda e, j=j, rowi=rowi, src_r=src_r, src_c=src_c:
                e.dma_start(
                    qtp[j].ap(rowi, 1, [[128, 16], [1, 128]]),
                    mstg1[src_r:src_r + 16, src_c:src_c + 128]),
                wait=[(sS, MS1_S)], inc16=[dV], tag=f"p7md{h}")
        MF1_DV = dV.n
        phslot1 = {}
        slot_use1 = {}
        seq1 = [0]
        NPH1 = 16
        phat1 = [View(attn_t, 8192 + i * 512) for i in range(NPH1)]
        for h in range(H):
            j, rb = h // 2, 64 * (h % 2)
            for tb in range(4):
                nsi = 4 * tb + 4
                for si in range(nsi):
                    k2 = 2 + (si % 2)
                    pb = bank(k2)
                    P.t(lambda e, j=j, rb=rb, si=si, tb=tb, pb=pb: e.matmul(
                            pb,
                            ktp[j].ap(rb, 33, [[1, 128]], c0=si * 128),
                            qtp[j].ap(rb, 33, [[1, 512]], c0=tb * 512),
                            start=True, stop=True),
                        wait=[(dV, MF1_DV),
                              (sS, st8.get(f"p7pe{k2}", 0))],
                        inc=[sT], tag=f"p7s{h}{tb}{si}")
                    tv = sT.n
                    wvx = []
                    if si >= 4 * tb:
                        kk4 = si - 4 * tb
                        P.v(lambda e, kk4=kk4, pb=pb: e.tensor_tensor(
                                pb, pb, c_cmask4[0:128, ts(kk4, 512)],
                                ALU.add),
                            wait=[(sT, tv)], inc=[sV], tag=f"p7msk{h}")
                        wvx = [(sV, sV.n)]
                    slot = seq1[0] % NPH1
                    seq1[0] += 1
                    ph = phat1[slot]
                    P.s(lambda e, pb=pb, ph=ph: e.activation(
                            ph[0:128, 0:512], pb, AF.Exp, bias=0.0,
                            scale=0.125),
                        wait=[(sT, tv),
                              (sT, slot_use1.get(slot, 0))] + wvx,
                        inc=[sS], tag=f"p7e{h}{tb}{si}")
                    st8[f"p7pe{k2}"] = sS.n
                    phslot1[si] = (slot, sS.n)
                for tix in range(4):
                    ti = 4 * tb + tix
                    kpv = 4 + (tix % 2)
                    pvb = bank(kpv)
                    for si in range(ti + 1):
                        slot, pr = phslot1[si]
                        P.t(lambda e, slot=slot, tix=tix, si=si, ti=ti, h=h,
                            pvb=pvb: e.matmul(
                                pvb[:, 0:VW1],
                                phat1[slot][0:128, ts(tix, 128)],
                                varena.ap(0, 128, [[1, VW1]],
                                          c0=si * 528 + h * VW1),
                                start=(si == 0), stop=(si == ti)),
                            wait=[(sS, pr), (sV, VONES1_V),
                                  (sV, st8.get(f"p7pv{kpv}", 0))],
                            inc=[sT], tag=f"p7pv{h}{ti}{si}")
                        slot_use1[slot] = sT.n
                    tv = sT.n
                    P.v(lambda e, tix=tix, pvb=pvb: e.reciprocal(
                            rl1[0:128, tix:tix + 1], pvb[:, VD1:VD1 + 1]),
                        wait=[(sT, tv)], inc=[sV], tag=f"p7rl{h}")
                    P.v(lambda e, ti=ti, tix=tix, h=h, pvb=pvb:
                        e.tensor_scalar(
                            attn[0:128, ti * 512 + h * 32:
                                 ti * 512 + (h + 1) * 32],
                            pvb[:, 0:VD1], rl1[0:128, tix:tix + 1], None,
                            ALU.mult),
                        inc=[sV], tag=f"p7n{h}{ti}")
                    st8[f"p7pv{kpv}"] = sV.n
        ATT1_V = sV.n
        taps[6] = ("arena_tm", attn, 512)
        p7b = contextlib.ExitStack()
        wo1b = sbv("wo1b", 128, 4 * 1024, BF16, p7b)
        P.g(lambda e: e.dma_start(
                wo1b.ap(0, 128, [[1024, 4], [1, 1024]]),
                wo1e.rearrange("(cc p) c -> p cc c", p=128)),
            inc16=[dG], tag="p7wo")
        WO_G = dG.n
        for tc in range(NTC):
            for cc in range(4):
                i = tc * 4 + cc
                pt = (PT0 if i % 2 == 0 else PT1)[0:128, 0:128]
                P.t(lambda e, tc=tc, cc=cc, pt=pt: e.transpose(
                        pt,
                        attn[0:128, tc * 512 + cc * 128:
                             tc * 512 + (cc + 1) * 128],
                        c_ident[0:128, 0:128]),
                    wait=[(sV, ATT1_V), (sS, st8.get(f"a1te{i - 2}", 0))],
                    inc=[sT], tag=f"a1t{i}")
                tv = sT.n
                P.s(lambda e, tc=tc, cc=cc, pt=pt: e.copy(
                        attn[0:128, 8192 + cc * 2048 + tc * 128:
                             8192 + cc * 2048 + (tc + 1) * 128], pt),
                    wait=[(sT, tv)], inc=[sS], tag=f"a1te{i}")
                st8[f"a1te{i}"] = sS.n
        AT1FM_S = sS.n
        for tc in range(NTC):
            for cb in range(2):
                k2 = (tc * 2 + cb) % 2
                pb = bank(k2)
                for cc in range(4):
                    P.t(lambda e, tc=tc, cb=cb, cc=cc, pb=pb: e.matmul(
                            pb,
                            attn[0:128, 8192 + cc * 2048 + tc * 128:
                                 8192 + cc * 2048 + (tc + 1) * 128],
                            wo1b[0:128, cc * 1024 + cb * 512:
                                 cc * 1024 + (cb + 1) * 512],
                            start=(cc == 0), stop=(cc == 3)),
                        wait=[(sS, AT1FM_S), (dG, WO_G),
                              (sS, st8.get(f"p7oe{k2}", 0))],
                        inc=[sT], tag=f"p7o{tc}{cb}")
                tv = sT.n
                P.s(lambda e, tc=tc, cb=cb, pb=pb: e.copy(
                        sarena[0:128, tc * 1024 + cb * 512:
                               tc * 1024 + (cb + 1) * 512], pb),
                    wait=[(sT, tv)], inc=[sS], tag=f"p7oe{tc}{cb}")
                st8[f"p7oe{k2}"] = sS.n
        S1_S = sS.n
        p7b.close()
        p7S.close()
        taps[7] = ("arena_tm", sarena, 1024)

    if stages >= 9:
        g8 = gate("p8")
        p8S = contextlib.ExitStack()
        xh = [sbv(f"xh{i}", 128, 1024, BF16, p8S) for i in range(2)]
        xin8 = [sbv(f"xin8{i}", 128, 1024, F32, p8S) for i in range(2)]
        xsq8 = sbv("xsq8", 128, 1024, F32, p8S)
        xht = sbv("xht", 128, 2048, BF16, p8S)
        ystg = sbv("ystg", 32, 1024, BF16, p8S)
        ebrow = sbv("ebrow", 1, 1024, BF16, p8S)
        xh_fin = {}
        for i in range(2):
            P.g(lambda e, i=i: e.dma_start(xin8[i][0:128, 0:1024],
                                           ccxrs[ts(i, 128), :]),
                wait=[(sC, RS_C), g8], inc16=[dG], tag=f"p8x{i}")
            wv = dG.n
            P.v(lambda e, i=i: e.tensor_scalar(
                    xin8[i][0:128, 0:1024], xin8[i][0:128, 0:1024],
                    65000.0, -33000.0, ALU.min, ALU.max),
                wait=[(dG, wv)], inc=[sV], tag=f"p8c{i}")
            vv = sV.n
            xh_fin[i] = emit_ln(xin8[i][0:128, 0:1024], xh[i][0:128, 0:1024],
                                D, 1e-5, slot=i, xq=xsq8[0:128, 0:1024],
                                wait_v=[(sV, vv)], wait_s=[(sV, vv)],
                                tag=f"p8ln{i}", final_func=AF.Tanh)
        for i2 in range(2):
            for nl in range(4):
                n = i2 * 4 + nl
                for dc in range(NDC):
                    i3 = n * NDC + dc
                    pt = (PT0 if i3 % 2 == 0 else PT1)[0:128, 0:32]
                    P.t(lambda e, i2=i2, nl=nl, dc=dc, pt=pt: e.transpose(
                            pt,
                            xh[i2][nl * 32:(nl + 1) * 32,
                                   dc * 128:(dc + 1) * 128],
                            c_ident[nl * 32:(nl + 1) * 32,
                                    nl * 32:(nl + 1) * 32],
                            tile_position=(nl * 32, 0)),
                        wait=[(sS, xh_fin[i2]),
                              (sS, st8.get(f"p8t{i3 - 2}", 0))],
                        inc=[sT], tag=f"p8t{i3}")
                    tv = sT.n
                    P.s(lambda e, n=n, dc=dc, pt=pt: e.copy(
                            xht[0:128, dc * 256 + n * 32:
                                dc * 256 + (n + 1) * 32], pt),
                        wait=[(sT, tv)], inc=[sS], tag=f"p8te{i3}")
                    st8[f"p8t{i3}"] = sS.n
        XHT_S = sS.n
        for n in range(EMINE):
            nb = n % 2
            if n >= 2:
                P.g(lambda e, n=n, nb=nb: e.dma_start(
                        expwb[nb].ap(0, 128, [[1024, 8], [1, 1024]]),
                        expw[n].rearrange("(dc p) c -> p dc c", p=128)),
                    wait=[(sT, st8.get(f"p8mw{nb}", 0))],
                    inc16=[dG], tag=f"p8w{n}")
            EXWn_G = dG.n
            P.g(lambda e, n=n: e.dma_start(
                    ebrow[0:1, 0:1024],
                    bass.AP(expb.tensor, n * 1024, [[1024, 1], [1, 1024]])),
                wait=[(sS, st8.get("p8ye", 0)), g8], inc16=[dG],
                tag=f"p8eb{n}")
            EB_W = (dG, dG.n)
            for Hb in range(2):
                k2 = Hb
                pb = bank(k2)
                for dc in range(NDC):
                    P.t(lambda e, n=n, nb=nb, Hb=Hb, dc=dc, pb=pb: e.matmul(
                            pb[0:32, :],
                            xht[0:128, dc * 256 + n * 32:
                                dc * 256 + (n + 1) * 32],
                            expwb[nb][0:128, dc * 1024 + Hb * 512:
                                      dc * 1024 + (Hb + 1) * 512],
                            start=(dc == 0), stop=False),
                        wait=[(sS, XHT_S), (dG, EXWn_G),
                              (sS, st8.get("p8ye", 0))],
                        inc=[sT], tag=f"p8m{n}{Hb}")
                P.t(lambda e, n=n, Hb=Hb, pb=pb: e.matmul(
                        pb[0:32, :], c_ones[0:1, 0:32],
                        ebrow[0:1, Hb * 512:(Hb + 1) * 512],
                        start=False, stop=True),
                    wait=[EB_W], inc=[sT], tag=f"p8mb{n}{Hb}")
                tv = sT.n
                P.s(lambda e, Hb=Hb, pb=pb: e.copy(
                        ystg[0:32, ts(Hb, 512)], pb[0:32, :]),
                    wait=[(sT, tv)], inc=[sS], tag=f"p8ye{n}{Hb}")
            st8["p8ye"] = sS.n
            st8[f"p8mw{nb}"] = sT.n
            sv = sS.n
            P.g(lambda e, n=n: e.dma_start(ccyt[n * 32:(n + 1) * 32, :],
                                           ystg[0:32, 0:1024]),
                wait=[(sS, sv)], inc16=[dG], tag=f"p8yd{n}")
        YT_G = dG.n
        P.g(lambda e: e.collective_compute(
                "AllGather", ALU.bypass,
                replica_groups=[list(range(N_CORES))],
                ins=[ccyt], outs=[ccytag]),
            wait=[(dG, YT_G)], inc=[sC], tag="AG")
        AG_C = sC.n
        p8S.close()

    if stages >= FULL:
        g9 = gate("p9")
        p9S = contextlib.ExitStack()
        ctbuf = [sbv(f"ctbuf{i}", 128, 2048, BF16, p9S) for i in range(2)]
        g9t = sbv("g9t", 128, 1024, BF16, p9S)
        t9t = sbv("t9t", 128, 1024, BF16, p9S)
        x9 = [sbv(f"x9{i}", 128, 1024, F32, p9S) for i in range(2)]
        xo9 = [sbv(f"xo9{i}", 128, 1024, F32, p9S) for i in range(2)]
        yt = [View(attn_t, ec * 1024) for ec in range(16)]
        OUT_W = [0]
        for ec in range(16):
            P.y(lambda e, ec=ec: e.dma_start(yt[ec][0:128, 0:1024],
                                             ccytag[ts(ec, 128), :]),
                wait=[(sC, AG_C), g9], inc16=[dW], tag=f"p9y{ec}")
        YT_W = dW.n
        for tc in range(NTC):
            cb2 = tc % 2
            P.y(lambda e, tc=tc, cb2=cb2: e.dma_start(
                    ctbuf[cb2].ap(0, 128, [[128, 16], [1, 128]]),
                    bass.AP(ctsp.tensor, tc * 128,
                            [[2048, 128], [128 * 2048, 16], [1, 128]])),
                wait=[(sT, st8.get(f"p9mm{cb2}", 0)), (dG, CT_G)],
                inc16=[dW], tag=f"p9ct{tc}")
            CTR_W = dW.n
            P.y(lambda e, tc=tc, cb2=cb2: e.dma_start(
                    x9[cb2][0:128, 0:1024], x1sp[ts(tc, 128), :]),
                wait=[(sV, st8.get(f"p9xv{cb2}", 0))],
                inc16=[dW], tag=f"p9xr{tc}")
            XR9_W = dW.n
            for ec in range(16):
                for half in range(2):
                    bk = 2 * cb2 + half
                    P.t(lambda e, tc=tc, ec=ec, half=half, bk=bk, cb2=cb2:
                        e.matmul(
                            bank(bk),
                            ctbuf[cb2][0:128, ts(ec, 128)],
                            yt[ec][0:128, ts(half, 512)],
                            start=(ec == 0), stop=(ec == 15)),
                        wait=[(dW, CTR_W), (dW, YT_W),
                              (sS, st8.get(f"p9ge{cb2}", 0))],
                        inc=[sT], tag=f"p9m{tc}{ec}{half}")
            st8[f"p9mm{cb2}"] = sT.n
            tv = sT.n
            P.s(lambda e, cb2=cb2: e.activation(
                    g9t[0:128, 0:1024],
                    PS[:, cb2 * 1024:(cb2 + 1) * 1024], AF.Silu),
                wait=[(sT, tv), (sV, st8.get("p9gv", 0))], inc=[sS],
                tag=f"p9ge{tc}")
            st8[f"p9ge{cb2}"] = sS.n
            sv = sS.n
            P.v(lambda e, tc=tc: e.tensor_tensor(
                    t9t[0:128, 0:1024], g9t[0:128, 0:1024],
                    sarena[0:128, ts(tc, 1024)], ALU.mult),
                wait=[(sS, sv), (sS, S1_S)], inc=[sV], tag=f"p9t1{tc}")
            vv = sV.n
            st8["p9gv"] = vv
            P.v(lambda e, cb2=cb2: e.tensor_tensor(
                    xo9[cb2][0:128, 0:1024], x9[cb2][0:128, 0:1024],
                    t9t[0:128, 0:1024], ALU.add),
                wait=[(sV, vv), (dW, XR9_W),
                      (dW, st8.get(f"p9ow{cb2}", 0))],
                inc=[sV], tag=f"p9x{tc}")
            xv = sV.n
            st8[f"p9xv{cb2}"] = xv
            P.y(lambda e, tc=tc, cb2=cb2: e.dma_start(
                    out_hbm[ts(tc, 128), :], xo9[cb2][0:128, 0:1024]),
                wait=[(sV, xv)], inc16=[dW], tag=f"p9o{tc}")
            st8[f"p9ow{cb2}"] = dW.n
        OUT_FIN = dW.n
        p9S.close()
        P.g(lambda e: e.nop(), wait=[(dW, OUT_FIN), (dG, dG.n), (dV, dV.n)],
            inc=[sG], tag="final")
        P.y(lambda e: e.nop(), wait=[(dW, OUT_FIN)], tag="final_sync")

    if stages < FULL:
        gt_ = gate("tap")
        tpS = contextlib.ExitStack()
        ttmp = sbv("ttmp", 128, 1024, F32, tpS)
        tap = taps.get(tapsel if tapsel is not None else stages)
        prevw = [0]

        def tap_tile(src_ap, tc, width):
            P.s(lambda e, src_ap=src_ap: e.copy(ttmp[0:128, 0:width], src_ap),
                wait=[gt_, (dW, prevw[0])], inc=[sS], tag=f"tap{tc}")
            sv = sS.n
            P.y(lambda e, tc=tc, width=width: e.dma_start(
                    out_hbm[ts(tc, 128), 0:width], ttmp[0:128, 0:width]),
                wait=[(sS, sv)], inc16=[dW], tag=f"tapd{tc}")
            prevw[0] = dW.n

        if tap is None:
            pass
        elif tap[0] == "arena_tm":
            arena, width = tap[1], tap[2]
            for tc in range(NTC):
                tap_tile(arena[0:128, tc * width:(tc + 1) * width], tc, width)
        elif tap[0] == "x1sp":
            for tc in range(NTC):
                P.y(lambda e, tc=tc: e.dma_start(
                        ttmp[0:128, 0:1024], x1sp[ts(tc, 128), :]),
                    wait=[gt_, (dW, prevw[0])], inc16=[dW], tag=f"tapl{tc}")
                wv = dW.n
                P.y(lambda e, tc=tc: e.dma_start(
                        out_hbm[ts(tc, 128), :], ttmp[0:128, 0:1024]),
                    wait=[(dW, wv)], inc16=[dW], tag=f"tapd{tc}")
                prevw[0] = dW.n
        elif tap[0] == "small":
            tsr, pp, ff = tap[1], tap[2], tap[3]
            P.s(lambda e: e.copy(ttmp[0:pp, 0:ff], tsr[0:pp, 0:ff]),
                wait=[gt_], inc=[sS], tag="taps")
            sv = sS.n
            P.y(lambda e: e.dma_start(out_hbm[0:pp, 0:ff], ttmp[0:pp, 0:ff]),
                wait=[(sS, sv)], inc16=[dW], tag="tapsd")
            prevw[0] = dW.n
        P.y(lambda e: e.nop(), wait=[(dW, prevw[0])], tag="tap_fin")
        tpS.close()

    P.finalize()
    return nc


_BUILD_CACHE = {}


def _get_program(stages):
    if stages not in _BUILD_CACHE:
        _BUILD_CACHE[stages] = build_program(stages)
    return _BUILD_CACHE[stages]


def _consts():
    import ml_dtypes
    bf = ml_dtypes.bfloat16
    c = {}
    c["identb"] = np.eye(128, dtype=np.float32).astype(bf)
    c["identnb"] = (-np.eye(128, dtype=np.float32)).astype(bf)
    c["onesrow"] = np.ones((1, T), dtype=np.float32).astype(bf)
    sig = np.arange(128)[:, None]
    tau = np.arange(512)[None, :]
    cm = np.zeros((128, 4 * 512), dtype=np.float32)
    for k in range(4):
        cm[:, k * 512:(k + 1) * 512] = np.where(128 * k + sig > tau, NEG, 0.0)
    c["cmask4"] = cm.astype(bf)
    jj = np.arange(16)[None, :]
    tau2 = np.arange(128)[:, None]
    c["sampmask"] = np.where(8 * jj <= tau2, 0.0, NEG).astype(np.float32)
    t_ = np.arange(128)[:, None]
    s_ = np.arange(128)[None, :]
    c["iotatri"] = np.where(t_ > s_, (t_ - s_).astype(np.float32), 1.0e30
                            ).astype(np.float32)
    c["mask01"] = (t_ > s_).astype(np.float32)
    c["iotarev2"] = np.broadcast_to((127.0 - np.arange(128))[None, :],
                                    (128, 128)).astype(np.float32).copy()
    cc = np.arange(16)[None, :]
    cp = np.arange(16)[:, None]
    c["iotac16"] = np.where(cc > cp, (cc - cp - 1).astype(np.float32), 1.0e30
                            ).astype(np.float32)
    c["mask01_16"] = (cc > cp).astype(np.float32)
    c["iotap1"] = np.broadcast_to((np.arange(128) + 1.0)[:, None],
                                  (128, 16)).astype(np.float32).copy()
    return c


def kernel(**inputs):
    import os
    stages = int(os.environ.get("KSTAGES", str(FULL)))
    nc = _get_program(stages)
    f32 = np.float32
    ctu = np.ascontiguousarray
    qkv_b = inputs["L0_attn_qkv_b"].astype(f32, copy=False)
    base = {
        "qkvw": ctu(inputs["L0_attn_qkv_w"].astype(f32, copy=False)),
        "qkvbT": ctu(np.asarray(qkv_b).reshape(24, 128).T),
        "qkvb_row": ctu(np.asarray(qkv_b).reshape(1, 3 * D)),
        "w1w": ctu(inputs["L0_attn_w1_write_w"].astype(f32, copy=False)),
        "w2w": ctu(inputs["L0_attn_w2_write_w"].astype(f32, copy=False)),
        "r1w": ctu(inputs["L0_attn_w1_read_w"].astype(f32, copy=False)),
        "r2w": ctu(inputs["L0_attn_w2_read_w"].astype(f32, copy=False)),
        "mvw": ctu(inputs["L0_attn_mem_value_w"].astype(f32, copy=False)),
        "mvb_row": ctu(np.asarray(inputs["L0_attn_mem_value_b"]).reshape(1, D)),
        "mgw": ctu(inputs["L0_attn_mem_gate_w"].astype(f32, copy=False)),
        "mgb_row": ctu(np.asarray(inputs["L0_attn_mem_gate_b"]).reshape(1, H)),
        "mscale_row": ctu(np.asarray(inputs["L0_attn_mem_scale"]).reshape(1, H)),
        "outw": ctu(inputs["L0_attn_out_w"].astype(f32, copy=False)),
        "outb_row": ctu(np.asarray(inputs["L0_attn_out_b"]).reshape(1, D)),
        "dlog_row": ctu(np.asarray(inputs["L0_attn_decay_logits"]).reshape(1, H)),
        "Jm": ctu(inputs["L0_attn_J"].astype(f32, copy=False)),
        "wgT": ctu(np.asarray(inputs["L0_ffn_w_gate"]).T.astype(np.int32)),
        "wuT": ctu(np.asarray(inputs["L0_ffn_w_up"]).T.astype(np.int32)),
        "wdT": ctu(np.asarray(inputs["L0_ffn_w_down"]).T.astype(np.int32)),
        "wq1": ctu(inputs["L1_attn_Wq"].astype(f32, copy=False)),
        "wkv1": ctu(inputs["L1_attn_Wkv"].astype(f32, copy=False)),
        "wk1": ctu(inputs["L1_attn_Wk"].astype(f32, copy=False)),
        "wv1": ctu(inputs["L1_attn_Wv"].astype(f32, copy=False)),
        "wo1e": ctu(np.asarray(inputs["L1_attn_Wo"]).reshape(H, DH, D)
                    [:, 0:VD1, :].reshape(H * VD1, D)),
        "qg_row": ctu(np.asarray(inputs["L1_attn_q_g"]).reshape(1, DH)),
        "kg_row": ctu(np.asarray(inputs["L1_attn_k_g"]).reshape(1, ROPE)),
        "phiT": ctu(np.asarray(inputs["L1_ffn_phi_w"]).T),
        "phib_row": ctu(np.asarray(inputs["L1_ffn_phi_b"]).reshape(1, 2048)),
    }
    for k, v in _consts().items():
        base[k] = v
    xfull = np.asarray(inputs["L0_attn_x"], dtype=f32)
    expw_full = np.asarray(inputs["L1_ffn_expert_w"], dtype=f32)
    expb_full = np.asarray(inputs["L1_ffn_expert_b"], dtype=f32)
    in_maps = []
    for cidx in range(N_CORES):
        m = dict(base)
        m["x"] = ctu(xfull[cidx])
        m["expw"] = ctu(expw_full[cidx * EMINE:(cidx + 1) * EMINE])
        m["expb"] = ctu(expb_full[cidx * EMINE:(cidx + 1) * EMINE]
                        .reshape(1, EMINE * D))
        in_maps.append(m)
    res = run_bass_kernel_spmd(nc, in_maps, core_ids=list(range(N_CORES)))
    out = np.stack([np.asarray(res.results[cidx]["out"], dtype=np.float32)
                    for cidx in range(N_CORES)], axis=0)
    return out


# version 06
# speedup vs baseline: 1.2887x; 1.2887x over previous
import contextlib
import numpy as np

import concourse.bass as bass
import concourse.mybir as mybir
from concourse.bass_utils import run_bass_kernel_spmd

F32 = mybir.dt.float32
BF16 = mybir.dt.bfloat16
I32 = mybir.dt.int32
AF = mybir.ActivationFunctionType
ALU = mybir.AluOpType
AX = mybir.AxisListType

N_CORES = 8
T = 2048
D = 1024
NTC = 16
NDC = 8
H = 16
DH = 64
VW = DH + 1
FFH = 3072
NHC = 24
ROPE = 32
VD1 = 32
VW1 = VD1 + 1
LAT = 128
NEXP = 64
PSLOT = 32
EMINE = 8
NEG = -3.0e38
XND = 2052
FULL = 9


def ts(i, w):
    return slice(i * w, (i + 1) * w)


class SemCtr:
    __slots__ = ("h", "n", "name")

    def __init__(self, h, name):
        self.h, self.n, self.name = h, 0, name


class View:
    __slots__ = ("t", "o", "rl")

    def __init__(self, t, o=0):
        self.t, self.o = t, o
        self.rl = int(t.shape[1])

    def __getitem__(self, idx):
        r, c = idx
        r0 = r.start or 0
        r1 = r.stop if r.stop is not None else int(self.t.shape[0])
        c0 = c.start or 0
        c1 = c.stop
        return bass.AP(self.t, r0 * self.rl + self.o + c0,
                       [[self.rl, r1 - r0], [1, c1 - c0]])

    def ap(self, r0, nr, free_dims, c0=0):
        return bass.AP(self.t, r0 * self.rl + self.o + c0,
                       [[self.rl, nr]] + [list(f) for f in free_dims])


class Prog:
    ENGS = ("tensor", "vector", "scalar", "gpsimd", "sync")

    def __init__(self, nc):
        self.nc = nc
        self.q = {e: [] for e in self.ENGS}
        self.es = contextlib.ExitStack()
        self.sems = []

    def sem(self, name):
        s = SemCtr(self.es.enter_context(self.nc.semaphore(name)), name)
        self.sems.append(s)
        return s

    def _emit(self, eng, fn, wait=(), inc=(), inc16=(), tag=""):
        wait = tuple((s, v) for (s, v) in wait if v > 0)
        self.q[eng].append((fn, wait, tuple(inc), tuple(inc16), tag))
        for s in inc:
            s.n += 1
        for s in inc16:
            s.n += 16

    def t(self, fn, **kw):
        self._emit("tensor", fn, **kw)

    def v(self, fn, wait=(), inc=(), novser=False, **kw):
        vs = getattr(self, "vser", None)
        if vs is not None:
            if not novser:
                wait = tuple(wait) + ((vs, vs.n),)
            if vs not in inc:
                inc = tuple(inc) + (vs,)
        self._emit("vector", fn, wait=wait, inc=inc, **kw)

    def s(self, fn, **kw):
        self._emit("scalar", fn, **kw)

    def g(self, fn, **kw):
        self._emit("gpsimd", fn, **kw)

    def y(self, fn, **kw):
        self._emit("sync", fn, **kw)

    def check_schedule(self):
        idx = {e: 0 for e in self.ENGS}
        cnt = {s.name: 0 for s in self.sems}
        progress = True
        while progress:
            progress = False
            for e in self.ENGS:
                while idx[e] < len(self.q[e]):
                    fn, wait, inc, inc16, tag = self.q[e][idx[e]]
                    if any(cnt[s.name] < v for s, v in wait):
                        break
                    for s in inc:
                        cnt[s.name] += 1
                    for s in inc16:
                        cnt[s.name] += 16
                    idx[e] += 1
                    progress = True
        stuck = {e: i for e, i in idx.items() if i < len(self.q[e])}
        if stuck:
            msg = []
            for e, i in stuck.items():
                fn, wait, inc, inc16, tag = self.q[e][i]
                miss = [(s.name, v, cnt[s.name]) for s, v in wait
                        if cnt[s.name] < v]
                msg.append(f"  {e}[{i}/{len(self.q[e])}] tag={tag} miss={miss}")
            raise RuntimeError("schedule deadlock:\n" + "\n".join(msg))

    def finalize(self):
        self.check_schedule()
        with self.nc.Block() as blk:
            for eng in self.ENGS:
                ops = self.q[eng]

                def make(ops=ops):
                    def run(e):
                        hi = {}
                        for fn, waits, incs, incs16, tag in ops:
                            for s, v in waits:
                                if hi.get(s.name, 0) < v:
                                    e.wait_ge(s.h, v)
                                    hi[s.name] = v
                            inst = fn(e)
                            for s in incs:
                                inst = inst.then_inc(s.h, 1)
                            for s in incs16:
                                inst = inst.then_inc(s.h, 16)
                    return run

                getattr(blk, eng)(make())


def build_program(stages=FULL):
    dbg92 = (stages == 92)
    if dbg92:
        stages = 1
    tapsel = None
    TAPLVL = {1: 1, 2: 2, 21: 3, 22: 3, 3: 4, 4: 5, 5: 6, 6: 8, 7: 8}
    if stages in TAPLVL:
        tapsel, stages = stages, TAPLVL[stages]
    nc = bass.Bass("TRN2", target_bir_lowering=False)
    P = Prog(nc)
    es = P.es

    def din(name, shape, dtype=F32):
        return nc.dram_tensor(name, list(shape), dtype, kind="ExternalInput").ap()

    x_hbm = din("x", (T, D))
    qkvw = din("qkvw", (D, 3 * D))
    qkvbT = din("qkvbT", (128, 24))
    qkvb_row = din("qkvb_row", (1, 3 * D))
    w1w = din("w1w", (2 * D, DH))
    w2w = din("w2w", (D, DH))
    r1w = din("r1w", (D, DH))
    r2w = din("r2w", (D, DH))
    mvw = din("mvw", (D, D))
    mvb_row = din("mvb_row", (1, D))
    mgw = din("mgw", (D, H))
    mgb_row = din("mgb_row", (1, H))
    mscale_row = din("mscale_row", (1, H))
    outw = din("outw", (D, D))
    outb_row = din("outb_row", (1, D))
    dlog_row = din("dlog_row", (1, H))
    Jm = din("Jm", (6, 6))
    wgT = din("wgT", (D, FFH), I32)
    wuT = din("wuT", (D, FFH), I32)
    wdT = din("wdT", (FFH, D), I32)
    wq1 = din("wq1", (D, D))
    wkv1 = din("wkv1", (D, LAT))
    wk1 = din("wk1", (LAT, H * ROPE))
    wv1 = din("wv1", (LAT, H * VD1))
    wo1e = din("wo1e", (H * VD1, D))
    qg_row = din("qg_row", (1, DH))
    kg_row = din("kg_row", (1, ROPE))
    phiT = din("phiT", (D, NEXP * PSLOT))
    phib_row = din("phib_row", (1, NEXP * PSLOT))
    expw = din("expw", (EMINE, D, D))
    expb = din("expb", (1, EMINE * D))
    identb = din("identb", (128, 128), BF16)
    identnb = din("identnb", (128, 128), BF16)
    onesrow = din("onesrow", (1, T), BF16)
    cmask4 = din("cmask4", (128, 4 * 512), BF16)
    sampmask = din("sampmask", (128, 16))
    iotatri = din("iotatri", (128, 128))
    mask01 = din("mask01", (128, 128))
    iotarev2 = din("iotarev2", (128, 128))
    iotac16 = din("iotac16", (16, 16))
    mask01_16 = din("mask01_16", (16, 16))
    iotap1 = din("iotap1", (128, 16))

    out_hbm = nc.dram_tensor("out", [T, D], F32, kind="ExternalOutput").ap()

    x1sp = nc.dram_tensor("x1sp", [T, D], F32, kind="Internal").ap()
    hvsp = nc.dram_tensor("hvsp", [NHC, 128, T], BF16, kind="Internal").ap()
    ctsp = nc.dram_tensor("ctsp", [2048, 2048], BF16, kind="Internal").ap()
    r2sp = nc.dram_tensor("r2sp", [16, 128, T], BF16, kind="Internal").ap()
    ccx = nc.dram_tensor("ccx", [2048, 1024], F32, kind="Internal").ap()
    ccxrs = nc.dram_tensor("ccxrs", [256, 1024], F32, kind="Internal").ap()
    ccyt = nc.dram_tensor("ccyt", [256, 1024], BF16, kind="Internal").ap()
    ccytag = nc.dram_tensor("ccytag", [2048, 1024], BF16, kind="Internal",
                            addr_space="Shared").ap()
    bounce = nc.dram_tensor("bounce", [8, 2048], F32, kind="Internal").ap()

    dW = P.sem("dW")
    dG = P.sem("dG")
    dS = P.sem("dS")
    dV = P.sem("dV")
    sT = P.sem("sT")
    sV = P.sem("sV")
    sS = P.sem("sS")
    sG = P.sem("sG")
    sC = P.sem("sC")
    P.vser = sV

    def sbt(name, p, f, dt=F32, stack=es):
        return stack.enter_context(nc.sbuf_tensor(name, [p, f], dt))

    def sbv(name, p, f, dt=F32, stack=es):
        return View(sbt(name, p, f, dt, stack))

    xnf_t = sbt("xnf", 128, NDC * XND, BF16)
    xnt_t = sbt("xnt", 128, NTC * 1024, BF16)
    attn_t = sbt("attn", 128, NTC * 1024, BF16)
    sarena_t = sbt("sarena", 128, NTC * 1024, BF16)
    varena_t = sbt("varena", 128, NTC * H * VW, BF16)
    xnf = View(xnf_t)
    xnt = View(xnt_t)
    attn = View(attn_t)
    sarena = View(sarena_t)
    varena = View(varena_t)
    c_ident = sbv("c_ident", 128, 128, BF16)
    c_identn = sbv("c_identn", 128, 128, BF16)
    c_ones = sbv("c_ones", 1, T, BF16)
    c_cmask4 = sbv("c_cmask4", 128, 4 * 512, BF16)
    c_sampm = sbv("c_sampm", 128, 16, F32)
    qkvbT_sb = sbv("qkvbT_sb", 128, 24, F32)
    lnsc = sbv("lnsc", 128, 48, F32)
    lndbg = sbv("lndbg", 128, 128, F32) if dbg92 else None
    rows_tmp = sbv("rows_tmp", 1, 1024, F32)

    PS = es.enter_context(nc.psum_tensor("PS", [128, 6 * 512], F32))
    PT0_t = es.enter_context(nc.psum_tensor("PT0", [128, 1024], BF16))
    PT1_t = es.enter_context(nc.psum_tensor("PT1", [128, 1024], BF16))
    PT0 = View(PT0_t)
    PT1 = View(PT1_t)

    def bank(i, w=512, off=0):
        return PS[:, i * 512 + off: i * 512 + off + w]

    st8 = {}
    taps = {}

    def gate(tag):
        P.g(lambda e: e.nop(),
            wait=[(sT, sT.n), (sV, sV.n), (sS, sS.n), (dV, dV.n),
                  (dW, dW.n), (dG, dG.n), (dS, dS.n)],
            inc=[sG], tag="gate_" + tag)
        return (sG, sG.n)

    def row2bf(src_dram, dst_ap_fn, w, tagn):
        for c0 in range(0, w, 1024):
            cw = min(1024, w - c0)
            P.y(lambda e, c0=c0, cw=cw: e.dma_start(
                    rows_tmp[0:1, 0:cw],
                    bass.AP(src_dram.tensor, c0, [[w, 1], [1, cw]])),
                wait=[(sV, sV.n)], inc16=[dW], tag=tagn)
            wv = dW.n
            P.v(lambda e, c0=c0, cw=cw: e.tensor_copy(
                    dst_ap_fn(c0, cw), rows_tmp[0:1, 0:cw]),
                wait=[(dW, wv)], inc=[sV], tag=tagn)
        return sV.n

    P.y(lambda e: e.dma_start(c_ident[0:128, 0:128], identb), inc16=[dW])
    P.y(lambda e: e.dma_start(c_identn[0:128, 0:128], identnb), inc16=[dW])
    P.y(lambda e: e.dma_start(c_ones[0:1, 0:T], onesrow), inc16=[dW])
    P.y(lambda e: e.dma_start(c_cmask4[0:128, 0:2048], cmask4), inc16=[dW])
    P.y(lambda e: e.dma_start(c_sampm[0:128, 0:16], sampmask), inc16=[dW])
    P.y(lambda e: e.dma_start(qkvbT_sb[0:128, 0:24], qkvbT), inc16=[dW])
    CONST_W = dW.n

    def emit_ln(src_ap, dst_ap, n, eps, slot, xq, wait_v=(), wait_s=(),
                tag="ln", final_func=None, clip=None):
        invn = 1.0 / n
        c = 8 * (slot % 4)
        P.v(lambda e: e.tensor_reduce(lnsc[0:128, c:c + 1], src_ap, AX.X,
                                      ALU.add, negate=True),
            wait=tuple(wait_v) + ((sS, st8.get(f"lnfin{slot % 4}", 0)),),
            inc=[sV], novser=True, tag=tag)
        P.v(lambda e: e.tensor_scalar(lnsc[0:128, c + 1:c + 2],
                                      lnsc[0:128, c:c + 1],
                                      invn, None, ALU.mult), inc=[sV], tag=tag)
        P.s(lambda e: e.activation(xq, src_ap, AF.Square),
            wait=tuple(wait_s) + ((sV, st8.get("lnsq", 0)),),
            inc=[sS], tag=tag + "sq")
        sq_s = sS.n
        P.v(lambda e: e.tensor_reduce(lnsc[0:128, c + 2:c + 3], xq, AX.X,
                                      ALU.add),
            wait=[(sS, sq_s)], inc=[sV], novser=True, tag=tag)
        st8["lnsq"] = sV.n
        P.v(lambda e: e.tensor_tensor(lnsc[0:128, c + 3:c + 4],
                                      lnsc[0:128, c + 1:c + 2],
                                      lnsc[0:128, c + 1:c + 2], ALU.mult),
            inc=[sV], tag=tag)
        P.v(lambda e: e.tensor_scalar(lnsc[0:128, c + 4:c + 5],
                                      lnsc[0:128, c + 2:c + 3],
                                      invn, None, ALU.mult), inc=[sV], tag=tag)
        P.v(lambda e: e.tensor_tensor(lnsc[0:128, c + 4:c + 5],
                                      lnsc[0:128, c + 4:c + 5],
                                      lnsc[0:128, c + 3:c + 4], ALU.subtract),
            inc=[sV], tag=tag)
        P.v(lambda e: e.tensor_scalar(lnsc[0:128, c + 4:c + 5],
                                      lnsc[0:128, c + 4:c + 5],
                                      eps, None, ALU.add), inc=[sV], tag=tag)
        vv = sV.n
        P.s(lambda e: e.activation(lnsc[0:128, c + 5:c + 6],
                                   lnsc[0:128, c + 4:c + 5],
                                   AF.Sqrt),
            wait=[(sV, vv)], inc=[sS], tag=tag)
        sv2 = sS.n
        P.v(lambda e: e.reciprocal(lnsc[0:128, c + 6:c + 7],
                                   lnsc[0:128, c + 5:c + 6]),
            wait=[(sS, sv2)], inc=[sV], tag=tag)
        P.v(lambda e: e.tensor_tensor(lnsc[0:128, c + 7:c + 8],
                                      lnsc[0:128, c + 1:c + 2],
                                      lnsc[0:128, c + 6:c + 7], ALU.mult),
            inc=[sV], tag=tag)
        vv2 = sV.n
        ff = final_func if final_func is not None else AF.Identity
        P.s(lambda e: e.activation(dst_ap, src_ap, ff,
                                   bias=lnsc[0:128, c + 7:c + 8],
                                   scale=lnsc[0:128, c + 6:c + 7]),
            wait=[(sV, vv2)], inc=[sS], tag=tag + "n")
        st8[f"lnfin{slot % 4}"] = sS.n
        return sS.n

    scopeL0 = contextlib.ExitStack()
    rowq = sbv("rowq", 1, 1024, BF16, scopeL0)
    rowm = sbv("rowm", 1, 2 * D, BF16, scopeL0)
    rowg = sbv("rowg", 1, 32, BF16, scopeL0)
    msc = sbv("msc", 128, 256, F32, scopeL0)
    gated = sbv("gated", 128, 16, F32, scopeL0)
    mscale_bc = sbv("mscale_bc", 128, 16, F32, scopeL0)
    toe = sbv("toe", 128, 16 * 128, BF16, scopeL0)
    dpow = sbv("dpow", 128, 16, F32, scopeL0)
    ddbc = sbv("ddbc", 128, 16 * 128, BF16, scopeL0)
    Th = sbv("Th", 16, 16 * 16, BF16, scopeL0)
    mjj = sbv("mjj", 128, 36, BF16, scopeL0)

    P.y(lambda e: e.dma_start(rows_tmp[0:1, 0:1024],
                              bass.AP(qkvb_row.tensor, 2 * D,
                                      [[3 * D, 1], [1, 1024]])),
        inc16=[dW], tag="rq")
    _wv = dW.n
    P.v(lambda e: e.tensor_copy(rowq[0:1, 0:1024], rows_tmp[0:1, 0:1024]),
        wait=[(dW, _wv)], inc=[sV], tag="rq")
    row2bf(mvb_row, lambda c0, cw: rowm[0:1, c0:c0 + cw], D, "rm")
    row2bf(outb_row, lambda c0, cw: rowm[0:1, D + c0:D + c0 + cw], D, "ro")
    row2bf(mgb_row, lambda c0, cw: rowg[0:1, c0:c0 + cw], H, "rg")
    ROWS_V = sV.n
    P.s(lambda e: e.nop(), wait=[(sV, ROWS_V)], inc=[sS], tag="rowsbar")
    ROWS_S = sS.n

    P.y(lambda e: e.dma_start(rows_tmp[0:1, 0:H], mscale_row),
        wait=[(sV, ROWS_V)], inc16=[dW])
    wv = dW.n
    P.y(lambda e: e.dma_start(bounce[0:1, 0:H], rows_tmp[0:1, 0:H]),
        wait=[(dW, wv)], inc16=[dW])
    wv = dW.n
    P.y(lambda e: e.dma_start(mscale_bc[0:128, 0:H],
                              bass.AP(bounce.tensor, 0, [[0, 128], [1, H]])),
        wait=[(dW, wv)], inc16=[dW], tag="mscbc")
    MSCALE_W = dW.n

    setupS = contextlib.ExitStack()
    iotatri_sb = sbv("iotatri_sb", 128, 128, F32, setupS)
    mask01_sb = sbv("mask01_sb", 128, 128, F32, setupS)
    iotarev2_sb = sbv("iotarev2_sb", 128, 128, F32, setupS)
    iotac16_sb = sbv("iotac16_sb", 16, 16, F32, setupS)
    mask0116_sb = sbv("mask0116_sb", 16, 16, F32, setupS)
    iotap1_sb = sbv("iotap1_sb", 128, 16, F32, setupS)
    lndrow = sbv("lndrow", 1, 16, F32, setupS)
    lndbc = sbv("lndbc", 128, 16, F32, setupS)
    lndCbc = sbv("lndCbc", 128, 16, F32, setupS)
    Thf = sbv("Thf", 16, 16, F32, setupS)
    toef = sbv("toef", 128, 128, F32, setupS)
    mjj_a = sbv("mjj_a", 128, 6, F32, setupS)
    mjj_b = sbv("mjj_b", 128, 6, F32, setupS)

    for src, dst, pp, ff in ((iotatri, iotatri_sb, 128, 128),
                             (mask01, mask01_sb, 128, 128),
                             (iotarev2, iotarev2_sb, 128, 128),
                             (iotac16, iotac16_sb, 16, 16),
                             (mask01_16, mask0116_sb, 16, 16),
                             (iotap1, iotap1_sb, 128, 16)):
        P.y(lambda e, src=src, dst=dst, pp=pp, ff=ff: e.dma_start(
            dst[0:pp, 0:ff], src), inc16=[dW], tag="dk0")
    DKC_W = dW.n
    P.y(lambda e: e.dma_start(rows_tmp[0:1, 0:H], dlog_row),
        wait=[(dW, MSCALE_W)], inc16=[dW])
    wv = dW.n
    P.s(lambda e: e.activation(lndrow[0:1, 0:H], rows_tmp[0:1, 0:H],
                               AF.Sigmoid),
        wait=[(dW, wv)], inc=[sS], tag="dk1")
    P.s(lambda e: e.activation(lndrow[0:1, 0:H], lndrow[0:1, 0:H], AF.Ln),
        inc=[sS], tag="dk2")
    sv = sS.n
    P.y(lambda e: e.dma_start(bounce[1:2, 0:H], lndrow[0:1, 0:H]),
        wait=[(sS, sv)], inc16=[dW])
    wv = dW.n
    P.y(lambda e: e.dma_start(lndbc[0:128, 0:H],
                              bass.AP(bounce.tensor, 2048, [[0, 128], [1, H]])),
        wait=[(dW, wv)], inc16=[dW], tag="dk3")
    LND_W = dW.n
    P.v(lambda e: e.tensor_scalar(lndCbc[0:128, 0:H], lndbc[0:128, 0:H],
                                  128.0, None, ALU.mult),
        wait=[(dW, LND_W)], inc=[sV], tag="dk4")
    LND_V = sV.n
    prev_g = 0
    for h in range(H):
        P.s(lambda e, h=h: e.activation(toef[0:128, 0:128],
                                        iotatri_sb[0:128, 0:128],
                                        AF.Exp, bias=0.0,
                                        scale=lndbc[0:128, h:h + 1]),
            wait=[(dW, DKC_W), (dW, LND_W), (sV, prev_g)], inc=[sS],
            tag=f"toe{h}")
        sv = sS.n
        P.v(lambda e, h=h: e.tensor_tensor(toe[0:128, ts(h, 128)],
                                           toef[0:128, 0:128],
                                           mask01_sb[0:128, 0:128], ALU.mult),
            wait=[(sS, sv)], inc=[sV], tag=f"toem{h}")
        P.s(lambda e, h=h: e.activation(ddbc[0:128, ts(h, 128)],
                                        iotarev2_sb[0:128, 0:128],
                                        AF.Exp, bias=0.0,
                                        scale=lndbc[0:128, h:h + 1]),
            inc=[sS], tag=f"dd{h}")
        P.s(lambda e, h=h: e.activation(Thf[0:16, 0:16],
                                        iotac16_sb[0:16, 0:16], AF.Exp,
                                        bias=0.0,
                                        scale=lndCbc[0:16, h:h + 1]),
            wait=[(sV, LND_V), (sV, prev_g)], inc=[sS], tag=f"th{h}")
        sv = sS.n
        P.v(lambda e, h=h: e.tensor_tensor(Th[0:16, ts(h, 16)],
                                           Thf[0:16, 0:16],
                                           mask0116_sb[0:16, 0:16], ALU.mult),
            wait=[(sS, sv)], inc=[sV], tag=f"thm{h}")
        prev_g = sV.n
    TOE_V = sV.n
    DD_S = sS.n
    P.v(lambda e: e.tensor_tensor(dpow[0:128, 0:16], iotap1_sb[0:128, 0:16],
                                  lndbc[0:128, 0:16], ALU.mult),
        wait=[(dW, DKC_W), (dW, LND_W)], inc=[sV], tag="dpow")
    sv = sV.n
    P.s(lambda e: e.activation(dpow[0:128, 0:16], dpow[0:128, 0:16], AF.Exp),
        wait=[(sV, sv)], inc=[sS], tag="dpow2")
    DPOW_S = sS.n
    P.y(lambda e: e.dma_start(rows_tmp[0:1, 0:36],
                              bass.AP(Jm.tensor, 0, [[36, 1], [1, 36]])),
        wait=[(dW, LND_W)], inc16=[dW], tag="mjj0")
    wv = dW.n
    P.y(lambda e: e.dma_start(bounce[2:3, 0:36], rows_tmp[0:1, 0:36]),
        wait=[(dW, wv)], inc16=[dW])
    wv = dW.n
    for i1 in range(6):
        P.g(lambda e, i1=i1: e.dma_start(
            mjj_a[6 * i1:6 * i1 + 6, 0:6],
            bass.AP(bounce.tensor, 2 * 2048 + 6 * i1, [[0, 6], [1, 6]])),
            wait=[(dW, wv)], inc16=[dG], tag="mjja")
    P.g(lambda e: e.dma_start(
        mjj_b[0:36, 0:6],
        bass.AP(bounce.tensor, 2 * 2048, [[0, 6], [6, 6], [1, 6]])),
        wait=[(dW, wv)], inc16=[dG], tag="mjjb")
    wvg = dG.n
    P.v(lambda e: e.tensor_tensor(
            mjj[0:36, 0:36],
            mjj_a.ap(0, 36, [[1, 6], [0, 6]]),
            mjj_b.ap(0, 36, [[0, 6], [1, 6]]),
            ALU.mult),
        wait=[(dG, wvg)], inc=[sV], tag="mjjm")
    MJJ_V = sV.n
    setupS.close()

    p0S = contextlib.ExitStack()
    _xin0 = sbv("xin0", 128, 1024, F32, p0S)
    xin = [_xin0, _xin0]
    xsq = sbv("xsq", 128, 1024, F32, p0S)

    ln0fin = {}
    SETUP_DONE = ((sV, TOE_V), (sS, DPOW_S), (sV, MJJ_V), (sS, DD_S))
    for tc in range(NTC):
        b = tc % 2
        P.y(lambda e, tc=tc, b=b: e.dma_start(xin[b][0:128, 0:1024],
                                              x_hbm[ts(tc, 128), :]),
            wait=[(sS, ln0fin.get(tc - 1, 0))] + list(SETUP_DONE),
            inc16=[dW], tag=f"p0x{tc}")
        wv = dW.n
        ln0fin[tc] = emit_ln(xin[b][0:128, 0:1024], xnt[0:128, ts(tc, 1024)],
                             D, 1e-5, slot=tc, xq=xsq[0:128, 0:1024],
                             wait_v=[(dW, wv)], wait_s=[(dW, wv)],
                             tag=f"ln0_{tc}")
        if dbg92:
            c92 = 8 * (tc % 4)
            P.s(lambda e, tc=tc, c92=c92: e.copy(
                    lndbg[0:128, tc * 8:tc * 8 + 8], lnsc[0:128, c92:c92 + 8]),
                inc=[sS], tag=f"lndbg{tc}")
            st8[f"lnfin{tc % 4}"] = sS.n
    LN0_S = sS.n

    P.v(lambda e: e.memset(
        xnf.ap(0, 128, [[XND, NDC], [1, 4]]), 0.0), inc=[sV], tag="p0pad")
    PAD_V = sV.n

    def transposes_to_fm(fin_of_tc, tag):
        for tc in range(NTC):
            for dc in range(NDC):
                i = tc * NDC + dc
                pt = (PT0 if i % 2 == 0 else PT1)[0:128, 0:128]
                P.t(lambda e, tc=tc, dc=dc, pt=pt: e.transpose(
                        pt,
                        xnt[0:128, tc * 1024 + dc * 128:
                            tc * 1024 + (dc + 1) * 128],
                        c_ident[0:128, 0:128]),
                    wait=[(sS, fin_of_tc[tc]), (dW, CONST_W),
                          (sS, st8.get(f"{tag}e{i - 2}", 0))],
                    inc=[sT], tag=f"{tag}t{i}")
                tv = sT.n
                P.s(lambda e, tc=tc, dc=dc, pt=pt: e.copy(
                        xnf[0:128, dc * XND + 4 + tc * 128:
                            dc * XND + 4 + (tc + 1) * 128], pt),
                    wait=[(sT, tv)], inc=[sS], tag=f"{tag}e{i}")
                st8[f"{tag}e{i}"] = sS.n
        return sS.n

    XNF_S = transposes_to_fm(ln0fin, "p0f")
    p0S.close()
    taps[1] = ("arena_tm", xnt, 1024)
    if dbg92:
        taps[1] = ("small", lndbg, 128, 128)

    if stages >= 2:
        P.v(lambda e: e.memset(
            varena.ap(0, 128, [[H * VW, NTC], [VW, H], [1, 1]], c0=DH),
            1.0), inc=[sV], tag="vones")
        VONES_V = sV.n
        wvb = View(sarena_t, 0)
        P.g(lambda e: e.dma_start(
            wvb.ap(0, 128, [[1024, 8], [1, 1024]]),
            qkvw[:, 2 * D:3 * D].rearrange("(dc p) c -> p dc c", p=128)),
            inc16=[dG], tag="vw")
        VW_G = dG.n
        for tc in range(NTC):
            for cb in range(2):
                k = (tc * 2 + cb) % 2
                pb = bank(k)
                for dc in range(NDC):
                    P.t(lambda e, tc=tc, cb=cb, dc=dc, pb=pb: e.matmul(
                            pb,
                            xnf[0:128, dc * XND + 4 + tc * 128:
                                dc * XND + 4 + (tc + 1) * 128],
                            wvb[0:128, dc * 1024 + cb * 512:
                                dc * 1024 + (cb + 1) * 512],
                            start=(dc == 0), stop=False),
                        wait=[(dG, VW_G), (sS, XNF_S),
                              (sS, st8.get(f"vev{k}", 0))],
                        inc=[sT], tag=f"vmm{tc}{cb}")
                P.t(lambda e, tc=tc, cb=cb, pb=pb: e.matmul(
                        pb, c_ones[0:1, ts(tc, 128)],
                        rowq[0:1, cb * 512:(cb + 1) * 512],
                        start=False, stop=True),
                    wait=[(sS, ROWS_S)], inc=[sT], tag=f"vb{tc}{cb}")
                tv = sT.n
                P.s(lambda e, tc=tc, cb=cb, pb=pb: e.copy(
                        varena.ap(0, 128, [[VW, 8], [1, DH]],
                                  c0=tc * H * VW + cb * 8 * VW),
                        pb[:, 0:512].rearrange("p (h d) -> p h d", h=8)),
                    wait=[(sT, tv), (sV, VONES_V)], inc=[sS],
                    tag=f"vev{tc}{cb}")
                st8[f"vev{k}"] = sS.n
        V_S = sS.n

        paS = contextlib.ExitStack()
        qt = [[View(xnt_t, (2 * b + j) * 2048) for j in range(2)]
              for b in range(2)]
        kt = [[View(xnt_t, (4 + 2 * b + j) * 2048) for j in range(2)]
              for b in range(2)]
        stgq = [View(sarena_t, 8192 + b * 2048) for b in range(2)]
        stgk = [View(sarena_t, 12288 + b * 2048) for b in range(2)]
        NPH = 18
        phat = [View(sarena_t, i * 512) for i in range(16)] + \
               [sbv(f"phat{i}", 128, 512, BF16, paS) for i in (16, 17)]
        wqk = [sbv(f"wqk{b}", 128, 2048, BF16, paS) for b in range(2)]
        mcols = sbv("mcols", 128, 32, BF16, paS)
        mstg = sbv("mstg", 32, 128, BF16, paS)
        rl = sbv("rl", 128, 8, F32, paS)

        for b in range(2):
            for j in range(2):
                P.v(lambda e, b=b, j=j: e.memset(kt[b][j][64:65, 0:T], 1.0),
                    wait=[(sS, XNF_S)], inc=[sV], tag="kones")
        KONES_V = sV.n

        phslot = {}
        slot_use = {}
        seq = [0]

        for p in range(8):
            b = p % 2
            P.g(lambda e, p=p, b=b: e.dma_start(
                wqk[b].ap(0, 128, [[128, 8], [1, 128]]),
                qkvw[:, p * 128:(p + 1) * 128]
                .rearrange("(dc p) c -> p dc c", p=128)),
                wait=[(sT, st8.get(f"a0last{b}", 0)),
                      (sV, st8.get(f"a0lastv{b}", 0)),
                      (sS, LN0_S)],
                inc16=[dG], tag=f"a0wq{p}")
            P.g(lambda e, p=p, b=b: e.dma_start(
                wqk[b].ap(0, 128, [[128, 8], [1, 128]], c0=1024),
                qkvw[:, D + p * 128: D + (p + 1) * 128]
                .rearrange("(dc p) c -> p dc c", p=128)),
                inc16=[dG], tag=f"a0wk{p}")
            WQK_G = dG.n
            for qk in range(2):
                stg = (stgq if qk == 0 else stgk)[b]
                for tb in range(4):
                    k2 = (qk * 4 + tb) % 2
                    pb = bank(k2)
                    for dc in range(NDC):
                        P.t(lambda e, qk=qk, tb=tb, dc=dc, pb=pb, b=b:
                            e.matmul(
                                pb,
                                wqk[b][0:128, qk * 1024 + dc * 128:
                                       qk * 1024 + (dc + 1) * 128],
                                xnf[0:128, dc * XND + 4 + tb * 512:
                                    dc * XND + 4 + (tb + 1) * 512],
                                start=(dc == 0), stop=(dc == NDC - 1)),
                            wait=[(dG, WQK_G), (sS, XNF_S),
                                  (sS, st8.get(f"a0ev{k2}", 0)),
                                  (sV, st8.get(f"a0mx{k2}", 0))],
                            inc=[sT], tag=f"a0qkv{p}{qk}{tb}")
                    tv = sT.n
                    bcol = p + 8 * qk
                    P.s(lambda e, pb=pb, stg=stg, tb=tb, bcol=bcol:
                        e.activation(stg[0:128, ts(tb, 512)], pb, AF.Identity,
                                     bias=qkvbT_sb[0:128, bcol:bcol + 1],
                                     scale=1.0),
                        wait=[(sT, tv), (dW, CONST_W),
                              (sV, st8.get(f"a0stg{qk}{b}", 0))],
                        inc=[sS], tag=f"a0st{p}{qk}{tb}")
                    st8[f"a0ev{k2}"] = sS.n
            EV_S = sS.n
            for qk in range(2):
                stg = (stgq if qk == 0 else stgk)[b]
                dd = (qt if qk == 0 else kt)[b]
                P.s(lambda e, stg=stg, dd=dd: e.dma_start(
                        dd[0][0:64, 0:T], stg[0:64, 0:T]),
                    wait=[(sS, EV_S)], inc16=[dV], tag=f"a0d{p}{qk}a")
                P.s(lambda e, stg=stg, dd=dd: e.dma_start(
                        dd[1][0:64, 0:T], stg[64:128, 0:T]),
                    inc16=[dV], tag=f"a0d{p}{qk}b")
                st8[f"a0stg{qk}{b}"] = dV.n
            QK_DV = dV.n
            for ti in range(NTC):
                ns = 16 * (ti + 1)
                for hh in range(2):
                    k2 = (2 * ti + hh) % 2
                    P.t(lambda e, ti=ti, ns=ns, hh=hh, k2=k2, b=b: e.matmul(
                            bank(k2, w=ns),
                            qt[b][hh][0:64, ts(ti, 128)],
                            kt[b][hh].ap(0, 64, [[8, ns]]),
                            start=True, stop=True),
                        wait=[(dV, QK_DV), (sV, st8.get(f"a0mx{k2}", 0)),
                              (sS, st8.get(f"a0ev{k2}", 0))],
                        inc=[sT], tag=f"a0sp{p}{ti}{hh}")
                    tv = sT.n
                    P.v(lambda e, ns=ns, k2=k2: e.tensor_tensor(
                            bank(k2, w=16, off=ns - 16),
                            bank(k2, w=16, off=ns - 16),
                            c_sampm[0:128, 0:16], ALU.add),
                        wait=[(sT, tv), (dW, CONST_W)], inc=[sV],
                        novser=True, tag=f"a0spm{p}{ti}{hh}")
                    P.v(lambda e, ti=ti, ns=ns, hh=hh, k2=k2: e.tensor_reduce(
                            mcols[0:128, hh * 16 + ti: hh * 16 + ti + 1],
                            bank(k2, w=ns), AX.X, ALU.max),
                        inc=[sV], tag=f"a0mx{p}{ti}{hh}")
                    st8[f"a0mx{k2}"] = sV.n
            MX_V = sV.n
            P.t(lambda e: e.transpose(PT0[0:32, 0:128], mcols[0:128, 0:32],
                                      c_identn[0:128, 0:128]),
                wait=[(sV, MX_V), (sS, st8.get("a0mse", 0))],
                inc=[sT], tag=f"a0mt{p}")
            tv = sT.n
            P.s(lambda e: e.copy(mstg[0:32, 0:128], PT0[0:32, 0:128]),
                wait=[(sT, tv)], inc=[sS], tag=f"a0mse{p}")
            st8["a0mse"] = sS.n
            sv = sS.n
            for hh in range(2):
                dst = qt[b][hh]
                P.s(lambda e, hh=hh, dst=dst: e.dma_start(
                        dst.ap(64, 1, [[128, 16], [1, 128]]),
                        mstg[hh * 16:(hh + 1) * 16, 0:128]),
                    wait=[(sS, sv)], inc16=[dV], tag=f"a0md{p}{hh}")
            MF_DV = dV.n
            for hh in range(2):
                hcur = 2 * p + hh
                qq, kk = qt[b][hh], kt[b][hh]
                for tb in range(4):
                    nsi = 4 * tb + 4
                    for si in range(nsi):
                        k2 = 2 + (si % 2)
                        pb = bank(k2)
                        P.t(lambda e, si=si, tb=tb, qq=qq, kk=kk, pb=pb:
                            e.matmul(pb, kk[0:65, ts(si, 128)],
                                     qq[0:65, ts(tb, 512)],
                                     start=True, stop=True),
                            wait=[(dV, MF_DV), (sV, KONES_V),
                                  (sS, st8.get(f"a0pe{k2}", 0))],
                            inc=[sT], tag=f"a0s{p}{hh}{tb}{si}")
                        tv = sT.n
                        wvx = []
                        if si >= 4 * tb:
                            kk4 = si - 4 * tb
                            P.v(lambda e, kk4=kk4, pb=pb: e.tensor_tensor(
                                    pb, pb, c_cmask4[0:128, ts(kk4, 512)],
                                    ALU.add),
                                wait=[(sT, tv), (dW, CONST_W)], inc=[sV],
                                novser=True, tag=f"a0msk{p}")
                            wvx = [(sV, sV.n)]
                        slot = seq[0] % NPH
                        seq[0] += 1
                        ph = phat[slot]
                        P.s(lambda e, pb=pb, ph=ph: e.activation(
                                ph[0:128, 0:512], pb, AF.Exp, bias=0.0,
                                scale=0.125),
                            wait=[(sT, tv),
                                  (sT, slot_use.get(slot, 0))] + wvx,
                            inc=[sS], tag=f"a0e{p}{hh}{tb}{si}")
                        st8[f"a0pe{k2}"] = sS.n
                        phslot[si] = (slot, sS.n)
                    for tix in range(4):
                        ti = 4 * tb + tix
                        kpv = 4 + (tix % 2)
                        pvb = bank(kpv)
                        for si in range(ti + 1):
                            slot, pr = phslot[si]
                            P.t(lambda e, slot=slot, tix=tix, si=si, ti=ti,
                                hcur=hcur, pvb=pvb: e.matmul(
                                    pvb[:, 0:VW],
                                    phat[slot][0:128, ts(tix, 128)],
                                    varena[0:128,
                                           si * H * VW + hcur * VW:
                                           si * H * VW + (hcur + 1) * VW],
                                    start=(si == 0), stop=(si == ti)),
                                wait=[(sS, pr), (sS, V_S),
                                      (sV, st8.get(f"a0pv{kpv}", 0))],
                                inc=[sT], tag=f"a0pv{p}{hh}{ti}{si}")
                            slot_use[slot] = sT.n
                        tv = sT.n
                        P.v(lambda e, tix=tix, pvb=pvb: e.reciprocal(
                                rl[0:128, tix:tix + 1], pvb[:, DH:DH + 1]),
                            wait=[(sT, tv)], inc=[sV], novser=True,
                            tag=f"a0rl{p}")
                        P.v(lambda e, ti=ti, tix=tix, hcur=hcur, pvb=pvb:
                            e.tensor_scalar(
                                attn[0:128, ti * 1024 + hcur * 64:
                                     ti * 1024 + (hcur + 1) * 64],
                                pvb[:, 0:DH], rl[0:128, tix:tix + 1], None,
                                ALU.mult),
                            inc=[sV], tag=f"a0n{p}{hh}{ti}")
                        st8[f"a0pv{kpv}"] = sV.n
            st8[f"a0last{b}"] = sT.n
            st8[f"a0lastv{b}"] = sV.n
        ATT0_V = sV.n
        ATT0_S = sS.n
        ATT0_T = sT.n
        paS.close()
        taps[2] = ("arena_tm", attn, 1024)

    if stages >= 3:
        plkS = contextlib.ExitStack()
        w2oh = [View(xnt_t, 0), View(xnt_t, 2048)]
        ptmp = View(xnt_t, 4096)
        wtm = View(xnt_t, 6144)
        Oraw = View(xnt_t, 10240)
        ostg = View(xnt_t, 12288)
        Lu = View(sarena_t, 0)
        r2h = [View(sarena_t, 3072), View(sarena_t, 5120)]
        w2oraw = [View(sarena_t, 7168), View(sarena_t, 9216)]
        wbig = View(sarena_t, 11264)
        wfmbuf = [sbv(f"wfmbuf{i}", 64, 512, BF16, plkS) for i in range(2)]
        lsq = sbv("lsq", 128, 96, F32, plkS)
        lss = sbv("lss", 128, 64, F32, plkS)
        psumh = sbv("psumh", 36, 16, BF16, plkS)
        psumf = sbv("psumf", 36, 16, F32, plkS)
        ptsb = sbv("ptsb", 16, 36, BF16, plkS)
        gtsb = sbv("gtsb", 16, 36, BF16, plkS)
        gh = sbv("gh", 36, 16, BF16, plkS)
        iscr = sbv("iscr", 128, 128, BF16, plkS)
        gsig = sbv("gsig", 128, 16, F32, plkS)
        gtmp = sbv("gtmp", 128, 16, F32, plkS)

        ATTG = ((sV, ATT0_V), (sS, ATT0_S))

        off = 0
        wpos = {}
        for name, src, kch, w in (("w1", w1w, 16, 64), ("w2", w2w, 8, 64),
                                  ("r1", r1w, 8, 64), ("r2", r2w, 8, 64),
                                  ("mg", mgw, 8, 16)):
            wpos[name] = off
            P.g(lambda e, src=src, kch=kch, w=w, off=off: e.dma_start(
                wbig.ap(0, 128, [[w, kch], [1, w]], c0=off),
                src.rearrange("(k p) c -> p k c", p=128)),
                wait=[(sT, ATT0_T)] + list(ATTG), inc16=[dG], tag=f"pw{name}")
            off += kch * w
        PW_G = dG.n

        for wi, name in enumerate(("w1", "w2", "r1", "r2")):
            if name == "w1":
                parts = [(dc, wpos["w1"] + dc * 64, -2) for dc in range(8)] + \
                        [(dc, wpos["w1"] + (8 + dc) * 64, -1) for dc in range(8)]
            else:
                parts = [(dc, wpos[name] + dc * 64, 0) for dc in range(8)]
            n_parts = len(parts)
            for tb in range(4):
                k2 = (wi * 4 + tb) % 2
                pb = bank(k2)
                for i, (dc, woff, sh) in enumerate(parts):
                    P.t(lambda e, dc=dc, woff=woff, sh=sh, tb=tb, pb=pb,
                        i=i, n_parts=n_parts: e.matmul(
                            pb[0:64, :],
                            wbig[0:128, woff:woff + 64],
                            xnf[0:128, dc * XND + 4 + tb * 512 + sh:
                                dc * XND + 4 + (tb + 1) * 512 + sh],
                            start=(i == 0), stop=(i == n_parts - 1)),
                        wait=[(dG, PW_G), (sS, XNF_S), (sV, PAD_V),
                              (sS, st8.get(f"pwe{k2}", 0))],
                        inc=[sT], tag=f"pw{name}{tb}")
                tv = sT.n
                wf = wfmbuf[tb % 2]
                P.s(lambda e, pb=pb, wf=wf: e.copy(wf[0:64, 0:512],
                                                   pb[0:64, :]),
                    wait=[(sT, tv), (sT, st8.get(f"pwt{tb % 2}", 0))],
                    inc=[sS], tag=f"pwe{name}{tb}")
                st8[f"pwe{k2}"] = sS.n
                sv = sS.n
                for k in range(4):
                    tc = tb * 4 + k
                    i2 = wi * 16 + tc
                    pt = (PT0 if i2 % 2 == 0 else PT1)[0:128, 128:192]
                    P.t(lambda e, k=k, wf=wf, pt=pt: e.transpose(
                            pt, wf[0:64, ts(k, 128)], c_ident[0:64, 0:64]),
                        wait=[(sS, sv), (sS, st8.get(f"pwte{i2 - 2}", 0))],
                        inc=[sT], tag=f"pwt{name}{tc}")
                    tv2 = sT.n
                    P.s(lambda e, tc=tc, wi=wi, pt=pt: e.copy(
                            wtm[0:128, tc * 256 + wi * 64:
                                tc * 256 + (wi + 1) * 64],
                            bass.AP(pt.tensor, pt.offset,
                                    [[1024, 128], [1, 64]])),
                        wait=[(sT, tv2)], inc=[sS], tag=f"pwte{name}{tc}")
                    st8[f"pwte{i2}"] = sS.n
                st8[f"pwt{tb % 2}"] = sT.n
        WTM_S = sS.n

        IGRP = [(0, 3, 1, 0), (1, 2, 2, 3), (2, 1, 3, 5)]
        for tc in range(NTC):
            for lt in range(2):
                aoff = tc * 256 + (0 if lt == 0 else 128)
                luoff = tc * 192 + lt * 96
                for src0 in range(2):
                    toff = 0 if src0 == 0 else 96
                    for (i, nj, j0, po) in IGRP:
                        p1o = aoff + (0 if src0 == 0 else 64)
                        p2o = aoff + (64 if src0 == 0 else 0)
                        P.v(lambda e, i=i, nj=nj, j0=j0, p1o=p1o, p2o=p2o,
                            po=po, toff=toff: e.tensor_tensor(
                                Oraw.ap(0, 128, [[6, 16], [1, nj]],
                                        c0=toff + po),
                                wtm.ap(0, 128, [[4, 16], [0, nj]],
                                       c0=p1o + i),
                                wtm.ap(0, 128, [[4, 16], [1, nj]],
                                       c0=p2o + j0),
                                ALU.mult),
                            wait=[(sS, WTM_S), (sT, ATT0_T),
                                  (sV, st8.get("lu_guard", 0)),
                                  (sS, st8.get("lu_guard_s", 0))],
                            inc=[sV], tag=f"ext{tc}{lt}")
                P.v(lambda e: e.tensor_tensor(
                        Oraw.ap(0, 128, [[1, 96]], c0=192),
                        Oraw.ap(0, 128, [[1, 96]], c0=0),
                        Oraw.ap(0, 128, [[1, 96]], c0=96),
                        ALU.subtract),
                    inc=[sV], tag=f"extL{tc}{lt}")
                lv = sV.n
                P.s(lambda e: e.activation(
                        lsq[0:128, 0:96],
                        Oraw.ap(0, 128, [[1, 96]], c0=192), AF.Square),
                    wait=[(sV, lv)], inc=[sS], tag=f"nsq{tc}{lt}")
                sv = sS.n
                P.v(lambda e: e.tensor_reduce(
                        lss[0:128, 0:16],
                        lsq.ap(0, 128, [[6, 16], [1, 6]]),
                        AX.X, ALU.add),
                    wait=[(sS, sv)], inc=[sV], tag=f"nrd{tc}{lt}")
                vv = sV.n
                P.s(lambda e: e.activation(lss[0:128, 16:32],
                                           lss[0:128, 0:16], AF.Sqrt),
                    wait=[(sV, vv)], inc=[sS], tag=f"nsr{tc}{lt}")
                sv = sS.n
                P.v(lambda e: e.tensor_scalar(lss[0:128, 16:32],
                                              lss[0:128, 16:32],
                                              1e-12, None, ALU.max),
                    wait=[(sS, sv)], inc=[sV], tag=f"nmx{tc}{lt}")
                P.v(lambda e: e.reciprocal(lss[0:128, 32:48],
                                           lss[0:128, 16:32]),
                    inc=[sV], tag=f"nrc{tc}{lt}")
                P.v(lambda e, luoff=luoff: e.tensor_tensor(
                        Lu.ap(0, 128, [[1, 96]], c0=luoff),
                        Oraw.ap(0, 128, [[1, 96]], c0=192),
                        lss.ap(0, 128, [[1, 16], [0, 6]], c0=32),
                        ALU.mult),
                    inc=[sV], tag=f"nlu{tc}{lt}")
                st8["lu_guard"] = sV.n
                st8["lu_guard_s"] = sS.n
        LU_V = sV.n

        for tc in range(NTC):
            for lt in range(2):
                luoff = tc * 192 + lt * 96
                base = 0 if lt == 1 else 1024
                for i in range(6):
                    P.v(lambda e, luoff=luoff, base=base, i=i: e.tensor_tensor(
                            Oraw.ap(0, 128, [[64, 16], [1, 6]],
                                    c0=base + i * 6),
                            Lu.ap(0, 128, [[6, 16], [0, 6]], c0=luoff + i),
                            Lu.ap(0, 128, [[6, 16], [1, 6]], c0=luoff),
                            ALU.mult),
                        wait=[(sV, LU_V), (sT, st8.get("o_guard", 0))],
                        inc=[sV], tag=f"out{tc}{lt}{i}")
            OV = sV.n
            for blk in range(16):
                i = tc * 16 + blk
                pt = (PT0 if i % 2 == 0 else PT1)[0:128, 256:384]
                P.t(lambda e, blk=blk, pt=pt: e.transpose(
                        pt, Oraw[0:128, ts(blk, 128)],
                        c_ident[0:128, 0:128]),
                    wait=[(sV, OV), (sS, st8.get(f"ote{i - 2}", 0))],
                    inc=[sT], tag=f"ot{i}")
                tv = sT.n
                P.s(lambda e, blk=blk, pt=pt: e.copy(
                        ostg[0:128, ts(blk, 128)], pt),
                    wait=[(sT, tv)], inc=[sS], tag=f"ote{i}")
                st8[f"ote{i}"] = sS.n
            st8["o_guard"] = sT.n
            sv = sS.n
            P.g(lambda e, tc=tc: e.dma_start(
                    bass.AP(r2sp.tensor, tc * 128,
                            [[T, 128], [128 * T, 16], [1, 128]]),
                    ostg.ap(0, 128, [[128, 16], [1, 128]])),
                wait=[(sS, sv)], inc16=[dG], tag=f"osp{tc}")
        OSP_G = dG.n

        for pg in range(8):
            for hh in range(2):
                P.g(lambda e, pg=pg, hh=hh: e.dma_start(
                        r2h[hh][0:36, 0:T],
                        r2sp[pg, hh * 64: hh * 64 + 36, :]),
                    wait=[(dG, OSP_G), (sT, st8.get("pl_lastT", 0)),
                          (sV, st8.get("pl_lastV", 0)),
                          (sV, LU_V)],
                    inc16=[dG], tag=f"plr{pg}{hh}")
                P.g(lambda e, pg=pg, hh=hh: e.dma_start(
                        w2oraw[hh][0:36, 0:T],
                        r2sp[8 + pg, hh * 64: hh * 64 + 36, :]),
                    inc16=[dG], tag=f"plw{pg}{hh}")
            RS_G = dG.n
            for hh in range(2):
                h = 2 * pg + hh
                for sl in range(4):
                    k2 = sl % 2
                    pb = bank(k2)
                    P.t(lambda e, hh=hh, sl=sl, pb=pb: e.matmul(
                            pb[0:36, :], mjj[0:36, 0:36],
                            w2oraw[hh][0:36, ts(sl, 512)],
                            start=True, stop=True),
                        wait=[(dG, RS_G), (sV, MJJ_V),
                              (sS, st8.get(f"mje{k2}", 0))],
                        inc=[sT], tag=f"mj{pg}{hh}{sl}")
                    tv = sT.n
                    P.s(lambda e, hh=hh, sl=sl, pb=pb: e.copy(
                            w2oh[hh][0:36, ts(sl, 512)], pb[0:36, :]),
                        wait=[(sT, tv)], inc=[sS], tag=f"mje{pg}{hh}{sl}")
                    st8[f"mje{k2}"] = sS.n
                W2O_S = sS.n
                P.v(lambda e, hh=hh, h=h: e.tensor_tensor(
                        ptmp[0:36, 0:T], w2oh[hh][0:36, 0:T],
                        ddbc.ap(0, 36, [[0, 16], [1, 128]], c0=h * 128),
                        ALU.mult),
                    wait=[(sS, W2O_S), (sS, DD_S),
                          (sV, st8.get("ptmp_guard", 0))],
                    inc=[sV], novser=True, tag=f"ph{pg}{hh}")
                vv = sV.n
                P.v(lambda e: e.tensor_reduce(
                        psumf[0:36, 0:16],
                        ptmp.ap(0, 36, [[128, 16], [1, 128]]),
                        AX.X, ALU.add),
                    wait=[(sV, vv)], inc=[sV], tag=f"pr{pg}{hh}")
                st8["ptmp_guard"] = sV.n
                P.v(lambda e: e.tensor_copy(psumh[0:36, 0:16],
                                            psumf[0:36, 0:16]),
                    inc=[sV], tag=f"prc{pg}{hh}")
                vv = sV.n
                P.t(lambda e: e.transpose(PT0[0:16, 384:420],
                                          psumh[0:36, 0:16],
                                          c_ident[0:36, 0:36]),
                    wait=[(sV, vv), (sS, st8.get("gse", 0))], inc=[sT],
                    tag=f"gp{pg}{hh}")
                tv = sT.n
                P.s(lambda e: e.copy(ptsb[0:16, 0:36], PT0[0:16, 384:420]),
                    wait=[(sT, tv)], inc=[sS], tag=f"gpe{pg}{hh}")
                sv = sS.n
                P.t(lambda e, h=h: e.matmul(PS[0:16, 1024:1060],
                                            Th[0:16, ts(h, 16)],
                                            ptsb[0:16, 0:36],
                                            start=True, stop=True),
                    wait=[(sS, sv), (sV, TOE_V),
                          (sS, st8.get("gse2", 0)),
                          (sS, st8.get("a0pe2", 0))],
                    inc=[sT], tag=f"gm{pg}{hh}")
                tv = sT.n
                P.s(lambda e: e.copy(gtsb[0:16, 0:36],
                                     bank(2, w=36)[0:16, :]),
                    wait=[(sT, tv)], inc=[sS], tag=f"gme{pg}{hh}")
                st8["gse2"] = sS.n
                sv = sS.n
                P.t(lambda e: e.transpose(PT0[0:36, 420:436],
                                          gtsb[0:16, 0:36],
                                          c_ident[0:16, 0:16]),
                    wait=[(sS, sv)], inc=[sT], tag=f"gt{pg}{hh}")
                tv = sT.n
                P.s(lambda e: e.copy(gh[0:36, 0:16], PT0[0:36, 420:436]),
                    wait=[(sT, tv)], inc=[sS], tag=f"gte{pg}{hh}")
                st8["gse"] = sS.n
                G_S = sS.n
                for c in range(16):
                    kc = 3 + (c % 2)
                    pb = bank(kc)
                    P.t(lambda e, hh=hh, c=c, pb=pb: e.matmul(
                            pb[:, 0:128], r2h[hh][0:36, ts(c, 128)],
                            w2oh[hh][0:36, ts(c, 128)],
                            start=True, stop=True),
                        wait=[(sS, W2O_S),
                              (sV, st8.get(f"ince{kc}", 0))],
                        inc=[sT], tag=f"inc{pg}{hh}{c}")
                    P.t(lambda e, hh=hh, c=c, pb=pb: e.matmul(
                            pb[:, 128:144], r2h[hh][0:36, ts(c, 128)],
                            gh[0:36, 0:16], start=False, stop=True),
                        wait=[(sS, G_S)], inc=[sT], tag=f"int{pg}{hh}{c}")
                    tv = sT.n
                    P.v(lambda e, h=h, c=c, pb=pb: e.scalar_tensor_tensor(
                            iscr[0:128, 0:128], pb[:, 0:128], 1.0,
                            toe[0:128, ts(h, 128)],
                            ALU.mult, ALU.mult,
                            accum_out=msc[0:128, h * 16 + c: h * 16 + c + 1]),
                        wait=[(sT, tv), (sV, TOE_V)], inc=[sV],
                        novser=True, tag=f"intra{pg}{hh}{c}")
                    P.v(lambda e, h=h, c=c, pb=pb: e.scalar_tensor_tensor(
                            msc[0:128, h * 16 + c: h * 16 + c + 1],
                            pb[:, 128 + c:129 + c],
                            dpow[0:128, h:h + 1],
                            msc[0:128, h * 16 + c: h * 16 + c + 1],
                            ALU.mult, ALU.add),
                        wait=[(sS, DPOW_S)], inc=[sV], tag=f"comb{pg}{hh}{c}")
                    st8[f"ince{kc}"] = sV.n
            st8["pl_lastT"] = sT.n
            st8["pl_lastV"] = sV.n
        MSC_V = sV.n
        taps[21] = ("small", msc, 128, 256)

        for tc in range(NTC):
            kc = tc % 2
            pb = bank(kc, w=16)
            for dc in range(NDC):
                P.t(lambda e, tc=tc, dc=dc, pb=pb: e.matmul(
                        pb,
                        xnf[0:128, dc * XND + 4 + tc * 128:
                            dc * XND + 4 + (tc + 1) * 128],
                        wbig[0:128, wpos["mg"] + dc * 16:
                             wpos["mg"] + (dc + 1) * 16],
                        start=(dc == 0), stop=False),
                    wait=[(dG, PW_G), (sS, XNF_S),
                          (sS, st8.get(f"mje{kc}", 0)),
                          (sV, st8.get(f"ggv{kc}", 0)),
                          (sV, st8.get(f"ince{3 + kc}", 0))],
                    inc=[sT], tag=f"gg{tc}{dc}")
            P.t(lambda e, tc=tc, pb=pb: e.matmul(
                    pb, c_ones[0:1, ts(tc, 128)], rowg[0:1, 0:16],
                    start=False, stop=True),
                wait=[(sS, ROWS_S)], inc=[sT], tag=f"ggb{tc}")
            tv = sT.n
            P.s(lambda e, pb=pb: e.activation(gsig[0:128, 0:16], pb,
                                              AF.Sigmoid),
                wait=[(sT, tv), (sV, st8.get("ggs_g", 0))], inc=[sS],
                tag=f"ggs{tc}")
            sv = sS.n
            P.v(lambda e, tc=tc: e.tensor_tensor(
                    gtmp[0:128, 0:16],
                    msc.ap(0, 128, [[16, 16], [1, 1]], c0=tc),
                    mscale_bc[0:128, 0:16], ALU.mult),
                wait=[(sV, MSC_V), (dW, MSCALE_W), (sS, sv),
                      (sV, st8.get("gt_guard", 0))],
                inc=[sV], tag=f"gg2{tc}")
            vv = sV.n
            P.s(lambda e: e.activation(gtmp[0:128, 0:16], gtmp[0:128, 0:16],
                                       AF.Sigmoid),
                wait=[(sV, vv)], inc=[sS], tag=f"gg3{tc}")
            sv = sS.n
            P.v(lambda e: e.tensor_tensor(gtmp[0:128, 0:16],
                                          gtmp[0:128, 0:16],
                                          gsig[0:128, 0:16], ALU.mult),
                wait=[(sS, sv)], inc=[sV], tag=f"gg4{tc}")
            st8["ggs_g"] = sV.n
            P.v(lambda e, tc=tc: e.tensor_reduce(
                    gated[0:128, tc:tc + 1], gtmp[0:128, 0:16], AX.X,
                    ALU.add),
                inc=[sV], tag=f"gg5{tc}")
            st8["gt_guard"] = sV.n
            st8[f"ggv{kc}"] = sV.n
        sv = sV.n
        P.s(lambda e: e.activation(gated[0:128, 0:16], gated[0:128, 0:16],
                                   AF.Identity, bias=0.0, scale=1.0 / 16.0),
            wait=[(sV, sv)], inc=[sS], tag="ggs6")
        GATED_S = sS.n
        taps[22] = ("small", gated, 128, 16)
        plkS.close()

    if stages >= 4:
        g3 = gate("p3")
        p3S = contextlib.ExitStack()
        atot = View(xnt_t, 0)
        mvwb = View(varena_t, 0)
        outwb = View(varena_t, 8192)
        P.g(lambda e: e.dma_start(
            mvwb.ap(0, 128, [[1024, 8], [1, 1024]]),
            mvw.rearrange("(dc p) c -> p dc c", p=128)),
            wait=[g3], inc16=[dG], tag="mvw")
        P.g(lambda e: e.dma_start(
            outwb.ap(0, 128, [[1024, 8], [1, 1024]]),
            outw.rearrange("(dc p) c -> p dc c", p=128)),
            inc16=[dG], tag="outw")
        MVW_G = dG.n
        for tc in range(NTC):
            for cb in range(2):
                k2 = (tc * 2 + cb) % 2
                pb = bank(k2)
                for dc in range(NDC):
                    P.t(lambda e, tc=tc, cb=cb, dc=dc, pb=pb: e.matmul(
                            pb,
                            xnf[0:128, dc * XND + 4 + tc * 128:
                                dc * XND + 4 + (tc + 1) * 128],
                            mvwb[0:128, dc * 1024 + cb * 512:
                                 dc * 1024 + (cb + 1) * 512],
                            start=(dc == 0), stop=False),
                        wait=[(dG, MVW_G), g3,
                              (sV, st8.get(f"mvev{k2}", 0))],
                        inc=[sT], tag=f"mv{tc}{cb}")
                P.t(lambda e, tc=tc, cb=cb, pb=pb: e.matmul(
                        pb, c_ones[0:1, ts(tc, 128)],
                        rowm[0:1, cb * 512:(cb + 1) * 512],
                        start=False, stop=True),
                    inc=[sT], tag=f"mvb{tc}{cb}")
                tv = sT.n
                P.v(lambda e, tc=tc, cb=cb, pb=pb: e.scalar_tensor_tensor(
                        atot[0:128, tc * 1024 + cb * 512:
                             tc * 1024 + (cb + 1) * 512],
                        pb, gated[0:128, tc:tc + 1],
                        attn[0:128, tc * 1024 + cb * 512:
                             tc * 1024 + (cb + 1) * 512],
                        ALU.mult, ALU.add),
                    wait=[(sT, tv), (sS, GATED_S)], inc=[sV],
                    tag=f"mvev{tc}{cb}")
                st8[f"mvev{k2}"] = sV.n
        ATOT_V = sV.n
        for tc in range(NTC):
            for dc in range(NDC):
                i = tc * NDC + dc
                pt = (PT0 if i % 2 == 0 else PT1)[0:128, 0:128]
                P.t(lambda e, tc=tc, dc=dc, pt=pt: e.transpose(
                        pt,
                        atot[0:128, tc * 1024 + dc * 128:
                             tc * 1024 + (dc + 1) * 128],
                        c_ident[0:128, 0:128]),
                    wait=[(sV, ATOT_V), (sS, st8.get(f"ate{i - 2}", 0))],
                    inc=[sT], tag=f"att{i}")
                tv = sT.n
                P.s(lambda e, tc=tc, dc=dc, pt=pt: e.copy(
                        attn[0:128, dc * 2048 + tc * 128:
                             dc * 2048 + (tc + 1) * 128], pt),
                    wait=[(sT, tv)], inc=[sS], tag=f"ate{i}")
                st8[f"ate{i}"] = sS.n
        ATFM_S = sS.n
        for tc in range(NTC):
            for cb in range(2):
                k2 = (tc * 2 + cb) % 2
                pb = bank(2 + k2)
                for dc in range(NDC):
                    P.t(lambda e, tc=tc, cb=cb, dc=dc, pb=pb: e.matmul(
                            pb,
                            attn[0:128, dc * 2048 + tc * 128:
                                 dc * 2048 + (tc + 1) * 128],
                            outwb[0:128, dc * 1024 + cb * 512:
                                  dc * 1024 + (cb + 1) * 512],
                            start=(dc == 0), stop=False),
                        wait=[(dG, MVW_G), (sS, ATFM_S),
                              (sS, st8.get(f"opev{k2}", 0))],
                        inc=[sT], tag=f"op{tc}{cb}")
                P.t(lambda e, tc=tc, cb=cb, pb=pb: e.matmul(
                        pb, c_ones[0:1, ts(tc, 128)],
                        rowm[0:1, D + cb * 512: D + (cb + 1) * 512],
                        start=False, stop=True),
                    inc=[sT], tag=f"opb{tc}{cb}")
                tv = sT.n
                P.s(lambda e, tc=tc, cb=cb, pb=pb: e.copy(
                        sarena[0:128, tc * 1024 + cb * 512:
                               tc * 1024 + (cb + 1) * 512], pb),
                    wait=[(sT, tv)], inc=[sS], tag=f"opev{tc}{cb}")
                st8[f"opev{k2}"] = sS.n
        S0_S = sS.n
        p3S.close()
        scopeL0.close()
        taps[3] = ("arena_tm", sarena, 1024)

    ln1fin = {}
    if stages >= 5:
        g4 = gate("p4")
        p4S = contextlib.ExitStack()
        S127 = 1.0 / 127.0
        wgu = sbv("wgu", 128, 2048, BF16, p4S)
        hvt = sbv("hvt", 128, 2048, BF16, p4S)
        gsl = sbv("gsl", 128, 512, BF16, p4S)
        hvb = View(varena_t, 8192)
        g1t = sbv("g1t", 128, 1024, BF16, p4S)
        t1t = sbv("t1t", 128, 1024, BF16, p4S)
        x1t = [sbv(f"x1t{i}", 128, 1024, F32, p4S) for i in range(2)]
        xinb = [sbv(f"xinb{i}", 128, 1024, F32, p4S) for i in range(2)]
        xsq4 = sbv("xsq4", 128, 1024, F32, p4S)

        def wdv(hc):
            if hc < 16:
                return attn[0:128, hc * 1024:(hc + 1) * 1024]
            return varena[0:128, (hc - 16) * 1024:(hc - 15) * 1024]

        for hc in range(NHC):
            P.g(lambda e, hc=hc: e.dma_start(
                    wgu.ap(0, 128, [[128, 8], [1, 128]]),
                    wgT[:, ts(hc, 128)].rearrange("(dc p) c -> p dc c",
                                                  p=128)),
                wait=[g4, (sT, st8.get("p4w", 0))],
                inc16=[dG], tag=f"p4wg{hc}")
            P.g(lambda e, hc=hc: e.dma_start(
                    wgu.ap(0, 128, [[128, 8], [1, 128]], c0=1024),
                    wuT[:, ts(hc, 128)].rearrange("(dc p) c -> p dc c",
                                                  p=128)),
                inc16=[dG], tag=f"p4wu{hc}")
            WL_G = dG.n
            for tb in range(4):
                bg = tb % 2
                bu = 2 + (tb % 2)
                for which, bk in ((0, bg), (1, bu)):
                    for dc in range(NDC):
                        P.t(lambda e, dc=dc, bk=bk, which=which, tb=tb:
                            e.matmul(
                                bank(bk),
                                wgu[0:128, which * 1024 + dc * 128:
                                    which * 1024 + (dc + 1) * 128],
                                xnf[0:128, dc * XND + 4 + tb * 512:
                                    dc * XND + 4 + (tb + 1) * 512],
                                start=(dc == 0), stop=(dc == NDC - 1)),
                            wait=[(dG, WL_G), g4,
                                  (sV, st8.get(f"p4hv{bk}", 0))],
                            inc=[sT], tag=f"p4g{hc}{tb}{which}")
                GU_T = sT.n
                P.s(lambda e, bg=bg: e.activation(gsl[0:128, 0:512],
                                                  bank(bg), AF.Silu,
                                                  bias=0.0, scale=S127),
                    wait=[(sT, GU_T), (sV, st8.get("p4gsl", 0))],
                    inc=[sS], tag=f"p4sl{hc}{tb}")
                sv = sS.n
                P.v(lambda e, tb=tb, bu=bu: e.scalar_tensor_tensor(
                        hvt[0:128, ts(tb, 512)], bank(bu), S127,
                        gsl[0:128, 0:512], ALU.mult, ALU.mult),
                    wait=[(sS, sv), (dG, st8.get("p4hsd", 0))],
                    inc=[sV], novser=True, tag=f"p4h{hc}{tb}")
                st8[f"p4hv{bg}"] = sV.n
                st8[f"p4hv{bu}"] = sV.n
                st8["p4gsl"] = sV.n
            hvv = sV.n
            st8["p4w"] = sT.n
            P.g(lambda e, hc=hc: e.dma_start(hvsp[hc, :, :],
                                             hvt[0:128, 0:T]),
                wait=[(sV, hvv)], inc16=[dG], tag=f"p4hs{hc}")
            st8["p4hsd"] = dG.n
        HV_G = dG.n
        for hc in range(NHC):
            P.g(lambda e, hc=hc: e.dma_start(wdv(hc), wdT[ts(hc, 128), :]),
                inc16=[dG], tag=f"p4wd{hc}")
        WD_G = dG.n

        def wdvh(hc, half):
            if hc < 16:
                return attn[0:128, hc * 1024 + half * 512:
                            hc * 1024 + (half + 1) * 512]
            return varena[0:128, (hc - 16) * 1024 + half * 512:
                          (hc - 16) * 1024 + (half + 1) * 512]

        for tgi in range(8):
            tc0 = 2 * tgi
            P.y(lambda e, tc0=tc0: e.dma_start(
                    hvb.ap(0, 128, [[256, NHC], [1, 256]]),
                    bass.AP(hvsp.tensor, tc0 * 128,
                            [[T, 128], [128 * T, NHC], [1, 256]])),
                wait=[(dG, HV_G), (sT, st8.get("p4yw", 0)),
                      (sS, st8.get(f"p4x{tc0 - 2}", 0)),
                      (sS, st8.get(f"p4x{tc0 - 1}", 0))],
                inc16=[dW], tag=f"p4hr{tgi}")
            HR_W = dW.n
            for j in range(2):
                P.y(lambda e, tc=tc0 + j, j=j: e.dma_start(
                        xinb[j][0:128, 0:1024], x_hbm[ts(tc, 128), :]),
                    inc16=[dW], tag=f"p4xr{tc0 + j}")
            XR_W = dW.n
            for hc in range(NHC):
                for j in range(2):
                    for half in range(2):
                        bk = 2 * j + half
                        P.t(lambda e, j=j, half=half, bk=bk, hc=hc: e.matmul(
                                bank(bk),
                                hvb[0:128, hc * 256 + j * 128:
                                    hc * 256 + (j + 1) * 128],
                                wdvh(hc, half),
                                start=(hc == 0), stop=(hc == NHC - 1)),
                            wait=[(dW, HR_W), (dG, WD_G),
                                  (sS, st8.get(f"p4ge{bk}", 0))],
                            inc=[sT], tag=f"p4y{tgi}_{hc}_{bk}")
            Y_T = sT.n
            st8["p4yw"] = Y_T
            for j in range(2):
                tc = tc0 + j
                P.s(lambda e, j=j: e.activation(
                        g1t[0:128, 0:1024], PS[:, j * 1024:(j + 1) * 1024],
                        AF.Silu, bias=0.0, scale=S127),
                    wait=[(sT, Y_T), (sV, st8.get("p4g1v", 0))],
                    inc=[sS], tag=f"p4ge{tc}")
                st8[f"p4ge{2 * j}"] = sS.n
                st8[f"p4ge{2 * j + 1}"] = sS.n
                sv = sS.n
                P.v(lambda e, tc=tc: e.tensor_tensor(
                        t1t[0:128, 0:1024], g1t[0:128, 0:1024],
                        sarena[0:128, ts(tc, 1024)], ALU.mult),
                    wait=[(sS, sv)], inc=[sV], novser=True,
                    tag=f"p4t1{tc}")
                vv = sV.n
                st8["p4g1v"] = vv
                P.v(lambda e, j=j: e.tensor_tensor(
                        x1t[j][0:128, 0:1024], xinb[j][0:128, 0:1024],
                        t1t[0:128, 0:1024], ALU.add),
                    wait=[(sV, vv), (dW, XR_W),
                          (dW, st8.get(f"p4spw{j}", 0))],
                    inc=[sV], novser=True, tag=f"p4x1{tc}")
                x1v = sV.n
                P.y(lambda e, tc=tc, j=j: e.dma_start(
                        x1sp[ts(tc, 128), :], x1t[j][0:128, 0:1024]),
                    wait=[(sV, x1v)], inc16=[dW], tag=f"p4sp{tc}")
                st8[f"p4spw{j}"] = dW.n
                ln1fin[tc] = emit_ln(x1t[j][0:128, 0:1024],
                                     xnt[0:128, ts(tc, 1024)], D, 1e-5,
                                     slot=tc, xq=xsq4[0:128, 0:1024],
                                     wait_v=[(sV, x1v)], wait_s=[(sV, x1v)],
                                     tag=f"ln1_{tc}")
                st8[f"p4x{tc}"] = ln1fin[tc]
        LN1_S = sS.n
        p4S.close()
        taps[4] = ("x1sp",)

    if stages >= 6:
        gate("p5")
        XNF1_S = transposes_to_fm(ln1fin, "p5f")
        taps[5] = ("arena_tm", xnt, 1024)

    if stages >= 7:
        g6 = gate("p6")
        p6S = contextlib.ExitStack()
        phiwb = View(sarena_t, 0)
        rowp = sbv("rowp", 1, NEXP * PSLOT, BF16, p6S)
        escr = sbv("escr", 128, 2048, BF16, p6S)
        ctile = sbv("ctile", 128, 2048, BF16, p6S)
        ctstg = sbv("ctstg", 128, 2048, BF16, p6S)
        dsum = sbv("dsum", 128, 64, F32, p6S)
        csum = sbv("csum", 128, 32, F32, p6S)
        rdsum = sbv("rdsum", 128, 64, F32, p6S)
        rcsum = sbv("rcsum", 128, 32, F32, p6S)
        negm6 = sbv("negm6", 128, 4, F32, p6S)
        xstage = sbv("xstage", 128, 1024, F32, p6S)
        row2bf(phib_row, lambda c0, cw: rowp[0:1, c0:c0 + cw], NEXP * PSLOT,
               "rp")
        ROWP_V = sV.n
        P.s(lambda e: e.nop(), wait=[(sV, ROWP_V)], inc=[sS])
        ROWP_S = sS.n

        def dtile(tc):
            if tc < 8:
                return View(attn_t, tc * 2048)
            return View(varena_t, (tc - 8) * 2048)

        P.g(lambda e: e.dma_start(
                phiwb.ap(0, 128, [[2048, 8], [1, 2048]]),
                phiT.rearrange("(dc p) c -> p dc c", p=128)),
            wait=[g6], inc16=[dG], tag="p6phi")
        PHI_G = dG.n
        for tc in range(NTC):
            for eb in range(4):
                for dc in range(NDC):
                    P.t(lambda e, tc=tc, eb=eb, dc=dc: e.matmul(
                            bank(eb),
                            xnf[0:128, dc * XND + 4 + tc * 128:
                                dc * XND + 4 + (tc + 1) * 128],
                            phiwb[0:128, dc * 2048 + eb * 512:
                                  dc * 2048 + (eb + 1) * 512],
                            start=(dc == 0), stop=False),
                        wait=[(dG, PHI_G), g6, (sS, XNF1_S),
                              (sS, st8.get("p6e", 0)),
                              (sV, st8.get("p6v", 0))],
                        inc=[sT], tag=f"p6l{tc}{eb}")
                P.t(lambda e, tc=tc, eb=eb: e.matmul(
                        bank(eb), c_ones[0:1, ts(tc, 128)],
                        rowp[0:1, ts(eb, 512)],
                        start=False, stop=True),
                    wait=[(sS, ROWP_S)], inc=[sT], tag=f"p6lb{tc}{eb}")
            tv = sT.n
            P.v(lambda e, tc=tc: e.tensor_reduce(
                    negm6[0:128, tc % 4: tc % 4 + 1], PS[:, 0:2048], AX.X,
                    ALU.max, negate=True),
                wait=[(sT, tv), (sS, st8.get("p6eg", 0))],
                inc=[sV], tag=f"p6m{tc}")
            vv = sV.n
            P.s(lambda e, tc=tc: e.activation(
                    escr[0:128, 0:2048], PS[:, 0:2048], AF.Exp,
                    bias=negm6[0:128, tc % 4: tc % 4 + 1], scale=1.0),
                wait=[(sV, vv), (sV, st8.get("p6cv", 0))],
                inc=[sS], tag=f"p6e{tc}")
            st8["p6e"] = sS.n
            st8["p6eg"] = sS.n
            sv = sS.n
            P.v(lambda e: e.tensor_reduce(
                    dsum[0:128, 0:64],
                    escr.ap(0, 128, [[32, 64], [1, 32]]),
                    AX.X, ALU.add),
                wait=[(sS, sv)], inc=[sV], novser=True, tag=f"p6ds{tc}")
            P.v(lambda e: e.tensor_reduce(
                    csum[0:128, 0:32],
                    escr.ap(0, 128, [[1, 32], [32, 64]]),
                    AX.X, ALU.add),
                inc=[sV], tag=f"p6cs{tc}")
            P.v(lambda e: e.reciprocal(rdsum[0:128, 0:64], dsum[0:128, 0:64]),
                inc=[sV])
            P.v(lambda e: e.reciprocal(rcsum[0:128, 0:32], csum[0:128, 0:32]),
                inc=[sV])
            vv = sV.n
            dt = dtile(tc)
            P.v(lambda e, dt=dt: e.tensor_tensor(
                    dt.ap(0, 128, [[1, 2048]]),
                    escr[0:128, 0:2048],
                    rdsum.ap(0, 128, [[1, 64], [0, 32]]),
                    ALU.mult),
                wait=[(sV, vv), (sT, st8.get("p6xt", 0))],
                inc=[sV], tag=f"p6d{tc}")
            P.v(lambda e: e.tensor_tensor(
                    ctile[0:128, 0:2048], escr[0:128, 0:2048],
                    rcsum.ap(0, 128, [[0, 64], [1, 32]]),
                    ALU.mult),
                inc=[sV], tag=f"p6c{tc}")
            st8["p6v"] = sV.n
            st8["p6cv"] = sV.n
            cv = sV.n
            for blk in range(16):
                i = tc * 16 + blk
                pt = (PT0 if i % 2 == 0 else PT1)[0:128, 0:128]
                P.t(lambda e, blk=blk, pt=pt: e.transpose(
                        pt, ctile[0:128, ts(blk, 128)],
                        c_ident[0:128, 0:128]),
                    wait=[(sV, cv), (sS, st8.get(f"cte{i - 2}", 0))],
                    inc=[sT], tag=f"ctt{i}")
                tv2 = sT.n
                P.s(lambda e, blk=blk, pt=pt: e.copy(
                        ctstg[0:128, ts(blk, 128)], pt),
                    wait=[(sT, tv2)], inc=[sS], tag=f"cte{i}")
                st8[f"cte{i}"] = sS.n
            sv = sS.n
            P.g(lambda e, tc=tc: e.dma_start(
                    bass.AP(ctsp.tensor, tc * 128,
                            [[2048, 128], [128 * 2048, 16], [1, 128]]),
                    ctstg.ap(0, 128, [[128, 16], [1, 128]])),
                wait=[(sS, sv)], inc16=[dG], tag=f"p6ct{tc}")
        D_V = sV.n
        CT_G = dG.n
        for ec in range(16):
            for tc in range(NTC):
                dt = dtile(tc)
                for half in range(2):
                    P.t(lambda e, dt=dt, ec=ec, tc=tc, half=half: e.matmul(
                            bank(4 + half),
                            dt[0:128, ec * 128:(ec + 1) * 128],
                            xnt[0:128, tc * 1024 + half * 512:
                                tc * 1024 + (half + 1) * 512],
                            start=(tc == 0), stop=(tc == NTC - 1)),
                        wait=[(sV, D_V), (sS, LN1_S),
                              (sS, st8.get("p6xe", 0))],
                        inc=[sT], tag=f"p6x{ec}{tc}{half}")
            st8["p6xt"] = sT.n
            tv = sT.n
            P.s(lambda e: e.copy(xstage[0:128, 0:1024], PS[:, 2048:3072]),
                wait=[(sT, tv)], inc=[sS], tag=f"p6xe{ec}")
            st8["p6xe"] = sS.n
            sv = sS.n
            P.g(lambda e, ec=ec: e.dma_start(ccx[ts(ec, 128), :],
                                             xstage[0:128, 0:1024]),
                wait=[(sS, sv)], inc16=[dG], tag=f"p6xd{ec}")
        XP_G = dG.n
        p6S.close()

    if stages >= 8:
        g7 = gate("p7")
        p7S = contextlib.ExitStack()
        kvfm = sbv("kvfm", 128, 2048, BF16, p7S)
        qn1 = sbv("qn1", 128, 512, BF16, p7S)
        kn1 = sbv("kn1", 128, 512, BF16, p7S)
        rms1 = sbv("rms1", 128, 64, F32, p7S)
        xsq7 = sbv("xsq7", 128, 1024, F32, p7S)
        ggrep = sbv("ggrep", 128, 1, F32, p7S)
        mcols1 = sbv("mcols1", 128, 256, BF16, p7S)
        mstg1 = sbv("mstg1", 128, 256, BF16, p7S)
        rl1 = sbv("rl1", 128, 8, F32, p7S)
        qstg1 = [View(attn_t, 8192 + g * 2048) for g in range(4)]
        kstg1 = [View(varena_t, 8448 + g * 2048) for g in range(4)]
        qtp = [View(xnt_t, j * 2048) for j in range(8)]
        ktp = [View(sarena_t, j * 2048) for j in range(8)]
        expwb = [View(xnf_t, i * 8208) for i in range(2)]

        p7a = contextlib.ExitStack()
        wq1b = sbv("wq1b", 128, 8 * 1024, BF16, p7a)
        wkv1b = sbv("wkv1b", 128, 8 * 128, BF16, p7a)
        wk1b = sbv("wk1b", 128, 512, BF16, p7a)
        wv1b = sbv("wv1b", 128, 512, BF16, p7a)

        P.g(lambda e: e.dma_start(
                wq1b.ap(0, 128, [[1024, 8], [1, 1024]]),
                wq1.rearrange("(dc p) c -> p dc c", p=128)),
            wait=[g7], inc16=[dG], tag="p7wq")
        P.g(lambda e: e.dma_start(
                wkv1b.ap(0, 128, [[128, 8], [1, 128]]),
                wkv1.rearrange("(dc p) c -> p dc c", p=128)),
            inc16=[dG], tag="p7wkv")
        P.g(lambda e: e.dma_start(wk1b[0:128, 0:512], wk1), inc16=[dG])
        P.g(lambda e: e.dma_start(wv1b[0:128, 0:512], wv1), inc16=[dG])
        W7_G = dG.n
        P.y(lambda e: e.dma_start(rows_tmp[0:1, 0:DH], qg_row),
            wait=[g7], inc16=[dW])
        P.y(lambda e: e.dma_start(rows_tmp[0:1, 64:64 + ROPE], kg_row),
            inc16=[dW])
        wv = dW.n
        P.v(lambda e: e.tensor_tensor(rows_tmp[0:1, 128:128 + ROPE],
                                      rows_tmp[0:1, 0:ROPE],
                                      rows_tmp[0:1, 64:64 + ROPE], ALU.mult),
            wait=[(dW, wv)], inc=[sV], tag="p7gg")
        vv = sV.n
        P.y(lambda e: e.dma_start(bounce[3:4, 0:ROPE],
                                  rows_tmp[0:1, 128:128 + ROPE]),
            wait=[(sV, vv)], inc16=[dW])
        wv = dW.n
        P.g(lambda e: e.dma_start(
                ggrep[0:128, 0:1],
                bass.AP(bounce.tensor, 3 * 2048, [[0, 4], [1, 32]])),
            wait=[(dW, wv)], inc16=[dG], tag="p7ggr")
        GG_W = (dG, dG.n)
        for tb in range(4):
            k2 = tb % 2
            pb = bank(k2)
            for dc in range(NDC):
                P.t(lambda e, tb=tb, dc=dc, pb=pb: e.matmul(
                        pb, wkv1b[0:128, ts(dc, 128)],
                        xnf[0:128, dc * XND + 4 + tb * 512:
                            dc * XND + 4 + (tb + 1) * 512],
                        start=(dc == 0), stop=(dc == NDC - 1)),
                    wait=[(dG, W7_G), g7, (sS, st8.get(f"p7kv{k2}", 0)),
                          (sS, st8.get("p6xe", 0))],
                    inc=[sT], tag=f"p7kv{tb}")
            tv = sT.n
            P.s(lambda e, tb=tb, pb=pb: e.copy(kvfm[0:128, ts(tb, 512)], pb),
                wait=[(sT, tv)], inc=[sS], tag=f"p7kve{tb}")
            st8[f"p7kv{k2}"] = sS.n
        KV_S = sS.n
        for tc in range(NTC):
            for cb in range(2):
                for dc in range(NDC):
                    P.t(lambda e, tc=tc, cb=cb, dc=dc: e.matmul(
                            bank(cb),
                            xnf[0:128, dc * XND + 4 + tc * 128:
                                dc * XND + 4 + (tc + 1) * 128],
                            wq1b[0:128, dc * 1024 + cb * 512:
                                 dc * 1024 + (cb + 1) * 512],
                            start=(dc == 0), stop=(dc == NDC - 1)),
                        wait=[(dG, W7_G), (sS, KV_S),
                              (sS, st8.get("p7qe", 0)),
                              (sV, st8.get("p7qv", 0))],
                        inc=[sT], tag=f"p7q{tc}{cb}")
            tv = sT.n
            P.s(lambda e: e.activation(xsq7[0:128, 0:1024], PS[:, 0:1024],
                                       AF.Square),
                wait=[(sT, tv), (sV, st8.get("p7sqv", 0))], inc=[sS],
                tag=f"p7qs{tc}")
            sv = sS.n
            P.v(lambda e: e.tensor_reduce(
                    rms1[0:128, 0:16],
                    xsq7.ap(0, 128, [[64, 16], [1, 64]]),
                    AX.X, ALU.add),
                wait=[(sS, sv)], inc=[sV], tag=f"p7qr{tc}")
            st8["p7sqv"] = sV.n
            vv = sV.n
            P.v(lambda e: e.tensor_scalar(rms1[0:128, 16:32],
                                          rms1[0:128, 0:16],
                                          1.0 / DH, 1e-6, ALU.mult, ALU.add),
                wait=[(sV, vv)], inc=[sV], tag=f"p7qm{tc}")
            vv = sV.n
            P.s(lambda e: e.activation(rms1[0:128, 16:32], rms1[0:128, 16:32],
                                       AF.Sqrt),
                wait=[(sV, vv)], inc=[sS], tag=f"p7qq{tc}")
            sv = sS.n
            P.v(lambda e: e.reciprocal(rms1[0:128, 32:48],
                                       rms1[0:128, 16:32]),
                wait=[(sS, sv)], inc=[sV], tag=f"p7qc{tc}")
            vv = sV.n
            P.v(lambda e: e.tensor_tensor(
                    qn1[0:128, 0:512],
                    bass.AP(PS, 0, [[3072, 128], [64, 16], [1, 32]]),
                    rms1.ap(0, 128, [[1, 16], [0, 32]], c0=32),
                    ALU.mult),
                wait=[(sV, vv)], inc=[sV], tag=f"p7qn{tc}")
            st8["p7qv"] = sV.n
            qnv = sV.n
            P.s(lambda e: e.nop(), wait=[(sV, qnv)], inc=[sS])
            st8["p7qe"] = sS.n
            for g in range(4):
                i = tc * 8 + g
                pt = (PT0 if i % 2 == 0 else PT1)[0:128, 0:128]
                P.t(lambda e, g=g, pt=pt: e.transpose(
                        pt, qn1[0:128, ts(g, 128)],
                        c_ident[0:128, 0:128]),
                    wait=[(sV, qnv), (sS, st8.get(f"p7rt{i - 2}", 0))],
                    inc=[sT], tag=f"p7qt{tc}{g}")
                tv = sT.n
                P.s(lambda e, g=g, tc=tc, pt=pt: e.copy(
                        qstg1[g][0:128, ts(tc, 128)], pt),
                    wait=[(sT, tv)], inc=[sS], tag=f"p7qte{tc}{g}")
                st8[f"p7rt{i}"] = sS.n
            P.t(lambda e, tc=tc: e.matmul(
                    bank(2), kvfm[0:128, ts(tc, 128)], wk1b[0:128, 0:512],
                    start=True, stop=True),
                wait=[(sS, KV_S), (sS, st8.get("p7ke", 0))],
                inc=[sT], tag=f"p7k{tc}")
            tv = sT.n
            P.s(lambda e: e.activation(xsq7[0:128, 0:512], bank(2),
                                       AF.Square),
                wait=[(sT, tv), (sV, st8.get("p7krv", 0))], inc=[sS],
                tag=f"p7ks{tc}")
            sv = sS.n
            P.v(lambda e: e.tensor_reduce(
                    rms1[0:128, 48:64],
                    xsq7.ap(0, 128, [[32, 16], [1, 32]]),
                    AX.X, ALU.add),
                wait=[(sS, sv)], inc=[sV], tag=f"p7kr{tc}")
            vv = sV.n
            P.v(lambda e: e.tensor_scalar(rms1[0:128, 48:64],
                                          rms1[0:128, 48:64],
                                          1.0 / ROPE, 1e-6, ALU.mult, ALU.add),
                wait=[(sV, vv)], inc=[sV], tag=f"p7km{tc}")
            vv = sV.n
            P.s(lambda e: e.activation(rms1[0:128, 48:64], rms1[0:128, 48:64],
                                       AF.Sqrt),
                wait=[(sV, vv)], inc=[sS], tag=f"p7kq{tc}")
            sv = sS.n
            P.v(lambda e: e.reciprocal(rms1[0:128, 48:64],
                                       rms1[0:128, 48:64]),
                wait=[(sS, sv)], inc=[sV], tag=f"p7kc{tc}")
            vv = sV.n
            P.v(lambda e: e.tensor_tensor(
                    kn1[0:128, 0:512], bank(2),
                    rms1.ap(0, 128, [[1, 16], [0, 32]], c0=48),
                    ALU.mult),
                wait=[(sV, vv)], inc=[sV], tag=f"p7kn{tc}")
            knv = sV.n
            st8["p7krv"] = knv
            P.s(lambda e: e.nop(), wait=[(sV, knv)], inc=[sS])
            st8["p7ke"] = sS.n
            for g in range(4):
                i = tc * 8 + 4 + g
                pt = (PT0 if i % 2 == 0 else PT1)[0:128, 0:128]
                P.t(lambda e, g=g, pt=pt: e.transpose(
                        pt, kn1[0:128, ts(g, 128)], c_ident[0:128, 0:128]),
                    wait=[(sV, knv), (sS, st8.get(f"p7rt{i - 2}", 0))],
                    inc=[sT], tag=f"p7kt{tc}{g}")
                tv = sT.n
                P.s(lambda e, g=g, tc=tc, pt=pt: e.copy(
                        kstg1[g][0:128, ts(tc, 128)], pt),
                    wait=[(sT, tv)], inc=[sS], tag=f"p7kte{tc}{g}")
                st8[f"p7rt{i}"] = sS.n
            P.t(lambda e, tc=tc: e.matmul(
                    bank(3), kvfm[0:128, ts(tc, 128)], wv1b[0:128, 0:512],
                    start=True, stop=True),
                wait=[(sS, KV_S), (sS, st8.get("p7vee", 0))],
                inc=[sT], tag=f"p7v{tc}")
            tv = sT.n
            P.s(lambda e, tc=tc: e.copy(
                    varena.ap(0, 128, [[VW1, 16], [1, VD1]], c0=tc * 528),
                    bank(3).rearrange("p (h d) -> p h d", h=16)),
                wait=[(sT, tv)], inc=[sS], tag=f"p7ve{tc}")
            st8["p7vee"] = sS.n
        QKV1_T = sT.n
        QKV1_S = sS.n
        p7a.close()
        P.v(lambda e: e.memset(
            varena.ap(0, 128, [[528, NTC], [VW1, 16], [1, 1]], c0=VD1),
            1.0), wait=[(sS, QKV1_S)], inc=[sV], tag="p7vones")
        VONES1_V = sV.n
        for g in range(4):
            P.s(lambda e, g=g: e.activation(kstg1[g][0:128, 0:T],
                                            kstg1[g][0:128, 0:T],
                                            AF.Identity, bias=0.0,
                                            scale=ggrep[0:128, 0:1]),
                wait=[GG_W, (sS, QKV1_S)], inc=[sS], tag=f"p7gg{g}")
        GGS_S = sS.n
        for n in range(2):
            P.g(lambda e, n=n: e.dma_start(
                    expwb[n].ap(0, 128, [[1024, 8], [1, 1024]]),
                    expw[n].rearrange("(dc p) c -> p dc c", p=128)),
                wait=[(sT, QKV1_T)], inc16=[dG], tag=f"p8w{n}")
        EXW_G = dG.n
        P.g(lambda e: e.collective_compute(
                "ReduceScatter", ALU.add,
                replica_groups=[list(range(N_CORES))],
                ins=[ccx], outs=[ccxrs]),
            wait=[(dG, XP_G)], inc=[sC], tag="RS")
        RS_C = sC.n
        for h in range(H):
            j, rb = h // 2, 64 * (h % 2)
            g, gi = h // 4, h % 4
            P.s(lambda e, j=j, rb=rb, g=g, gi=gi: e.dma_start(
                    qtp[j].ap(rb, 32, [[1, T]]),
                    qstg1[g][gi * 32:(gi + 1) * 32, 0:T]),
                wait=[(sS, QKV1_S)], inc16=[dV], tag=f"p7qd{h}")
            P.s(lambda e, j=j, rb=rb, g=g, gi=gi: e.dma_start(
                    ktp[j].ap(rb, 32, [[1, T]]),
                    kstg1[g][gi * 32:(gi + 1) * 32, 0:T]),
                wait=[(sS, GGS_S)], inc16=[dV], tag=f"p7kd{h}")
        PAIR_DV = dV.n
        for j in range(8):
            for rb in (32, 96):
                P.v(lambda e, j=j, rb=rb: e.memset(
                        ktp[j][rb:rb + 1, 0:T], 1.0),
                    wait=[(dV, PAIR_DV)], inc=[sV], tag=f"p7ko{j}")
        KONES1_V = sV.n
        for h in range(H):
            j, rb = h // 2, 64 * (h % 2)
            for ti in range(NTC):
                ns = 16 * (ti + 1)
                k2 = (h * NTC + ti) % 2
                P.t(lambda e, j=j, rb=rb, ti=ti, ns=ns, k2=k2: e.matmul(
                        bank(k2, w=ns),
                        qtp[j].ap(rb, 32, [[1, 128]], c0=ti * 128),
                        ktp[j].ap(rb, 32, [[8, ns]]),
                        start=True, stop=True),
                    wait=[(dV, PAIR_DV), (sV, KONES1_V),
                          (sV, st8.get(f"p7mx{k2}", 0)),
                          (sS, st8.get("p7vee", 0))],
                    inc=[sT], tag=f"p7sp{h}{ti}")
                tv = sT.n
                P.v(lambda e, ns=ns, k2=k2: e.tensor_tensor(
                        bank(k2, w=16, off=ns - 16),
                        bank(k2, w=16, off=ns - 16),
                        c_sampm[0:128, 0:16], ALU.add),
                    wait=[(sT, tv)], inc=[sV], novser=True,
                    tag=f"p7spm{h}{ti}")
                P.v(lambda e, h=h, ti=ti, ns=ns, k2=k2: e.tensor_reduce(
                        mcols1[0:128, h * 16 + ti: h * 16 + ti + 1],
                        bank(k2, w=ns), AX.X, ALU.max),
                    inc=[sV], tag=f"p7mx{h}{ti}")
                st8[f"p7mx{k2}"] = sV.n
        MX1_V = sV.n
        for half in range(2):
            P.t(lambda e, half=half: e.transpose(
                    PT0[0:128, 0:128], mcols1[0:128, ts(half, 128)],
                    c_identn[0:128, 0:128]),
                wait=[(sV, MX1_V), (sS, st8.get("p7mse", 0))],
                inc=[sT], tag=f"p7mt{half}")
            tv = sT.n
            P.s(lambda e, half=half: e.copy(mstg1[0:128, ts(half, 128)],
                                            PT0[0:128, 0:128]),
                wait=[(sT, tv)], inc=[sS], tag=f"p7mse{half}")
            st8["p7mse"] = sS.n
        MS1_S = sS.n
        for h in range(H):
            j = h // 2
            rowi = 32 + 64 * (h % 2)
            src_r = (h % 8) * 16
            src_c = (h // 8) * 128
            P.s(lambda e, j=j, rowi=rowi, src_r=src_r, src_c=src_c:
                e.dma_start(
                    qtp[j].ap(rowi, 1, [[128, 16], [1, 128]]),
                    mstg1[src_r:src_r + 16, src_c:src_c + 128]),
                wait=[(sS, MS1_S)], inc16=[dV], tag=f"p7md{h}")
        MF1_DV = dV.n
        phslot1 = {}
        slot_use1 = {}
        seq1 = [0]
        NPH1 = 16
        phat1 = [View(attn_t, 8192 + i * 512) for i in range(NPH1)]
        for h in range(H):
            j, rb = h // 2, 64 * (h % 2)
            for tb in range(4):
                nsi = 4 * tb + 4
                for si in range(nsi):
                    k2 = 2 + (si % 2)
                    pb = bank(k2)
                    P.t(lambda e, j=j, rb=rb, si=si, tb=tb, pb=pb: e.matmul(
                            pb,
                            ktp[j].ap(rb, 33, [[1, 128]], c0=si * 128),
                            qtp[j].ap(rb, 33, [[1, 512]], c0=tb * 512),
                            start=True, stop=True),
                        wait=[(dV, MF1_DV),
                              (sS, st8.get(f"p7pe{k2}", 0))],
                        inc=[sT], tag=f"p7s{h}{tb}{si}")
                    tv = sT.n
                    wvx = []
                    if si >= 4 * tb:
                        kk4 = si - 4 * tb
                        P.v(lambda e, kk4=kk4, pb=pb: e.tensor_tensor(
                                pb, pb, c_cmask4[0:128, ts(kk4, 512)],
                                ALU.add),
                            wait=[(sT, tv)], inc=[sV], novser=True,
                            tag=f"p7msk{h}")
                        wvx = [(sV, sV.n)]
                    slot = seq1[0] % NPH1
                    seq1[0] += 1
                    ph = phat1[slot]
                    P.s(lambda e, pb=pb, ph=ph: e.activation(
                            ph[0:128, 0:512], pb, AF.Exp, bias=0.0,
                            scale=0.125),
                        wait=[(sT, tv),
                              (sT, slot_use1.get(slot, 0))] + wvx,
                        inc=[sS], tag=f"p7e{h}{tb}{si}")
                    st8[f"p7pe{k2}"] = sS.n
                    phslot1[si] = (slot, sS.n)
                for tix in range(4):
                    ti = 4 * tb + tix
                    kpv = 4 + (tix % 2)
                    pvb = bank(kpv)
                    for si in range(ti + 1):
                        slot, pr = phslot1[si]
                        P.t(lambda e, slot=slot, tix=tix, si=si, ti=ti, h=h,
                            pvb=pvb: e.matmul(
                                pvb[:, 0:VW1],
                                phat1[slot][0:128, ts(tix, 128)],
                                varena.ap(0, 128, [[1, VW1]],
                                          c0=si * 528 + h * VW1),
                                start=(si == 0), stop=(si == ti)),
                            wait=[(sS, pr), (sV, VONES1_V),
                                  (sV, st8.get(f"p7pv{kpv}", 0))],
                            inc=[sT], tag=f"p7pv{h}{ti}{si}")
                        slot_use1[slot] = sT.n
                    tv = sT.n
                    P.v(lambda e, tix=tix, pvb=pvb: e.reciprocal(
                            rl1[0:128, tix:tix + 1], pvb[:, VD1:VD1 + 1]),
                        wait=[(sT, tv)], inc=[sV], novser=True,
                        tag=f"p7rl{h}")
                    P.v(lambda e, ti=ti, tix=tix, h=h, pvb=pvb:
                        e.tensor_scalar(
                            attn[0:128, ti * 512 + h * 32:
                                 ti * 512 + (h + 1) * 32],
                            pvb[:, 0:VD1], rl1[0:128, tix:tix + 1], None,
                            ALU.mult),
                        inc=[sV], tag=f"p7n{h}{ti}")
                    st8[f"p7pv{kpv}"] = sV.n
        ATT1_V = sV.n
        taps[6] = ("arena_tm", attn, 512)
        p7b = contextlib.ExitStack()
        wo1b = sbv("wo1b", 128, 4 * 1024, BF16, p7b)
        P.g(lambda e: e.dma_start(
                wo1b.ap(0, 128, [[1024, 4], [1, 1024]]),
                wo1e.rearrange("(cc p) c -> p cc c", p=128)),
            inc16=[dG], tag="p7wo")
        WO_G = dG.n
        for tc in range(NTC):
            for cc in range(4):
                i = tc * 4 + cc
                pt = (PT0 if i % 2 == 0 else PT1)[0:128, 0:128]
                P.t(lambda e, tc=tc, cc=cc, pt=pt: e.transpose(
                        pt,
                        attn[0:128, tc * 512 + cc * 128:
                             tc * 512 + (cc + 1) * 128],
                        c_ident[0:128, 0:128]),
                    wait=[(sV, ATT1_V), (sS, st8.get(f"a1te{i - 2}", 0))],
                    inc=[sT], tag=f"a1t{i}")
                tv = sT.n
                P.s(lambda e, tc=tc, cc=cc, pt=pt: e.copy(
                        attn[0:128, 8192 + cc * 2048 + tc * 128:
                             8192 + cc * 2048 + (tc + 1) * 128], pt),
                    wait=[(sT, tv)], inc=[sS], tag=f"a1te{i}")
                st8[f"a1te{i}"] = sS.n
        AT1FM_S = sS.n
        for tc in range(NTC):
            for cb in range(2):
                k2 = (tc * 2 + cb) % 2
                pb = bank(k2)
                for cc in range(4):
                    P.t(lambda e, tc=tc, cb=cb, cc=cc, pb=pb: e.matmul(
                            pb,
                            attn[0:128, 8192 + cc * 2048 + tc * 128:
                                 8192 + cc * 2048 + (tc + 1) * 128],
                            wo1b[0:128, cc * 1024 + cb * 512:
                                 cc * 1024 + (cb + 1) * 512],
                            start=(cc == 0), stop=(cc == 3)),
                        wait=[(sS, AT1FM_S), (dG, WO_G),
                              (sS, st8.get(f"p7oe{k2}", 0))],
                        inc=[sT], tag=f"p7o{tc}{cb}")
                tv = sT.n
                P.s(lambda e, tc=tc, cb=cb, pb=pb: e.copy(
                        sarena[0:128, tc * 1024 + cb * 512:
                               tc * 1024 + (cb + 1) * 512], pb),
                    wait=[(sT, tv)], inc=[sS], tag=f"p7oe{tc}{cb}")
                st8[f"p7oe{k2}"] = sS.n
        S1_S = sS.n
        p7b.close()
        p7S.close()
        taps[7] = ("arena_tm", sarena, 1024)

    if stages >= 9:
        g8 = gate("p8")
        p8S = contextlib.ExitStack()
        xh = [sbv(f"xh{i}", 128, 1024, BF16, p8S) for i in range(2)]
        xin8 = [sbv(f"xin8{i}", 128, 1024, F32, p8S) for i in range(2)]
        xsq8 = sbv("xsq8", 128, 1024, F32, p8S)
        xht = sbv("xht", 128, 2048, BF16, p8S)
        ystg = sbv("ystg", 32, 1024, BF16, p8S)
        ebrow = sbv("ebrow", 1, 1024, BF16, p8S)
        xh_fin = {}
        for i in range(2):
            P.g(lambda e, i=i: e.dma_start(xin8[i][0:128, 0:1024],
                                           ccxrs[ts(i, 128), :]),
                wait=[(sC, RS_C), g8], inc16=[dG], tag=f"p8x{i}")
            wv = dG.n
            P.v(lambda e, i=i: e.tensor_scalar(
                    xin8[i][0:128, 0:1024], xin8[i][0:128, 0:1024],
                    65000.0, -33000.0, ALU.min, ALU.max),
                wait=[(dG, wv)], inc=[sV], tag=f"p8c{i}")
            vv = sV.n
            xh_fin[i] = emit_ln(xin8[i][0:128, 0:1024], xh[i][0:128, 0:1024],
                                D, 1e-5, slot=i, xq=xsq8[0:128, 0:1024],
                                wait_v=[(sV, vv)], wait_s=[(sV, vv)],
                                tag=f"p8ln{i}", final_func=AF.Tanh)
        for i2 in range(2):
            for nl in range(4):
                n = i2 * 4 + nl
                for dc in range(NDC):
                    i3 = n * NDC + dc
                    pt = (PT0 if i3 % 2 == 0 else PT1)[0:128, 0:32]
                    P.t(lambda e, i2=i2, nl=nl, dc=dc, pt=pt: e.transpose(
                            pt,
                            xh[i2][nl * 32:(nl + 1) * 32,
                                   dc * 128:(dc + 1) * 128],
                            c_ident[nl * 32:(nl + 1) * 32,
                                    nl * 32:(nl + 1) * 32],
                            tile_position=(nl * 32, 0)),
                        wait=[(sS, xh_fin[i2]),
                              (sS, st8.get(f"p8t{i3 - 2}", 0))],
                        inc=[sT], tag=f"p8t{i3}")
                    tv = sT.n
                    P.s(lambda e, n=n, dc=dc, pt=pt: e.copy(
                            xht[0:128, dc * 256 + n * 32:
                                dc * 256 + (n + 1) * 32], pt),
                        wait=[(sT, tv)], inc=[sS], tag=f"p8te{i3}")
                    st8[f"p8t{i3}"] = sS.n
        XHT_S = sS.n
        for n in range(EMINE):
            nb = n % 2
            if n >= 2:
                P.g(lambda e, n=n, nb=nb: e.dma_start(
                        expwb[nb].ap(0, 128, [[1024, 8], [1, 1024]]),
                        expw[n].rearrange("(dc p) c -> p dc c", p=128)),
                    wait=[(sT, st8.get(f"p8mw{nb}", 0))],
                    inc16=[dG], tag=f"p8w{n}")
            EXWn_G = dG.n
            P.g(lambda e, n=n: e.dma_start(
                    ebrow[0:1, 0:1024],
                    bass.AP(expb.tensor, n * 1024, [[1024, 1], [1, 1024]])),
                wait=[(sS, st8.get("p8ye", 0)), g8], inc16=[dG],
                tag=f"p8eb{n}")
            EB_W = (dG, dG.n)
            for Hb in range(2):
                k2 = Hb
                pb = bank(k2)
                for dc in range(NDC):
                    P.t(lambda e, n=n, nb=nb, Hb=Hb, dc=dc, pb=pb: e.matmul(
                            pb[0:32, :],
                            xht[0:128, dc * 256 + n * 32:
                                dc * 256 + (n + 1) * 32],
                            expwb[nb][0:128, dc * 1024 + Hb * 512:
                                      dc * 1024 + (Hb + 1) * 512],
                            start=(dc == 0), stop=False),
                        wait=[(sS, XHT_S), (dG, EXWn_G),
                              (sS, st8.get("p8ye", 0))],
                        inc=[sT], tag=f"p8m{n}{Hb}")
                P.t(lambda e, n=n, Hb=Hb, pb=pb: e.matmul(
                        pb[0:32, :], c_ones[0:1, 0:32],
                        ebrow[0:1, Hb * 512:(Hb + 1) * 512],
                        start=False, stop=True),
                    wait=[EB_W], inc=[sT], tag=f"p8mb{n}{Hb}")
                tv = sT.n
                P.s(lambda e, Hb=Hb, pb=pb: e.copy(
                        ystg[0:32, ts(Hb, 512)], pb[0:32, :]),
                    wait=[(sT, tv)], inc=[sS], tag=f"p8ye{n}{Hb}")
            st8["p8ye"] = sS.n
            st8[f"p8mw{nb}"] = sT.n
            sv = sS.n
            P.g(lambda e, n=n: e.dma_start(ccyt[n * 32:(n + 1) * 32, :],
                                           ystg[0:32, 0:1024]),
                wait=[(sS, sv)], inc16=[dG], tag=f"p8yd{n}")
        YT_G = dG.n
        P.g(lambda e: e.collective_compute(
                "AllGather", ALU.bypass,
                replica_groups=[list(range(N_CORES))],
                ins=[ccyt], outs=[ccytag]),
            wait=[(dG, YT_G)], inc=[sC], tag="AG")
        AG_C = sC.n
        p8S.close()

    if stages >= FULL:
        g9 = gate("p9")
        p9S = contextlib.ExitStack()
        ctbuf = [sbv(f"ctbuf{i}", 128, 2048, BF16, p9S) for i in range(2)]
        g9t = sbv("g9t", 128, 1024, BF16, p9S)
        t9t = sbv("t9t", 128, 1024, BF16, p9S)
        x9 = [sbv(f"x9{i}", 128, 1024, F32, p9S) for i in range(2)]
        xo9 = [sbv(f"xo9{i}", 128, 1024, F32, p9S) for i in range(2)]
        yt = [View(attn_t, ec * 1024) for ec in range(16)]
        OUT_W = [0]
        for ec in range(16):
            P.y(lambda e, ec=ec: e.dma_start(yt[ec][0:128, 0:1024],
                                             ccytag[ts(ec, 128), :]),
                wait=[(sC, AG_C), g9], inc16=[dW], tag=f"p9y{ec}")
        YT_W = dW.n
        for tc in range(NTC):
            cb2 = tc % 2
            P.y(lambda e, tc=tc, cb2=cb2: e.dma_start(
                    ctbuf[cb2].ap(0, 128, [[128, 16], [1, 128]]),
                    bass.AP(ctsp.tensor, tc * 128,
                            [[2048, 128], [128 * 2048, 16], [1, 128]])),
                wait=[(sT, st8.get(f"p9mm{cb2}", 0)), (dG, CT_G)],
                inc16=[dW], tag=f"p9ct{tc}")
            CTR_W = dW.n
            P.y(lambda e, tc=tc, cb2=cb2: e.dma_start(
                    x9[cb2][0:128, 0:1024], x1sp[ts(tc, 128), :]),
                wait=[(sV, st8.get(f"p9xv{cb2}", 0))],
                inc16=[dW], tag=f"p9xr{tc}")
            XR9_W = dW.n
            for ec in range(16):
                for half in range(2):
                    bk = 2 * cb2 + half
                    P.t(lambda e, tc=tc, ec=ec, half=half, bk=bk, cb2=cb2:
                        e.matmul(
                            bank(bk),
                            ctbuf[cb2][0:128, ts(ec, 128)],
                            yt[ec][0:128, ts(half, 512)],
                            start=(ec == 0), stop=(ec == 15)),
                        wait=[(dW, CTR_W), (dW, YT_W),
                              (sS, st8.get(f"p9ge{cb2}", 0))],
                        inc=[sT], tag=f"p9m{tc}{ec}{half}")
            st8[f"p9mm{cb2}"] = sT.n
            tv = sT.n
            P.s(lambda e, cb2=cb2: e.activation(
                    g9t[0:128, 0:1024],
                    PS[:, cb2 * 1024:(cb2 + 1) * 1024], AF.Silu),
                wait=[(sT, tv), (sV, st8.get("p9gv", 0))], inc=[sS],
                tag=f"p9ge{tc}")
            st8[f"p9ge{cb2}"] = sS.n
            sv = sS.n
            P.v(lambda e, tc=tc: e.tensor_tensor(
                    t9t[0:128, 0:1024], g9t[0:128, 0:1024],
                    sarena[0:128, ts(tc, 1024)], ALU.mult),
                wait=[(sS, sv), (sS, S1_S)], inc=[sV], tag=f"p9t1{tc}")
            vv = sV.n
            st8["p9gv"] = vv
            P.v(lambda e, cb2=cb2: e.tensor_tensor(
                    xo9[cb2][0:128, 0:1024], x9[cb2][0:128, 0:1024],
                    t9t[0:128, 0:1024], ALU.add),
                wait=[(sV, vv), (dW, XR9_W),
                      (dW, st8.get(f"p9ow{cb2}", 0))],
                inc=[sV], tag=f"p9x{tc}")
            xv = sV.n
            st8[f"p9xv{cb2}"] = xv
            P.y(lambda e, tc=tc, cb2=cb2: e.dma_start(
                    out_hbm[ts(tc, 128), :], xo9[cb2][0:128, 0:1024]),
                wait=[(sV, xv)], inc16=[dW], tag=f"p9o{tc}")
            st8[f"p9ow{cb2}"] = dW.n
        OUT_FIN = dW.n
        p9S.close()
        P.g(lambda e: e.nop(), wait=[(dW, OUT_FIN), (dG, dG.n), (dV, dV.n)],
            inc=[sG], tag="final")
        P.y(lambda e: e.nop(), wait=[(dW, OUT_FIN)], tag="final_sync")

    if stages < FULL:
        gt_ = gate("tap")
        tpS = contextlib.ExitStack()
        ttmp = sbv("ttmp", 128, 1024, F32, tpS)
        tap = taps.get(tapsel if tapsel is not None else stages)
        prevw = [0]

        def tap_tile(src_ap, tc, width):
            P.s(lambda e, src_ap=src_ap: e.copy(ttmp[0:128, 0:width], src_ap),
                wait=[gt_, (dW, prevw[0])], inc=[sS], tag=f"tap{tc}")
            sv = sS.n
            P.y(lambda e, tc=tc, width=width: e.dma_start(
                    out_hbm[ts(tc, 128), 0:width], ttmp[0:128, 0:width]),
                wait=[(sS, sv)], inc16=[dW], tag=f"tapd{tc}")
            prevw[0] = dW.n

        if tap is None:
            pass
        elif tap[0] == "arena_tm":
            arena, width = tap[1], tap[2]
            for tc in range(NTC):
                tap_tile(arena[0:128, tc * width:(tc + 1) * width], tc, width)
        elif tap[0] == "x1sp":
            for tc in range(NTC):
                P.y(lambda e, tc=tc: e.dma_start(
                        ttmp[0:128, 0:1024], x1sp[ts(tc, 128), :]),
                    wait=[gt_, (dW, prevw[0])], inc16=[dW], tag=f"tapl{tc}")
                wv = dW.n
                P.y(lambda e, tc=tc: e.dma_start(
                        out_hbm[ts(tc, 128), :], ttmp[0:128, 0:1024]),
                    wait=[(dW, wv)], inc16=[dW], tag=f"tapd{tc}")
                prevw[0] = dW.n
        elif tap[0] == "small":
            tsr, pp, ff = tap[1], tap[2], tap[3]
            P.s(lambda e: e.copy(ttmp[0:pp, 0:ff], tsr[0:pp, 0:ff]),
                wait=[gt_], inc=[sS], tag="taps")
            sv = sS.n
            P.y(lambda e: e.dma_start(out_hbm[0:pp, 0:ff], ttmp[0:pp, 0:ff]),
                wait=[(sS, sv)], inc16=[dW], tag="tapsd")
            prevw[0] = dW.n
        P.y(lambda e: e.nop(), wait=[(dW, prevw[0])], tag="tap_fin")
        tpS.close()

    P.finalize()
    return nc


_BUILD_CACHE = {}


def _get_program(stages):
    if stages not in _BUILD_CACHE:
        _BUILD_CACHE[stages] = build_program(stages)
    return _BUILD_CACHE[stages]


def _consts():
    import ml_dtypes
    bf = ml_dtypes.bfloat16
    c = {}
    c["identb"] = np.eye(128, dtype=np.float32).astype(bf)
    c["identnb"] = (-np.eye(128, dtype=np.float32)).astype(bf)
    c["onesrow"] = np.ones((1, T), dtype=np.float32).astype(bf)
    sig = np.arange(128)[:, None]
    tau = np.arange(512)[None, :]
    cm = np.zeros((128, 4 * 512), dtype=np.float32)
    for k in range(4):
        cm[:, k * 512:(k + 1) * 512] = np.where(128 * k + sig > tau, NEG, 0.0)
    c["cmask4"] = cm.astype(bf)
    jj = np.arange(16)[None, :]
    tau2 = np.arange(128)[:, None]
    c["sampmask"] = np.where(8 * jj <= tau2, 0.0, NEG).astype(np.float32)
    t_ = np.arange(128)[:, None]
    s_ = np.arange(128)[None, :]
    c["iotatri"] = np.where(t_ > s_, (t_ - s_).astype(np.float32), 1.0e30
                            ).astype(np.float32)
    c["mask01"] = (t_ > s_).astype(np.float32)
    c["iotarev2"] = np.broadcast_to((127.0 - np.arange(128))[None, :],
                                    (128, 128)).astype(np.float32).copy()
    cc = np.arange(16)[None, :]
    cp = np.arange(16)[:, None]
    c["iotac16"] = np.where(cc > cp, (cc - cp - 1).astype(np.float32), 1.0e30
                            ).astype(np.float32)
    c["mask01_16"] = (cc > cp).astype(np.float32)
    c["iotap1"] = np.broadcast_to((np.arange(128) + 1.0)[:, None],
                                  (128, 16)).astype(np.float32).copy()
    return c


def kernel(**inputs):
    import os
    stages = int(os.environ.get("KSTAGES", str(FULL)))
    nc = _get_program(stages)
    f32 = np.float32
    ctu = np.ascontiguousarray
    qkv_b = inputs["L0_attn_qkv_b"].astype(f32, copy=False)
    base = {
        "qkvw": ctu(inputs["L0_attn_qkv_w"].astype(f32, copy=False)),
        "qkvbT": ctu(np.asarray(qkv_b).reshape(24, 128).T),
        "qkvb_row": ctu(np.asarray(qkv_b).reshape(1, 3 * D)),
        "w1w": ctu(inputs["L0_attn_w1_write_w"].astype(f32, copy=False)),
        "w2w": ctu(inputs["L0_attn_w2_write_w"].astype(f32, copy=False)),
        "r1w": ctu(inputs["L0_attn_w1_read_w"].astype(f32, copy=False)),
        "r2w": ctu(inputs["L0_attn_w2_read_w"].astype(f32, copy=False)),
        "mvw": ctu(inputs["L0_attn_mem_value_w"].astype(f32, copy=False)),
        "mvb_row": ctu(np.asarray(inputs["L0_attn_mem_value_b"]).reshape(1, D)),
        "mgw": ctu(inputs["L0_attn_mem_gate_w"].astype(f32, copy=False)),
        "mgb_row": ctu(np.asarray(inputs["L0_attn_mem_gate_b"]).reshape(1, H)),
        "mscale_row": ctu(np.asarray(inputs["L0_attn_mem_scale"]).reshape(1, H)),
        "outw": ctu(inputs["L0_attn_out_w"].astype(f32, copy=False)),
        "outb_row": ctu(np.asarray(inputs["L0_attn_out_b"]).reshape(1, D)),
        "dlog_row": ctu(np.asarray(inputs["L0_attn_decay_logits"]).reshape(1, H)),
        "Jm": ctu(inputs["L0_attn_J"].astype(f32, copy=False)),
        "wgT": ctu(np.asarray(inputs["L0_ffn_w_gate"]).T.astype(np.int32)),
        "wuT": ctu(np.asarray(inputs["L0_ffn_w_up"]).T.astype(np.int32)),
        "wdT": ctu(np.asarray(inputs["L0_ffn_w_down"]).T.astype(np.int32)),
        "wq1": ctu(inputs["L1_attn_Wq"].astype(f32, copy=False)),
        "wkv1": ctu(inputs["L1_attn_Wkv"].astype(f32, copy=False)),
        "wk1": ctu(inputs["L1_attn_Wk"].astype(f32, copy=False)),
        "wv1": ctu(inputs["L1_attn_Wv"].astype(f32, copy=False)),
        "wo1e": ctu(np.asarray(inputs["L1_attn_Wo"]).reshape(H, DH, D)
                    [:, 0:VD1, :].reshape(H * VD1, D)),
        "qg_row": ctu(np.asarray(inputs["L1_attn_q_g"]).reshape(1, DH)),
        "kg_row": ctu(np.asarray(inputs["L1_attn_k_g"]).reshape(1, ROPE)),
        "phiT": ctu(np.asarray(inputs["L1_ffn_phi_w"]).T),
        "phib_row": ctu(np.asarray(inputs["L1_ffn_phi_b"]).reshape(1, 2048)),
    }
    for k, v in _consts().items():
        base[k] = v
    xfull = np.asarray(inputs["L0_attn_x"], dtype=f32)
    expw_full = np.asarray(inputs["L1_ffn_expert_w"], dtype=f32)
    expb_full = np.asarray(inputs["L1_ffn_expert_b"], dtype=f32)
    in_maps = []
    for cidx in range(N_CORES):
        m = dict(base)
        m["x"] = ctu(xfull[cidx])
        m["expw"] = ctu(expw_full[cidx * EMINE:(cidx + 1) * EMINE])
        m["expb"] = ctu(expb_full[cidx * EMINE:(cidx + 1) * EMINE]
                        .reshape(1, EMINE * D))
        in_maps.append(m)
    res = run_bass_kernel_spmd(nc, in_maps, core_ids=list(range(N_CORES)))
    out = np.stack([np.asarray(res.results[cidx]["out"], dtype=np.float32)
                    for cidx in range(N_CORES)], axis=0)
    return out
